# version 01
# baseline (speedup 1.0000x reference)
import numpy as np
import ml_dtypes
import concourse.bass as bass
import concourse.mybir as mybir
from concourse.bass_utils import run_bass_kernel_spmd

F32 = mybir.dt.float32
BF16 = mybir.dt.bfloat16
AF = mybir.ActivationFunctionType
ALU = mybir.AluOpType

OWN = 2048
HALO = 1792
S = OWN + HALO
SOFF = 244
SW = S - SOFF
SSM = 4096
D = 1024
NT = 8
HEADS = 16
DH = 64
WIN = 256
DI = 2730
NI_T = 22
NLAYER = 6
ATT_OUT = [501, 759, 1017, 1275, 1533, 1791]
FFN_OUT = [502, 760, 1018, 1276, 1534, 1792]
FFN_TB = 896

SUBLAYERS = 12
DEBUG_DUMP = False

bf16 = ml_dtypes.bfloat16


class Chan:
    def __init__(self, name, engine, inc):
        self.name, self.engine, self.inc = name, engine, inc


CHANNELS = {
    "pe": Chan("pe", "tensor", 1),
    "dve": Chan("dve", "vector", 1),
    "act": Chan("act", "scalar", 1),
    "pool": Chan("pool", "gpsimd", 1),
    "dmw": Chan("dmw", "gpsimd", 16),
    "dmw1": Chan("dmw1", "gpsimd", 16),
    "dmx": Chan("dmx", "sync", 16),
    "dms": Chan("dms", "sync", 16),
}
ENGINES = ("tensor", "vector", "scalar", "gpsimd", "sync")


NEPOCH = 6


class Builder:

    def __init__(self):
        self.items = {e: [] for e in ENGINES}
        self.cnt = {c: 0 for c in CHANNELS}
        self.dma_idx = {c: 0 for c in CHANNELS if CHANNELS[c].inc == 16}
        self.dma_cum = {}
        self.dma_rot = {c: False for c in self.dma_idx}
        self.last_w = {}
        self.readers = {}
        self.seen = {e: {} for e in ENGINES}
        self.serial = False

    def semkey(self, ch):
        return f"{ch}@{self.dma_idx[ch]}"

    @staticmethod
    def _ov(recs, lo, hi):
        return [x for x in recs if not (hi <= x[0] or x[1] <= lo)]

    def _dep(self, waits, semkey, val):
        ch = semkey.split("@")[0]
        if "@" in semkey:
            val = self.dma_cum.get(semkey, 0)
            if semkey == self.semkey(ch):
                self.dma_rot[ch] = True
        waits[semkey] = max(waits.get(semkey, 0), val)

    def op(self, ch, fn, reads=(), writes=(), tag=""):
        c = CHANNELS[ch]
        eng = c.engine
        waits = {}
        if self.serial:
            for chp, v in self.cnt.items():
                if v > 0:
                    waits[chp] = v
            for sk, v in self.dma_cum.items():
                if v > 0:
                    self._dep(waits, sk, v)
        else:
            for nm, lo, hi in reads:
                for _, _, sk, vv in self._ov(self.last_w.get(nm, []), lo, hi):
                    self._dep(waits, sk, vv)
            for nm, lo, hi in writes:
                for _, _, sk, vv in self._ov(self.last_w.get(nm, []), lo, hi):
                    self._dep(waits, sk, vv)
                for _, _, sk, vv in self._ov(self.readers.get(nm, []), lo, hi):
                    self._dep(waits, sk, vv)
        if c.inc == 16:
            if self.dma_rot[ch]:
                self.dma_idx[ch] = (self.dma_idx[ch] + 1) % NEPOCH
                self.dma_rot[ch] = False
                prev = self.dma_cum.get(self.semkey(ch), 0)
                if prev > 0:
                    waits[self.semkey(ch)] = max(
                        waits.get(self.semkey(ch), 0), prev)
            sk = self.semkey(ch)
            self.dma_cum[sk] = self.dma_cum.get(sk, 0) + 16
            ev = (sk, self.dma_cum[sk])
        else:
            self.cnt[ch] += 1
            ev = (ch, self.cnt[ch])
        final = []
        for skp, v in sorted(waits.items()):
            if v > self.seen[eng].get(skp, 0):
                final.append((skp, v))
                self.seen[eng][skp] = v
        self.items[eng].append((final, fn, ev[0], tag))
        for nm, lo, hi in reads:
            self.readers.setdefault(nm, []).append((lo, hi, ev[0], ev[1]))
        for nm, lo, hi in writes:
            lw = self.last_w.setdefault(nm, [])
            lw[:] = [x for x in lw if not (lo <= x[0] and x[1] <= hi)]
            lw.append((lo, hi, ev[0], ev[1]))
            rd = self.readers.get(nm)
            if rd:
                rd[:] = [x for x in rd if not (lo <= x[0] and x[1] <= hi)]
        return ev[1]

    def group(self, ch, fns, reads=(), writes=(), tag=""):
        def fused(e):
            inst = None
            for f in fns:
                inst = f(e)
            return inst
        return self.op(ch, fused, reads=reads, writes=writes, tag=tag)

    def barrier(self):
        snap = dict(self.cnt)
        snap.update(self.dma_cum)
        for ch in self.dma_idx:
            self.dma_rot[ch] = True
        for eng in ENGINES:
            w = []
            for skp, v in snap.items():
                if v > 0 and v > self.seen[eng].get(skp, 0):
                    w.append((skp, v))
                    self.seen[eng][skp] = v
            if w:
                self.items[eng].append((w, None, None, "barrier"))

    def lower(self, nc):
        sems = {}
        for chname, c in CHANNELS.items():
            if c.inc == 16:
                for i in range(NEPOCH):
                    sems[f"{chname}@{i}"] = nc.semaphore(f"sem_{chname}_{i}").__enter__()
            else:
                sems[chname] = nc.semaphore(f"sem_{chname}").__enter__()
        with nc.Block() as block:
            def body_for(eng_name):
                def body(eng):
                    for waits, fn, sk, tag in self.items[eng_name]:
                        for skp, v in waits:
                            eng.wait_ge(sems[skp], v)
                        if fn is not None:
                            inst = fn(eng)
                            inst.then_inc(sems[sk],
                                          16 if "@" in sk else 1)
                return body
            block.tensor(body_for("tensor"))
            block.vector(body_for("vector"))
            block.scalar(body_for("scalar"))
            block.gpsimd(body_for("gpsimd"))
            block.sync(body_for("sync"))


def _stage_inputs(inp):
    x = np.asarray(inp["L0_attn_x"], dtype=np.float32)

    prot = np.zeros((128, 128), dtype=np.float32)
    for d in range(128):
        dh = d % 64
        if dh < 32:
            prot[d + 32, d] = -1.0
        else:
            prot[d - 32, d] = 1.0

    half_idx = np.arange(0, DH, 2, dtype=np.float32)
    inv_freq = 1.0 / (10000.0 ** (half_idx / DH))
    base_scale = (half_idx + 0.4 * DH) / (1.4 * DH)
    dh_idx = np.arange(DH) % 32
    cs = np.arange(S, dtype=np.float32)

    def tables(own0):
        tg = (own0 + cs - HALO).astype(np.float32)
        theta = tg[None, :] * inv_freq[dh_idx][:, None]
        power = (tg - 2048.0) / 128.0
        scale = base_scale[dh_idx][:, None] ** power[None, :]
        A = np.cos(theta) * scale
        Bt = np.sin(theta) * scale
        C = np.cos(theta) / scale
        Dt = np.sin(theta) / scale
        two = lambda t: np.ascontiguousarray(
            np.concatenate([t, t], 0).astype(np.float32))
        return two(A), two(Bt), two(C), two(Dt)

    tabs = {0: tables(0), OWN: tables(OWN)}

    corr_left = np.zeros((1, 512), dtype=np.float32)
    ccs = np.arange(1536, 2048)
    tg = ccs - HALO
    corr_left[0, :] = np.where((tg >= 0) & (tg < 256), 256.0 - tg, 0.0)
    corr_left = corr_left.astype(bf16)
    corr_right = np.zeros((1, 512), dtype=bf16)


    fmT = lambda a: np.ascontiguousarray(a.T)
    col8 = lambda v: np.ascontiguousarray(v.reshape(8, 128).T)

    common = {
        "c_prot": prot.astype(bf16),
        "c_onescol": np.ones((128, 1), dtype=bf16),
        "c_onesmat": np.ones((128, 128), dtype=bf16),
        "c_onesrowf": np.ones((1, 128), dtype=np.float32),
        "c_onesrow": np.ones((1, 512), dtype=bf16),
        "c_negones": np.full((1, 128), -1.0, dtype=bf16),
        "c_ident": np.eye(128, dtype=np.float32),
        "c_blkones": np.ascontiguousarray(
            np.stack([np.r_[np.ones(64), np.zeros(64)],
                      np.r_[np.zeros(64), np.ones(64)]], 1).astype(bf16)),
        "ssm_a": col8(np.asarray(inp["L0_attn_a"], np.float32)),
        "ssm_b": col8(np.asarray(inp["L0_attn_b"], np.float32)),
        "w_outT": np.ascontiguousarray(np.asarray(inp["L0_attn_w_out"], np.float32).T),
        "b_out": np.asarray(inp["L0_attn_b_out"], np.float32).reshape(1, D),
        "Wqkv": np.asarray(inp["L0_ffn_Wqkv"], np.float32),
        "q_scale": np.asarray(inp["L0_ffn_q_scale"], np.float32),
        "k_scale": np.asarray(inp["L0_ffn_k_scale"], np.float32),
        "Wo": np.asarray(inp["L0_ffn_Wo"], np.float32),
        "ln_g": np.asarray(inp["L0_ffn_ln_g"], np.float32).reshape(NLAYER, 1, D),
        "ln_b": np.ascontiguousarray(np.stack(
            [col8(r) for r in np.asarray(inp["L0_ffn_ln_b"], np.float32)])),
        "W1": np.asarray(inp["L0_ffn_W1"], np.float32),
        "b1": np.asarray(inp["L0_ffn_b1"], np.float32).reshape(NLAYER, 1, 2 * DI),
        "W2": np.asarray(inp["L0_ffn_W2"], np.float32),
        "b2": np.asarray(inp["L0_ffn_b2"], np.float32).reshape(NLAYER, 1, D),
    }

    in_maps = []
    for core in range(8):
        b, half = core // 2, core % 2
        own0 = half * OWN
        if half == 0:
            xs = np.concatenate([np.zeros((OWN, D), np.float32), x[b, :OWN]], 0)
        else:
            xs = x[b]
        A, Bt, C, Dt = tabs[own0]
        m = dict(common)
        m.update({
            "x_ssm": fmT(xs),
            "x_own": np.ascontiguousarray(x[b, own0:own0 + OWN]),
            "corr": corr_left if half == 0 else corr_right,
            "zflag": (np.zeros((128, 1), np.float32) if half == 0
                      else np.ones((128, 1), np.float32)),
            "tabA": A, "tabB": Bt, "tabC": C, "tabD": Dt,
        })
        in_maps.append(m)
    return in_maps


def build_program(sublayers=SUBLAYERS):
    nc = bass.Bass("TRN2", target_bir_lowering=False)
    B = Builder()

    def din(name, shape, dtype=F32):
        return nc.dram_tensor(name, list(shape), dtype, kind="ExternalInput").ap()

    x_ssm = din("x_ssm", [D, SSM])
    x_own = din("x_own", [OWN, D])
    corr = din("corr", [1, 512], BF16)
    zflag = din("zflag", [128, 1])
    tabs_hbm = {"A": din("tabA", [128, S]), "B": din("tabB", [128, S]),
                "C": din("tabC", [128, S]), "D": din("tabD", [128, S])}
    c_prot = din("c_prot", [128, 128], BF16)
    c_onescol = din("c_onescol", [128, 1], BF16)
    c_onesmat = din("c_onesmat", [128, 128], BF16)
    c_onesrowf = din("c_onesrowf", [1, 128])
    c_onesrow = din("c_onesrow", [1, 512], BF16)
    c_negones = din("c_negones", [1, 128], BF16)
    c_ident = din("c_ident", [128, 128])
    c_blkones = din("c_blkones", [128, 2], BF16)
    ssm_a = din("ssm_a", [128, 8])
    ssm_b = din("ssm_b", [128, 8])
    w_outT = din("w_outT", [D, D])
    b_out = din("b_out", [1, D])
    Wqkv = din("Wqkv", [NLAYER, D, 3 * D])
    q_scale = din("q_scale", [NLAYER, DH])
    k_scale = din("k_scale", [NLAYER, DH])
    Wo_h = din("Wo", [NLAYER, D, D])
    ln_g = din("ln_g", [NLAYER, 1, D])
    ln_b = din("ln_b", [NLAYER, 128, 8])
    W1_h = din("W1", [NLAYER, D, 2 * DI])
    b1_h = din("b1", [NLAYER, 1, 2 * DI])
    W2_h = din("W2", [NLAYER, DI, D])
    b2_h = din("b2", [NLAYER, 1, D])

    out_t = nc.dram_tensor("out", [OWN, D], F32, kind="ExternalOutput").ap()
    s_scr = nc.dram_tensor("s_scratch", [D, OWN], F32, kind="Internal").ap()

    top = nc.sbuf_top
    base = ((top - nc.sbuf_bytes_remaining) + 63) // 64 * 64
    H = {}
    cur = [base]

    NAMEOF = {}

    def salloc(name, shape, dtype, off=None):
        nbytes = int(np.prod(shape[1:])) * mybir.dt.size(dtype)
        if off is None:
            o = cur[0]
            cur[0] = o + (nbytes + 63) // 64 * 64
        else:
            o = off
        assert o + nbytes <= top, f"SBUF overflow placing {name} at {o}+{nbytes} (top {top})"
        h = nc.alloc_sbuf_tensor_at(name, list(shape), dtype, offset=o)
        H[name] = h
        NAMEOF[id(h)] = name
        return h

    stream = salloc("stream", [128, NT * SW], F32)
    protb = salloc("protb", [128, 128], BF16)
    onesmatb = salloc("onesmatb", [128, 128], BF16)
    onesrowfb = salloc("onesrowfb", [1, 128], F32)
    onesrowb = salloc("onesrowb", [1, 512], BF16)
    negonesb = salloc("negonesb", [1, 128], BF16)
    identb = salloc("identb", [128, 128], F32)
    blkonesb = salloc("blkonesb", [128, 2], BF16)
    corrb = salloc("corrb", [1, 512], BF16)
    zflagb = salloc("zflagb", [128, 1], F32)
    ahb = salloc("ahb", [128, 8], F32)
    bcolb = salloc("bcolb", [128, 8], F32)
    carryb = salloc("carryb", [128, 8], F32)
    qspatb = salloc("qspatb", [2, 128], BF16)
    kspatb = salloc("kspatb", [2, 128], BF16)
    betcolb = salloc("betcolb", [128, 8], F32)
    gamrowb = salloc("gamrowb", [1, D], BF16)
    b2rowb = salloc("b2rowb", [1, 128], BF16)
    rowhi2 = salloc("rowhi2", [1, 512], BF16)
    shsnap = salloc("shsnap", [128, 8], F32)
    rowlo2 = salloc("rowlo2", [1, 512], BF16)
    b1rowb = salloc("b1rowb", [1, 256], BF16)
    rowa = salloc("rowa", [2, 512], F32)
    rowb_ = salloc("rowb", [2, 512], F32)
    rowhi = salloc("rowhi", [2, 512], BF16)
    rowlo = salloc("rowlo", [2, 512], BF16)

    OV = cur[0]

    def oalloc(name, shape, dtype, off):
        return salloc(name, shape, dtype, off=OV + off)

    xch = oalloc("xch", [128, NT * 512], F32, 0)
    ych = oalloc("ych", [128, NT * 512], F32, 16384)
    x2ch = oalloc("x2ch", [128, NT * 512], BF16, 32768)
    ybfb = oalloc("ybfb", [128, NT * 512], BF16, 40960)
    wout_sb = oalloc("wout_sb", [128, NT * 1024], BF16, 49152)
    sevacb = oalloc("sevacb", [128, 512], F32, 65536)
    borowb = oalloc("borowb", [1, D], BF16, 67584)
    sgf = oalloc("sgf", [128, NT * 512], F32, 0)
    sldf = oalloc("sldf", [128, NT * 512], F32, 16384)
    xotf = oalloc("xotf", [128, 1024], F32, 32768)
    xsb = oalloc("xsb", [128, NT * 768], BF16, 0)
    kbuf = oalloc("kbuf", [128, NT * 768], BF16, 12288)
    vbuf = oalloc("vbuf", [128, 6 * 1024], BF16, 24576)
    qbuf = oalloc("qbuf", [128, NT * 512], BF16, 36864)
    aob = oalloc("aob", [128, NT * 512], BF16, 45056)
    tQA = oalloc("tQA", [128, 512], F32, 53248)
    tQB = oalloc("tQB", [128, 512], F32, 55296)
    tKC = oalloc("tKC", [128, 768], F32, 57344)
    tKD = oalloc("tKD", [128, 768], F32, 60416)
    qrawb = oalloc("qrawb", [128, 512], BF16, 63488)
    q1b = oalloc("q1b", [128, 512], BF16, 64512)
    qsqb = oalloc("qsqb", [128, 512], BF16, 65536)
    uf32a = oalloc("uf32a", [128, 512], F32, 66560)
    uf32b = oalloc("uf32b", [128, 512], F32, 68608)
    ptb = oalloc("ptb", [128, 4 * 256], BF16, 70656)
    rsbuf = oalloc("rsbuf", [128, 256], F32, 72704)
    wqkb = oalloc("wqkb", [128, 2 * NT * 128], BF16, 73728)
    wvb = oalloc("wvb", [128, NT * 128], BF16, 77824)
    xnfb = oalloc("xnfb", [128, NT * FFN_TB], BF16, 0)
    gatedb = oalloc("gatedb", [128, NI_T * FFN_TB], BF16, 16384)
    gelbb = oalloc("gelbb", [128, 512], F32, 55808)
    fsqb = oalloc("fsqb", [128, 512], BF16, 57856)
    fxbb = oalloc("fxbb", [128, 512], BF16, 58880)
    ft32 = oalloc("ft32", [128, 512], F32, 59904)
    w1bb = oalloc("w1bb", [128, 2 * NT * 256], BF16, 61952)
    w2bb = oalloc("w2bb", [128, 2 * NI_T * 128], BF16, 70144)

    assert OV + 81920 <= top, f"overlay end over top ({OV + 81920} > {top})"
    phases = {
        "ph0": ["xch", "ych", "x2ch", "ybfb", "wout_sb", "sevacb", "borowb"],
        "fin": ["sgf", "sldf", "xotf"],
        "attn": ["xsb", "kbuf", "vbuf", "qbuf", "aob", "tQA", "tQB", "tKC",
                 "tKD", "qrawb", "q1b", "qsqb", "uf32a", "uf32b", "ptb",
                 "rsbuf", "wqkb", "wvb"],
        "ffn": ["xnfb", "gatedb", "gelbb", "fsqb", "fxbb", "ft32", "w1bb",
                "w2bb"],
    }
    for ph, names in phases.items():
        spans = sorted((H[n].manual_sbuf_range, n) for n in names)
        for i in range(len(spans) - 1):
            (a0, a1), n1 = spans[i]
            (b0, b1), n2 = spans[i + 1]
            assert a1 <= b0, f"{ph}: {n1} {a0}-{a1} overlaps {n2} {b0}-{b1}"

    ps = [nc.alloc_psum_tensor(f"bank{i}", [128, 512], F32) for i in range(8)]

    def K(name, lo=0, hi=1 << 30):
        return (name, lo, hi)

    def st_ap(dt, c0, c1):
        assert SOFF <= c0 < c1 <= S, (c0, c1)
        return stream.ap()[:, dt * SW + c0 - SOFF:dt * SW + c1 - SOFF]

    def st_key(dt, c0, c1):
        return ("stream", dt * SW + c0 - SOFF, dt * SW + c1 - SOFF)

    for dst, src in [(protb, c_prot), (onesmatb, c_onesmat),
                     (onesrowfb, c_onesrowf), (onesrowb, c_onesrow),
                     (negonesb, c_negones), (identb, c_ident),
                     (blkonesb, c_blkones), (corrb, corr), (zflagb, zflag),
                     (ahb, ssm_a), (bcolb, ssm_b)]:
        B.op("dmx", (lambda d, s: lambda e: e.dma_start(d.ap()[:, :], s[:, :]))(dst, src),
             writes=[K(NAMEOF[id(dst)])], tag="const")
    B.op("dmw", lambda e: e.dma_start(borowb.ap()[:, :], b_out[:, :]),
         writes=[K("borowb")], tag="b_out")
    B.op("dmw", lambda e: e.dma_start(
        wout_sb.ap()[:, :].rearrange("p (dt e) -> p dt e", dt=NT),
        w_outT[:, :].rearrange("(dt p) e -> p dt e", p=128)),
        writes=[K("wout_sb")], tag="woutT")
    B.op("act", lambda e: e.activation(ahb.ap()[:, :], ahb.ap()[:, :], AF.Tanh),
         reads=[K("ahb")], writes=[K("ahb")], tag="tanh")
    B.op("dve", lambda e: e.memset(carryb.ap()[:, :], 0.0),
         writes=[K("carryb")], tag="carry0")

    for kc in range(SSM // 512):
        c0 = kc * 512
        sc0 = c0 - 256
        xk = K("xch")
        B.op("dmx", (lambda c0: lambda e: e.dma_start(
            xch.ap()[:, :].rearrange("p (dt c) -> p dt c", dt=NT),
            x_ssm[:, c0:c0 + 512].rearrange("(dt p) c -> p dt c", p=128)))(c0),
            writes=[xk], tag=f"xld{kc}")
        B.op("dve", lambda e: e.tensor_copy(x2ch.ap()[:, :], xch.ap()[:, :]),
             reads=[xk], writes=[K("x2ch")], tag="xbf")
        B.group("pe", [(lambda dt: lambda e: e.matmul(
            ps[6].ap()[0:1, :], onesmatb.ap()[:, 0:1],
            x2ch.ap()[:, dt * 512:(dt + 1) * 512],
            start=(dt == 0), stop=(dt == NT - 1)))(dt) for dt in range(NT)],
            reads=[K("x2ch"), K("onesmatb")], writes=[K("ps6")], tag="sumx")
        B.op("act", lambda e: e.activation(x2ch.ap()[:, :], xch.ap()[:, :], AF.Square),
             reads=[xk], writes=[K("x2ch")], tag="sq")
        B.group("pe", [(lambda dt: lambda e: e.matmul(
            ps[7].ap()[0:1, :], onesmatb.ap()[:, 0:1],
            x2ch.ap()[:, dt * 512:(dt + 1) * 512],
            start=(dt == 0), stop=(dt == NT - 1)))(dt) for dt in range(NT)],
            reads=[K("x2ch"), K("onesmatb")], writes=[K("ps7")], tag="sumx2")
        B.op("act", lambda e: e.activation(rowa.ap()[0:1, :], ps[6].ap()[0:1, :],
                                           AF.Identity, bias=0.0, scale=1.0 / D),
             reads=[K("ps6")], writes=[K("rowa", 0, 2048)], tag="mu")
        B.op("act", lambda e: e.activation(rowb_.ap()[0:1, :], rowa.ap()[0:1, :],
                                           AF.Square),
             reads=[K("rowa", 0, 2048)], writes=[K("rowb", 0, 2048)], tag="mu2")
        B.op("dve", lambda e: e.scalar_tensor_tensor(
            rowb_.ap()[0:1, :], ps[7].ap()[0:1, :], 1.0 / D, rowb_.ap()[0:1, :],
            ALU.mult, ALU.subtract),
            reads=[K("ps7"), K("rowb", 0, 2048)], writes=[K("rowb", 0, 2048)], tag="var")
        B.op("dve", lambda e: e.tensor_scalar_add(rowb_.ap()[0:1, :],
                                                  rowb_.ap()[0:1, :], 1e-5),
             reads=[K("rowb", 0, 2048)], writes=[K("rowb", 0, 2048)], tag="eps")
        B.op("act", lambda e: e.activation(rowb_.ap()[0:1, :], rowb_.ap()[0:1, :],
                                           AF.Sqrt),
             reads=[K("rowb", 0, 2048)], writes=[K("rowb", 0, 2048)], tag="std")
        B.op("dve", lambda e: e.reciprocal(rowb_.ap()[0:1, :], rowb_.ap()[0:1, :]),
             reads=[K("rowb", 0, 2048)], writes=[K("rowb", 0, 2048)], tag="rstd")
        B.op("dve", lambda e: e.scalar_tensor_tensor(
            rowa.ap()[0:1, :], rowa.ap()[0:1, :], -1.0, rowb_.ap()[0:1, :],
            ALU.mult, ALU.mult),
            reads=[K("rowa", 0, 2048), K("rowb", 0, 2048)],
            writes=[K("rowa", 0, 2048)], tag="negmurstd")
        B.group("pe", [lambda e: e.matmul(ps[4].ap()[:, :], onesrowfb.ap()[:, :],
                                          rowb_.ap()[0:1, :], start=True, stop=True)],
                reads=[K("rowb", 0, 2048), K("onesrowfb")], writes=[K("ps4")],
                tag="bc_r1")
        B.group("pe", [lambda e: e.matmul(ps[5].ap()[:, :], onesrowfb.ap()[:, :],
                                          rowa.ap()[0:1, :], start=True, stop=True)],
                reads=[K("rowa", 0, 2048), K("onesrowfb")], writes=[K("ps5")],
                tag="bc_r2")
        for dt in range(NT):
            dk = K("xch", dt * 2048, (dt + 1) * 2048)
            B.op("dve", (lambda dt: lambda e: e.tensor_tensor(
                xch.ap()[:, dt * 512:(dt + 1) * 512],
                xch.ap()[:, dt * 512:(dt + 1) * 512],
                ps[4].ap()[:, :], ALU.mult))(dt),
                reads=[dk, K("ps4")], writes=[dk], tag="xn_mul")
            B.op("dve", (lambda dt: lambda e: e.tensor_tensor(
                xch.ap()[:, dt * 512:(dt + 1) * 512],
                xch.ap()[:, dt * 512:(dt + 1) * 512],
                ps[5].ap()[:, :], ALU.add))(dt),
                reads=[dk, K("ps5")], writes=[dk], tag="xn_add")
            lo = max(SOFF, sc0)
            if lo < sc0 + 512:
                off = lo - sc0
                B.op("act", (lambda dt, off, lo, sc0: lambda e: e.activation(
                    st_ap(dt, lo, sc0 + 512),
                    xch.ap()[:, dt * 512 + off:(dt + 1) * 512], AF.Copy))(dt, off, lo, sc0),
                    reads=[dk], writes=[st_key(dt, lo, sc0 + 512)], tag="xn_st")
            B.op("act", (lambda dt: lambda e: e.activation(
                xch.ap()[:, dt * 512:(dt + 1) * 512],
                xch.ap()[:, dt * 512:(dt + 1) * 512],
                AF.Identity, bias=0.0, scale=bcolb.ap()[:, dt:dt + 1]))(dt),
                reads=[dk, K("bcolb")], writes=[dk], tag="bx")
            yk = K("ych", dt * 2048, (dt + 1) * 2048)
            B.op("dve", (lambda dt: lambda e: e.tensor_tensor_scan(
                ych.ap()[:, dt * 512:(dt + 1) * 512],
                ahb.ap()[:, dt:dt + 1].broadcast_to([128, 512]),
                xch.ap()[:, dt * 512:(dt + 1) * 512],
                carryb.ap()[:, dt:dt + 1], ALU.mult, ALU.add))(dt),
                reads=[dk, K("ahb"), K("carryb", dt * 4, dt * 4 + 4)], writes=[yk],
                tag=f"scan{kc}_{dt}")
            B.op("dve", (lambda dt: lambda e: e.tensor_copy(
                carryb.ap()[:, dt:dt + 1],
                ych.ap()[:, dt * 512 + 511:dt * 512 + 512]))(dt),
                reads=[yk], writes=[K("carryb", dt * 4, dt * 4 + 4)], tag="carry")
        if kc >= 4:
            oc0 = (kc - 4) * 512
            B.op("dve", lambda e: e.tensor_copy(ybfb.ap()[:, :], ych.ap()[:, :]),
                 reads=[K("ych")], writes=[K("ybfb")], tag="ybf")
            for et in range(NT):
                bank = et % 2
                fns = []
                for dt in range(NT):
                    fns.append((lambda dt, et, bank: lambda e: e.matmul(
                        ps[bank].ap()[:, :],
                        wout_sb.ap()[:, dt * 1024 + et * 128:dt * 1024 + et * 128 + 128],
                        ybfb.ap()[:, dt * 512:(dt + 1) * 512],
                        start=(dt == 0), stop=False))(dt, et, bank))
                fns.append((lambda et, bank: lambda e: e.matmul(
                    ps[bank].ap()[:, :],
                    borowb.ap()[:, et * 128:et * 128 + 128],
                    onesrowb.ap()[:, :], start=False, stop=True))(et, bank))
                B.group("pe", fns,
                        reads=[K("ybfb"), K("wout_sb"), K("borowb"), K("onesrowb")],
                        writes=[K(f"ps{bank}")], tag=f"wout{et}")
                B.op("act", (lambda bank: lambda e: e.activation(
                    sevacb.ap()[:, :], ps[bank].ap()[:, :], AF.Copy))(bank),
                    reads=[K(f"ps{bank}")], writes=[K("sevacb")], tag="sevac")
                B.op("dms", (lambda et, oc0: lambda e: e.dma_start(
                    s_scr[et * 128:(et + 1) * 128, oc0:oc0 + 512],
                    sevacb.ap()[:, :]))(et, oc0),
                    reads=[K("sevacb")], tag="s_st")
    B.barrier()

    def emit_attn(l):
        Aq = ATT_OUT[l] - 1
        WQ0 = Aq // WIN
        for pat, scl in [(qspatb, q_scale), (kspatb, k_scale)]:
            B.op("dve", (lambda pat: lambda e: e.memset(pat.ap()[:, :], 0.0))(pat),
                 writes=[K(NAMEOF[id(pat)])], tag="patz")
            B.op("dmw", (lambda pat, scl, l=l: lambda e: e.dma_start(
                pat.ap()[0:1, 0:64], scl[l:l + 1, :]))(pat, scl),
                reads=[K(NAMEOF[id(pat)])], writes=[K(NAMEOF[id(pat)])], tag="pat0")
            B.op("dmw", (lambda pat, scl, l=l: lambda e: e.dma_start(
                pat.ap()[1:2, 64:128], scl[l:l + 1, :]))(pat, scl),
                reads=[K(NAMEOF[id(pat)])], writes=[K(NAMEOF[id(pat)])], tag="pat1")

        groups = []
        w = WQ0
        nwin_total = S // WIN
        while w < nwin_total:
            g1 = min(w + 1, nwin_total - 1)
            groups.append((w, g1))
            w = g1 + 1

        prev_kspan = None
        for (g0, g1) in groups:
            q0 = max(Aq, g0 * WIN)
            q1 = (g1 + 1) * WIN
            qspan = q1 - q0
            assert qspan <= 512
            k0 = (g0 - 1) * WIN
            k1 = q1
            kspan = k1 - k0
            ntt = kspan // 128
            assert kspan <= 768
            for dt in range(NT):
                if prev_kspan is None:
                    cast0 = k0
                    zfill = max(k0, SOFF + 1)
                    if zfill > k0:
                        B.op("pool", (lambda dt, n: lambda e: e.memset(
                            xsb.ap()[:, dt * 768:dt * 768 + n], 0.0))(dt, zfill - k0),
                            writes=[K("xsb", dt * 1536, dt * 1536 + 2 * (zfill - k0))],
                            tag="xs_z")
                else:
                    cast0 = k0 + WIN
                    zfill = cast0
                    B.op("act", (lambda dt, pk: lambda e: e.activation(
                        xsb.ap()[:, dt * 768:dt * 768 + WIN],
                        xsb.ap()[:, dt * 768 + pk - WIN:dt * 768 + pk],
                        AF.Copy))(dt, prev_kspan),
                        reads=[K("xsb", dt * 1536 + 2 * (prev_kspan - WIN),
                                 dt * 1536 + 2 * prev_kspan)],
                        writes=[K("xsb", dt * 1536, dt * 1536 + 2 * WIN)],
                        tag="xs_cp")
                src0 = zfill if dt < 4 else zfill - 1
                B.op("act", (lambda dt, src0, zfill, k0, k1: lambda e: e.activation(
                    xsb.ap()[:, dt * 768 + (zfill - k0):dt * 768 + (k1 - k0)],
                    st_ap(dt, src0, src0 + (k1 - zfill)), AF.Copy))(dt, src0, zfill, k0, k1),
                    reads=[st_key(dt, src0, src0 + (k1 - zfill))],
                    writes=[K("xsb", dt * 1536 + 2 * (zfill - k0),
                              dt * 1536 + 2 * (k1 - k0))],
                    tag="xs")
            prev_kspan = kspan
            for dst, tab, a0, a1 in [(tQA, "A", q0, q1), (tQB, "B", q0, q1),
                                     (tKC, "C", k0, k1), (tKD, "D", k0, k1)]:
                B.op("dmx", (lambda dst, tab, a0, a1: lambda e: e.dma_start(
                    dst.ap()[:, 0:a1 - a0], tabs_hbm[tab][:, a0:a1]))(dst, tab, a0, a1),
                    writes=[K(NAMEOF[id(dst)])], tag="tab")
            for side, wcol0, patb, tA_, tB_, obuf, ospan, s0, s1 in [
                    ("q", 0, qspatb, tQA, tQB, qbuf, 512, q0, q1),
                    ("k", D, kspatb, tKC, tKD, kbuf, 768, k0, k1)]:
                for hp in range(8):
                    wsel = hp % 2
                    B.op("dmw" if wsel == 0 else "dmw1",
                         (lambda hp, wcol0, wsel, l=l: lambda e: e.dma_start(
                        wqkb.ap()[:, wsel * 1024:(wsel + 1) * 1024].rearrange(
                            "p (dt h) -> p dt h", dt=NT),
                        Wqkv[l, :, wcol0 + hp * 128:wcol0 + (hp + 1) * 128].rearrange(
                            "(dt p) h -> p dt h", p=128)))(hp, wcol0, wsel),
                        writes=[K("wqkb", wsel * 2048, (wsel + 1) * 2048)], tag="wqk")
                    span = s1 - s0
                    nch = (span + 511) // 512
                    for cc in range(nch):
                        cw = min(512, span - cc * 512)
                        xoff = (s0 - k0) + cc * 512
                        bank = cc % 2
                        fns = [(lambda dt, cw, xoff, bank, wsel: lambda e: e.matmul(
                            ps[bank].ap()[:, 0:cw],
                            wqkb.ap()[:, wsel * 1024 + dt * 128:wsel * 1024 + (dt + 1) * 128],
                            xsb.ap()[:, dt * 768 + xoff:dt * 768 + xoff + cw],
                            start=(dt == 0), stop=(dt == NT - 1)))(dt, cw, xoff, bank, wsel)
                            for dt in range(NT)]
                        B.group("pe", fns,
                                reads=[K("wqkb", wsel * 2048, (wsel + 1) * 2048), K("xsb")],
                                writes=[K(f"ps{bank}")], tag=f"{side}mm")
                        B.op("act", (lambda bank, cw: lambda e: e.activation(
                            qsqb.ap()[:, 0:cw], ps[bank].ap()[:, 0:cw], AF.Square))(bank, cw),
                            reads=[K(f"ps{bank}")], writes=[K("qsqb")], tag="sqq")
                        B.group("pe", [(lambda cw: lambda e: e.matmul(
                            ps[2].ap()[0:2, 0:cw], blkonesb.ap()[:, :],
                            qsqb.ap()[:, 0:cw], start=True, stop=True))(cw)],
                            reads=[K("qsqb"), K("blkonesb")], writes=[K("ps2")], tag="ssq")
                        B.op("act", (lambda cw: lambda e: e.activation(
                            rowa.ap()[:, 0:cw], ps[2].ap()[0:2, 0:cw], AF.Sqrt))(cw),
                            reads=[K("ps2")], writes=[K("rowa")], tag="nrm")
                        B.op("dve", (lambda cw: lambda e: e.tensor_scalar_max(
                            rowa.ap()[:, 0:cw], rowa.ap()[:, 0:cw], 1e-12))(cw),
                            reads=[K("rowa")], writes=[K("rowa")], tag="nmx")
                        B.op("dve", (lambda cw: lambda e: e.reciprocal(
                            rowa.ap()[:, 0:cw], rowa.ap()[:, 0:cw]))(cw),
                            reads=[K("rowa")], writes=[K("rowa")], tag="nrc")
                        B.op("dve", (lambda cw: lambda e: e.tensor_copy(
                            rowhi.ap()[:, 0:cw], rowa.ap()[:, 0:cw]))(cw),
                            reads=[K("rowa")], writes=[K("rowhi")], tag="nhi")
                        B.op("dve", (lambda cw: lambda e: e.tensor_tensor(
                            rowlo.ap()[:, 0:cw], rowa.ap()[:, 0:cw],
                            rowhi.ap()[:, 0:cw], ALU.subtract))(cw),
                            reads=[K("rowa"), K("rowhi")], writes=[K("rowlo")], tag="nlo")
                        B.group("pe", [
                            (lambda patb, cw: lambda e: e.matmul(
                                ps[3].ap()[:, 0:cw], patb.ap()[:, :],
                                rowhi.ap()[:, 0:cw], start=True, stop=False))(patb, cw),
                            (lambda patb, cw: lambda e: e.matmul(
                                ps[3].ap()[:, 0:cw], patb.ap()[:, :],
                                rowlo.ap()[:, 0:cw], start=False, stop=True))(patb, cw)],
                            reads=[K("rowhi"), K("rowlo"), K(NAMEOF[id(patb)])],
                            writes=[K("ps3")], tag="rnbc")
                        B.op("act", (lambda cw: lambda e: e.activation(
                            uf32b.ap()[:, 0:cw], ps[3].ap()[:, 0:cw], AF.Copy))(cw),
                            reads=[K("ps3")], writes=[K("uf32b")], tag="rnev")
                        B.op("dve", (lambda bank, cw: lambda e: e.tensor_tensor(
                            uf32a.ap()[:, 0:cw], ps[bank].ap()[:, 0:cw],
                            uf32b.ap()[:, 0:cw], ALU.mult))(bank, cw),
                            reads=[K(f"ps{bank}"), K("uf32b")], writes=[K("uf32a")], tag="q1f")
                        B.op("dve", (lambda cw: lambda e: e.tensor_copy(
                            q1b.ap()[:, 0:cw], uf32a.ap()[:, 0:cw]))(cw),
                            reads=[K("uf32a")], writes=[K("q1b")], tag="q1b")
                        B.group("pe", [(lambda cw: lambda e: e.matmul(
                            ps[2].ap()[:, 0:cw], protb.ap()[:, :],
                            q1b.ap()[:, 0:cw], start=True, stop=True))(cw)],
                            reads=[K("q1b"), K("protb")], writes=[K("ps2")], tag="rot")
                        toff = cc * 512
                        B.op("dve", (lambda tA_, cw, toff: lambda e: e.tensor_tensor(
                            uf32a.ap()[:, 0:cw], tA_.ap()[:, toff:toff + cw],
                            uf32a.ap()[:, 0:cw], ALU.mult))(tA_, cw, toff),
                            reads=[K(NAMEOF[id(tA_)]), K("uf32a")], writes=[K("uf32a")], tag="rotA")
                        B.op("dve", (lambda tB_, cw, toff: lambda e: e.tensor_tensor(
                            uf32b.ap()[:, 0:cw], tB_.ap()[:, toff:toff + cw],
                            ps[2].ap()[:, 0:cw], ALU.mult))(tB_, cw, toff),
                            reads=[K(NAMEOF[id(tB_)]), K("ps2")], writes=[K("uf32b")], tag="rotB")
                        B.op("dve", (lambda obuf, hp, ospan, cw, cc: lambda e: e.tensor_tensor(
                            obuf.ap()[:, hp * ospan + cc * 512:hp * ospan + cc * 512 + cw],
                            uf32a.ap()[:, 0:cw], uf32b.ap()[:, 0:cw],
                            ALU.add))(obuf, hp, ospan, cw, cc),
                            reads=[K("uf32a"), K("uf32b")],
                            writes=[K(NAMEOF[id(obuf)], (hp * ospan + cc * 512) * 2,
                                      (hp * ospan + cc * 512 + cw) * 2)], tag="qkrot")
            for vh in range(8):
                B.op("dmw", (lambda vh, l=l: lambda e: e.dma_start(
                    wvb.ap()[:, :].rearrange("p (dt h) -> p dt h", dt=NT),
                    Wqkv[l, :, 2 * D + vh * 128:2 * D + (vh + 1) * 128].rearrange(
                        "(dt p) h -> p dt h", p=128)))(vh),
                    writes=[K("wvb")], tag="wv")
                for tt in range(ntt):
                    bank = 4 + tt % 2
                    fns = [(lambda dt, tt, bank: lambda e: e.matmul(
                        ps[bank].ap()[:, 0:128],
                        xsb.ap()[:, dt * 768 + tt * 128:dt * 768 + (tt + 1) * 128],
                        wvb.ap()[:, dt * 128:(dt + 1) * 128],
                        start=(dt == 0), stop=(dt == NT - 1)))(dt, tt, bank)
                        for dt in range(NT)]
                    B.group("pe", fns, reads=[K("xsb"), K("wvb")],
                            writes=[K(f"ps{bank}")], tag="vmm")
                    B.op("act", (lambda tt, vh, bank: lambda e: e.activation(
                        vbuf.ap()[:, tt * 1024 + vh * 128:tt * 1024 + (vh + 1) * 128],
                        ps[bank].ap()[:, 0:128], AF.Copy))(tt, vh, bank),
                        reads=[K(f"ps{bank}")],
                        writes=[K("vbuf", (tt * 1024 + vh * 128) * 2,
                                  (tt * 1024 + (vh + 1) * 128) * 2)],
                        tag="vev")
            for w in range(g0, g1 + 1):
                qw0 = max(Aq, w * WIN)
                qw1 = (w + 1) * WIN
                qn = qw1 - qw0
                jb0 = (w - 1) * WIN - k0
                for h in range(HEADS):
                    hp, hs = h // 2, (h % 2) * 64
                    live = []
                    for jc in range(4):
                        base1 = qw0 - (w - 1) * WIN - 128 * jc
                        base2 = (w - 1) * WIN + 128 * jc + WIN - qw0
                        if base1 + qn - 1 < 0 or base2 + 127 < 0:
                            continue
                        live.append(jc)
                    fns = []
                    for jc in live:
                        bank, colo = 2 + jc // 2, (jc % 2) * 256
                        fns.append((lambda jc, bank, colo, hp, hs, qn, qw0, jb0=jb0, q0=q0: lambda e: e.matmul(
                            ps[bank].ap()[:, colo:colo + qn],
                            kbuf.ap()[hs:hs + 64,
                                      hp * 768 + jb0 + jc * 128:hp * 768 + jb0 + (jc + 1) * 128],
                            qbuf.ap()[hs:hs + 64,
                                      hp * 512 + (qw0 - q0):hp * 512 + (qw0 - q0) + qn],
                            start=True, stop=True))(jc, bank, colo, hp, hs, qn, qw0))
                    B.group("pe", fns, reads=[K("kbuf"), K("qbuf")],
                            writes=[K("ps2"), K("ps3")], tag="sc")
                    for jc in range(4):
                        pk = K("ptb", jc * 512, jc * 512 + qn * 2)
                        if jc not in live:
                            B.op("pool", (lambda jc, qn: lambda e: e.memset(
                                ptb.ap()[:, jc * 256:jc * 256 + qn], 0.0))(jc, qn),
                                writes=[pk], tag="pz")
                            continue
                        bank, colo = 2 + jc // 2, (jc % 2) * 256
                        B.op("act", (lambda jc, bank, colo, qn: lambda e: e.activation(
                            ptb.ap()[:, jc * 256:jc * 256 + qn],
                            ps[bank].ap()[:, colo:colo + qn],
                            AF.Exp, bias=0.0, scale=8.0))(jc, bank, colo, qn),
                            reads=[K(f"ps{bank}")], writes=[pk], tag="exp")
                        base1 = qw0 - (w - 1) * WIN - 128 * jc
                        if base1 - 127 < 0:
                            B.op("pool", (lambda jc, qn, base1: lambda e: e.affine_select(
                                out=ptb.ap()[:, jc * 256:jc * 256 + qn],
                                in_=ptb.ap()[:, jc * 256:jc * 256 + qn],
                                compare_op=ALU.is_ge, fill=0.0, base=base1,
                                pattern=[[1, qn]], channel_multiplier=-1))(jc, qn, base1),
                                reads=[pk], writes=[pk], tag="m1")
                        base2 = (w - 1) * WIN + 128 * jc + WIN - qw0
                        if base2 - (qn - 1) < 0:
                            B.op("pool", (lambda jc, qn, base2: lambda e: e.affine_select(
                                out=ptb.ap()[:, jc * 256:jc * 256 + qn],
                                in_=ptb.ap()[:, jc * 256:jc * 256 + qn],
                                compare_op=ALU.is_ge, fill=0.0, base=base2,
                                pattern=[[-1, qn]], channel_multiplier=1))(jc, qn, base2),
                                reads=[pk], writes=[pk], tag="m2")
                    fns = [(lambda jc, qn: lambda e: e.matmul(
                        ps[5].ap()[:, 0:qn], onesmatb.ap()[:, :],
                        ptb.ap()[:, jc * 256:jc * 256 + qn],
                        start=(jc == 0), stop=False))(jc, qn) for jc in range(4)]
                    if 1536 <= qw0 < 2048:
                        fns.append((lambda qn, qw0: lambda e: e.matmul(
                            ps[5].ap()[:, 0:qn], negonesb.ap()[:, :],
                            corrb.ap()[:, qw0 - 1536:qw0 - 1536 + qn],
                            start=False, stop=True))(qn, qw0))
                    else:
                        fns[-1] = (lambda qn: lambda e: e.matmul(
                            ps[5].ap()[:, 0:qn], onesmatb.ap()[:, :],
                            ptb.ap()[:, 3 * 256:3 * 256 + qn],
                            start=False, stop=True))(qn)
                    B.group("pe", fns,
                            reads=[K("ptb"), K("onesmatb"), K("negonesb"), K("corrb")],
                            writes=[K("ps5")], tag="sum")
                    fns = [(lambda jc, qn, h, jb0=jb0: lambda e: e.matmul(
                        ps[6].ap()[0:64, 0:qn],
                        vbuf.ap()[:, (jb0 // 128 + jc) * 1024 + h * 64:
                                  (jb0 // 128 + jc) * 1024 + (h + 1) * 64],
                        ptb.ap()[:, jc * 256:jc * 256 + qn],
                        start=(jc == 0), stop=(jc == 3)))(jc, qn, h) for jc in range(4)]
                    B.group("pe", fns, reads=[K("ptb"), K("vbuf")],
                            writes=[K("ps6")], tag="av")
                    B.op("dve", (lambda qn: lambda e: e.reciprocal(
                        rsbuf.ap()[:, 0:qn], ps[5].ap()[:, 0:qn]))(qn),
                        reads=[K("ps5")], writes=[K("rsbuf")], tag="rs")
                    B.op("dve", (lambda qn, hp, hs, qw0, q0=q0: lambda e: e.tensor_tensor(
                        aob.ap()[hs:hs + 64,
                                 hp * 512 + (qw0 - q0):hp * 512 + (qw0 - q0) + qn],
                        ps[6].ap()[0:64, 0:qn], rsbuf.ap()[0:64, 0:qn],
                        ALU.mult))(qn, hp, hs, qw0),
                        reads=[K("ps6"), K("rsbuf")],
                        writes=[K("aob", (hp * 512 + (qw0 - q0)) * 2,
                                  (hp * 512 + (qw0 - q0) + qn) * 2)], tag="aon")
            for et in range(NT):
                wsel = et % 2
                B.op("dmw" if wsel == 0 else "dmw1",
                     (lambda et, wsel, l=l: lambda e: e.dma_start(
                    wqkb.ap()[:, wsel * 1024:(wsel + 1) * 1024].rearrange(
                        "p (it ec) -> p it ec", it=NT),
                    Wo_h[l, :, et * 128:(et + 1) * 128].rearrange(
                        "(it p) ec -> p it ec", p=128)))(et, wsel),
                    writes=[K("wqkb", wsel * 2048, (wsel + 1) * 2048)], tag="wo")
                bank = 6 + et % 2
                fns = [(lambda it, qspan, bank, wsel: lambda e: e.matmul(
                    ps[bank].ap()[:, 0:qspan],
                    wqkb.ap()[:, wsel * 1024 + it * 128:wsel * 1024 + (it + 1) * 128],
                    aob.ap()[:, it * 512:it * 512 + qspan],
                    start=(it == 0), stop=(it == NT - 1)))(it, qspan, bank, wsel)
                    for it in range(NT)]
                B.group("pe", fns,
                        reads=[K("wqkb", wsel * 2048, (wsel + 1) * 2048), K("aob")],
                        writes=[K(f"ps{bank}")], tag="womm")
                B.op("dve", (lambda et, qspan, bank, q0: lambda e: e.tensor_tensor(
                    st_ap(et, q0, q0 + qspan), st_ap(et, q0, q0 + qspan),
                    ps[bank].ap()[:, 0:qspan], ALU.add))(et, qspan, bank, q0),
                    reads=[st_key(et, q0, q0 + qspan), K(f"ps{bank}")],
                    writes=[st_key(et, q0, q0 + qspan)], tag="ares")
        B.barrier()

    def emit_ffn(l):
        Fq = FFN_OUT[l]
        B.op("dmw", (lambda l=l: lambda e: e.dma_start(gamrowb.ap()[:, :], ln_g[l, :, :]))(),
             writes=[K("gamrowb")], tag="gam")
        B.op("dmx", (lambda l=l: lambda e: e.dma_start(betcolb.ap()[:, :], ln_b[l, :, :]))(),
             writes=[K("betcolb")], tag="bet")
        span_all = S - Fq
        nblk = (span_all + FFN_TB - 1) // FFN_TB
        for blk in range(nblk):
            t0 = Fq + blk * FFN_TB
            t1 = min(S, t0 + FFN_TB)
            tb = t1 - t0
            nch = (tb + 511) // 512
            if t1 < S:
                sb0 = (blk % 2) * 4
                for dt in range(4, NT):
                    B.op("dve", (lambda dt, t1, sb0: lambda e: e.tensor_copy(
                        shsnap.ap()[:, sb0 + dt - 4:sb0 + dt - 3],
                        st_ap(dt, t1 - 1, t1)))(dt, t1, sb0),
                        reads=[st_key(dt, t1 - 1, t1)],
                        writes=[K("shsnap", (sb0 + dt - 4) * 4, (sb0 + dt - 3) * 4)],
                        tag="shsn")
            for cc in range(nch):
                cw = min(512, tb - cc * 512)
                cs0 = t0 + cc * 512
                for dt in range(NT):
                    src0 = cs0 if dt < 4 else cs0 - 1
                    patch = (blk > 0 and cc == 0 and dt >= 4)
                    if patch:
                        pb0 = ((blk - 1) % 2) * 4
                        B.op("dve", (lambda dt, pb0: lambda e: e.tensor_copy(
                            fxbb.ap()[:, 0:1], shsnap.ap()[:, pb0 + dt - 4:pb0 + dt - 3]))(dt, pb0),
                            reads=[K("shsnap", (pb0 + dt - 4) * 4, (pb0 + dt - 3) * 4)],
                            writes=[K("fxbb", 0, 2)], tag="fxb0")
                        B.op("act", (lambda dt, cs0, cw: lambda e: e.activation(
                            fxbb.ap()[:, 1:cw], st_ap(dt, cs0, cs0 + cw - 1),
                            AF.Copy))(dt, cs0, cw),
                            reads=[st_key(dt, cs0, cs0 + cw - 1)],
                            writes=[K("fxbb", 2, 2 * cw)], tag="fxb")
                    else:
                        B.op("act", (lambda dt, src0, cw: lambda e: e.activation(
                            fxbb.ap()[:, 0:cw], st_ap(dt, src0, src0 + cw), AF.Copy))(dt, src0, cw),
                            reads=[st_key(dt, src0, src0 + cw)], writes=[K("fxbb")], tag="fxb")
                    B.group("pe", [(lambda dt, cw: lambda e: e.matmul(
                        ps[6].ap()[0:1, 0:cw], onesmatb.ap()[:, 0:1],
                        fxbb.ap()[:, 0:cw], start=(dt == 0), stop=(dt == NT - 1)))(dt, cw)],
                        reads=[K("fxbb"), K("onesmatb")], writes=[K("ps6")], tag="fsx")
                    B.op("act", (lambda cw: lambda e: e.activation(
                        fsqb.ap()[:, 0:cw], fxbb.ap()[:, 0:cw], AF.Square))(cw),
                        reads=[K("fxbb")], writes=[K("fsqb")], tag="fsq")
                    B.group("pe", [(lambda dt, cw: lambda e: e.matmul(
                        ps[7].ap()[0:1, 0:cw], onesmatb.ap()[:, 0:1],
                        fsqb.ap()[:, 0:cw], start=(dt == 0), stop=(dt == NT - 1)))(dt, cw)],
                        reads=[K("fsqb"), K("onesmatb")], writes=[K("ps7")], tag="fsx2")
                B.op("act", (lambda cw: lambda e: e.activation(
                    rowa.ap()[0:1, 0:cw], ps[6].ap()[0:1, 0:cw],
                    AF.Identity, bias=0.0, scale=1.0 / D))(cw),
                    reads=[K("ps6")], writes=[K("rowa", 0, 2048)], tag="fmu")
                B.op("act", (lambda cw: lambda e: e.activation(
                    rowb_.ap()[0:1, 0:cw], rowa.ap()[0:1, 0:cw], AF.Square))(cw),
                    reads=[K("rowa", 0, 2048)], writes=[K("rowb", 0, 2048)], tag="fmu2")
                B.op("dve", (lambda cw: lambda e: e.scalar_tensor_tensor(
                    rowb_.ap()[0:1, 0:cw], ps[7].ap()[0:1, 0:cw], 1.0 / D,
                    rowb_.ap()[0:1, 0:cw], ALU.mult, ALU.subtract))(cw),
                    reads=[K("ps7"), K("rowb", 0, 2048)], writes=[K("rowb", 0, 2048)],
                    tag="fvar")
                B.op("dve", (lambda cw: lambda e: e.tensor_scalar_add(
                    rowb_.ap()[0:1, 0:cw], rowb_.ap()[0:1, 0:cw], 1e-5))(cw),
                    reads=[K("rowb", 0, 2048)], writes=[K("rowb", 0, 2048)], tag="feps")
                B.op("act", (lambda cw: lambda e: e.activation(
                    rowb_.ap()[0:1, 0:cw], rowb_.ap()[0:1, 0:cw],
                    AF.Sqrt))(cw),
                    reads=[K("rowb", 0, 2048)], writes=[K("rowb", 0, 2048)], tag="fstd")
                B.op("dve", (lambda cw: lambda e: e.reciprocal(
                    rowb_.ap()[0:1, 0:cw], rowb_.ap()[0:1, 0:cw]))(cw),
                    reads=[K("rowb", 0, 2048)], writes=[K("rowb", 0, 2048)], tag="frstd")
                B.op("dve", (lambda cw: lambda e: e.scalar_tensor_tensor(
                    rowa.ap()[0:1, 0:cw], rowa.ap()[0:1, 0:cw], -1.0,
                    rowb_.ap()[0:1, 0:cw], ALU.mult, ALU.mult))(cw),
                    reads=[K("rowa", 0, 2048), K("rowb", 0, 2048)],
                    writes=[K("rowa", 0, 2048)], tag="fr2")
                B.op("dve", (lambda cw: lambda e: e.tensor_copy(
                    rowhi.ap()[0:1, 0:cw], rowb_.ap()[0:1, 0:cw]))(cw),
                    reads=[K("rowb", 0, 2048)], writes=[K("rowhi", 0, 1024)], tag="fh1")
                B.op("dve", (lambda cw: lambda e: e.tensor_tensor(
                    rowlo.ap()[0:1, 0:cw], rowb_.ap()[0:1, 0:cw],
                    rowhi.ap()[0:1, 0:cw], ALU.subtract))(cw),
                    reads=[K("rowb", 0, 2048), K("rowhi", 0, 1024)],
                    writes=[K("rowlo", 0, 1024)], tag="fl1")
                B.op("dve", (lambda cw: lambda e: e.tensor_copy(
                    rowhi2.ap()[0:1, 0:cw], rowa.ap()[0:1, 0:cw]))(cw),
                    reads=[K("rowa", 0, 2048)], writes=[K("rowhi2")], tag="fh2")
                B.op("dve", (lambda cw: lambda e: e.tensor_tensor(
                    rowlo2.ap()[0:1, 0:cw], rowa.ap()[0:1, 0:cw],
                    rowhi2.ap()[0:1, 0:cw], ALU.subtract))(cw),
                    reads=[K("rowa", 0, 2048), K("rowhi2")],
                    writes=[K("rowlo2")], tag="fl2")
                for dt in range(NT):
                    B.group("pe", [
                        (lambda dt, cw: lambda e: e.matmul(
                            ps[4].ap()[:, 0:cw], gamrowb.ap()[:, dt * 128:(dt + 1) * 128],
                            rowhi.ap()[0:1, 0:cw], start=True, stop=False))(dt, cw),
                        (lambda dt, cw: lambda e: e.matmul(
                            ps[4].ap()[:, 0:cw], gamrowb.ap()[:, dt * 128:(dt + 1) * 128],
                            rowlo.ap()[0:1, 0:cw], start=False, stop=True))(dt, cw)],
                        reads=[K("rowhi", 0, 1024), K("rowlo", 0, 1024), K("gamrowb")],
                        writes=[K("ps4")], tag="fbc1")
                    B.group("pe", [
                        (lambda dt, cw: lambda e: e.matmul(
                            ps[5].ap()[:, 0:cw], gamrowb.ap()[:, dt * 128:(dt + 1) * 128],
                            rowhi2.ap()[0:1, 0:cw], start=True, stop=False))(dt, cw),
                        (lambda dt, cw: lambda e: e.matmul(
                            ps[5].ap()[:, 0:cw], gamrowb.ap()[:, dt * 128:(dt + 1) * 128],
                            rowlo2.ap()[0:1, 0:cw], start=False, stop=True))(dt, cw)],
                        reads=[K("rowhi2"), K("rowlo2"), K("gamrowb")],
                        writes=[K("ps5")], tag="fbc2")
                    src0 = cs0 if dt < 4 else cs0 - 1
                    if blk > 0 and cc == 0 and dt >= 4:
                        pb0 = ((blk - 1) % 2) * 4
                        B.op("dve", (lambda dt, pb0: lambda e: e.tensor_tensor(
                            ft32.ap()[:, 0:1], shsnap.ap()[:, pb0 + dt - 4:pb0 + dt - 3],
                            ps[4].ap()[:, 0:1], ALU.mult))(dt, pb0),
                            reads=[K("shsnap", (pb0 + dt - 4) * 4, (pb0 + dt - 3) * 4), K("ps4")],
                            writes=[K("ft32", 0, 4)], tag="fxn1a")
                        B.op("dve", (lambda dt, cs0, cw: lambda e: e.tensor_tensor(
                            ft32.ap()[:, 1:cw], st_ap(dt, cs0, cs0 + cw - 1),
                            ps[4].ap()[:, 1:cw], ALU.mult))(dt, cs0, cw),
                            reads=[st_key(dt, cs0, cs0 + cw - 1), K("ps4")],
                            writes=[K("ft32", 4, 4 * cw)], tag="fxn1b")
                    else:
                        B.op("dve", (lambda dt, src0, cw: lambda e: e.tensor_tensor(
                            ft32.ap()[:, 0:cw], st_ap(dt, src0, src0 + cw),
                            ps[4].ap()[:, 0:cw], ALU.mult))(dt, src0, cw),
                            reads=[st_key(dt, src0, src0 + cw), K("ps4")],
                            writes=[K("ft32")], tag="fxn1")
                    B.op("dve", (lambda dt, cc, cw: lambda e: e.scalar_tensor_tensor(
                        xnfb.ap()[:, dt * FFN_TB + cc * 512:dt * FFN_TB + cc * 512 + cw],
                        ft32.ap()[:, 0:cw], betcolb.ap()[:, dt:dt + 1],
                        ps[5].ap()[:, 0:cw], ALU.add, ALU.add))(dt, cc, cw),
                        reads=[K("ft32"), K("betcolb"), K("ps5")],
                        writes=[K("xnfb", (dt * FFN_TB + cc * 512) * 2,
                                  (dt * FFN_TB + cc * 512 + cw) * 2)], tag="fxn2")
            for ip in range(NI_T):
                iw = min(128, DI - ip * 128)
                wsel = ip % 2
                B.op("dmw" if wsel == 0 else "dmw1",
                     (lambda ip, iw, wsel, l=l: lambda e: e.dma_start(
                    w1bb.ap()[:, wsel * 2048:wsel * 2048 + NT * iw].rearrange(
                        "p (dt h) -> p dt h", dt=NT),
                    W1_h[l, :, ip * 128:ip * 128 + iw].rearrange(
                        "(dt p) h -> p dt h", p=128)))(ip, iw, wsel),
                    writes=[K("w1bb", wsel * 4096, wsel * 4096 + 4096)], tag="w1a")
                B.op("dmw" if wsel == 0 else "dmw1",
                     (lambda ip, iw, wsel, l=l: lambda e: e.dma_start(
                    w1bb.ap()[:, wsel * 2048 + NT * iw:wsel * 2048 + 2 * NT * iw].rearrange(
                        "p (dt h) -> p dt h", dt=NT),
                    W1_h[l, :, DI + ip * 128:DI + ip * 128 + iw].rearrange(
                        "(dt p) h -> p dt h", p=128)))(ip, iw, wsel),
                    writes=[K("w1bb", wsel * 4096, wsel * 4096 + 4096)], tag="w1g")
                B.op("dmw", (lambda ip, iw, l=l: lambda e: e.dma_start(
                    b1rowb.ap()[0:1, 0:iw], b1_h[l, :, ip * 128:ip * 128 + iw]))(ip, iw),
                    writes=[K("b1rowb", 0, 256)], tag="b1a")
                B.op("dmw", (lambda ip, iw, l=l: lambda e: e.dma_start(
                    b1rowb.ap()[0:1, 128:128 + iw],
                    b1_h[l, :, DI + ip * 128:DI + ip * 128 + iw]))(ip, iw),
                    writes=[K("b1rowb", 256, 512)], tag="b1g")
                for cc in range(nch):
                    cw = min(512, tb - cc * 512)
                    for ag in range(2):
                        bank = ag
                        fns = [(lambda dt, iw, cc, cw, ag, bank, wsel: lambda e: e.matmul(
                            ps[bank].ap()[0:iw, 0:cw],
                            w1bb.ap()[:, wsel * 2048 + ag * NT * iw + dt * iw:
                                      wsel * 2048 + ag * NT * iw + (dt + 1) * iw],
                            xnfb.ap()[:, dt * FFN_TB + cc * 512:dt * FFN_TB + cc * 512 + cw],
                            start=(dt == 0), stop=False))(dt, iw, cc, cw, ag, bank, wsel)
                            for dt in range(NT)]
                        fns.append((lambda iw, cc, cw, ag, bank: lambda e: e.matmul(
                            ps[bank].ap()[0:iw, 0:cw],
                            b1rowb.ap()[0:1, ag * 128:ag * 128 + iw],
                            onesrowb.ap()[:, 0:cw], start=False, stop=True))(iw, cc, cw, ag, bank))
                        B.group("pe", fns,
                                reads=[K("w1bb", wsel * 4096, wsel * 4096 + 4096),
                                       K("xnfb"), K("b1rowb"), K("onesrowb")],
                                writes=[K(f"ps{bank}")], tag="w1mm")
                    B.op("act", (lambda iw, cw: lambda e: e.activation(
                        gelbb.ap()[0:iw, 0:cw], ps[1].ap()[0:iw, 0:cw], AF.Gelu))(iw, cw),
                        reads=[K("ps1")], writes=[K("gelbb")], tag="gelu")
                    B.op("dve", (lambda ip, iw, cc, cw: lambda e: e.tensor_tensor(
                        gatedb.ap()[0:iw, ip * FFN_TB + cc * 512:ip * FFN_TB + cc * 512 + cw],
                        ps[0].ap()[0:iw, 0:cw], gelbb.ap()[0:iw, 0:cw],
                        ALU.mult))(ip, iw, cc, cw),
                        reads=[K("ps0"), K("gelbb")],
                        writes=[K("gatedb", (ip * FFN_TB + cc * 512) * 2,
                                  (ip * FFN_TB + cc * 512 + cw) * 2)], tag="gated")
            for et in range(NT):
                wsel = et % 2
                B.op("dmw", (lambda et, l=l: lambda e: e.dma_start(
                    b2rowb.ap()[:, :], b2_h[l, :, et * 128:(et + 1) * 128]))(et),
                    writes=[K("b2rowb")], tag="b2")
                B.op("dmw" if wsel == 0 else "dmw1",
                     (lambda et, wsel, l=l: lambda e: e.dma_start(
                    w2bb.ap()[:, wsel * NI_T * 128:wsel * NI_T * 128 + 21 * 128].rearrange(
                        "p (it ec) -> p it ec", it=21),
                    W2_h[l, 0:2688, et * 128:(et + 1) * 128].rearrange(
                        "(it p) ec -> p it ec", p=128)))(et, wsel),
                    writes=[K("w2bb", wsel * 5632, (wsel + 1) * 5632)], tag="w2a")
                B.op("dmw" if wsel == 0 else "dmw1",
                     (lambda et, wsel, l=l: lambda e: e.dma_start(
                    w2bb.ap()[0:42, wsel * NI_T * 128 + 21 * 128:wsel * NI_T * 128 + 22 * 128],
                    W2_h[l, 2688:2730, et * 128:(et + 1) * 128]))(et, wsel),
                    writes=[K("w2bb", wsel * 5632, (wsel + 1) * 5632)], tag="w2b")
                for cc in range(nch):
                    cw = min(512, tb - cc * 512)
                    bank = 2 + cc % 2
                    fns = []
                    for it in range(NI_T):
                        iw = min(128, DI - it * 128)
                        fns.append((lambda it, iw, cw, bank, wsel, cc=cc: lambda e: e.matmul(
                            ps[bank].ap()[:, 0:cw],
                            w2bb.ap()[0:iw, wsel * NI_T * 128 + it * 128:
                                      wsel * NI_T * 128 + it * 128 + 128],
                            gatedb.ap()[0:iw, it * FFN_TB + cc * 512:it * FFN_TB + cc * 512 + cw],
                            start=(it == 0), stop=False))(it, iw, cw, bank, wsel))
                    fns.append((lambda et, cw, bank: lambda e: e.matmul(
                        ps[bank].ap()[:, 0:cw], b2rowb.ap()[:, 0:128],
                        onesrowb.ap()[:, 0:cw], start=False, stop=True))(et, cw, bank))
                    B.group("pe", fns,
                            reads=[K("w2bb", wsel * 5632, (wsel + 1) * 5632),
                                   K("gatedb"), K("b2rowb"), K("onesrowb")],
                            writes=[K(f"ps{bank}")], tag="w2mm")
                    B.op("dve", (lambda et, t0, cc, cw, bank: lambda e: e.tensor_tensor(
                        st_ap(et, t0 + cc * 512, t0 + cc * 512 + cw),
                        st_ap(et, t0 + cc * 512, t0 + cc * 512 + cw),
                        ps[bank].ap()[:, 0:cw], ALU.add))(et, t0, cc, cw, bank),
                        reads=[st_key(et, t0 + cc * 512, t0 + cc * 512 + cw),
                               K(f"ps{bank}")],
                        writes=[st_key(et, t0 + cc * 512, t0 + cc * 512 + cw)],
                        tag="fres")
        if l < NLAYER - 1:
            z0, z1 = Fq, HALO
            c = z0
            while c < z1:
                cw = min(512, z1 - c)
                for dt in range(NT):
                    B.op("act", (lambda dt, c, cw: lambda e: e.activation(
                        st_ap(dt, c, c + cw), st_ap(dt, c, c + cw),
                        AF.Identity, bias=0.0, scale=zflagb.ap()[:, :]))(dt, c, cw),
                        reads=[st_key(dt, c, c + cw), K("zflagb")],
                        writes=[st_key(dt, c, c + cw)], tag="zap")
                c += cw
        B.barrier()

    for sl in range(min(sublayers, 2 * NLAYER)):
        l, which = sl // 2, sl % 2
        if which == 0:
            emit_attn(l)
        else:
            emit_ffn(l)

    for oc in range(OWN // 512):
        c0 = HALO + oc * 512
        B.op("dmx", (lambda oc: lambda e: e.dma_start(
            sldf.ap()[:, :].rearrange("p (dt c) -> p dt c", dt=NT),
            s_scr[:, oc * 512:(oc + 1) * 512].rearrange("(dt p) c -> p dt c", p=128)))(oc),
            writes=[K("sldf")], tag="s_ld")
        for dt in range(NT):
            B.op("act", (lambda dt, c0: lambda e: e.activation(
                sgf.ap()[:, dt * 512:(dt + 1) * 512],
                st_ap(dt, c0, c0 + 512), AF.Silu))(dt, c0),
                reads=[st_key(dt, c0, c0 + 512)],
                writes=[K("sgf", dt * 2048, (dt + 1) * 2048)], tag="silu")
            B.op("dve", (lambda dt: lambda e: e.tensor_tensor(
                sgf.ap()[:, dt * 512:(dt + 1) * 512],
                sgf.ap()[:, dt * 512:(dt + 1) * 512],
                sldf.ap()[:, dt * 512:(dt + 1) * 512], ALU.mult))(dt),
                reads=[K("sgf", dt * 2048, (dt + 1) * 2048),
                       K("sldf", dt * 2048, (dt + 1) * 2048)],
                writes=[K("sgf", dt * 2048, (dt + 1) * 2048)], tag="sg")
        for tt in range(4):
            t0 = oc * 512 + tt * 128
            B.op("dmx", (lambda t0: lambda e: e.dma_start(
                xotf.ap()[:, :], x_own[t0:t0 + 128, :]))(t0),
                writes=[K("xotf")], tag="xo_ld")
            fns = []
            for dt in range(NT):
                bank = 2 + dt // 4
                col = (dt % 4) * 128
                fns.append((lambda dt, tt, bank, col: lambda e: e.transpose(
                    ps[bank].ap()[:, col:col + 128],
                    sgf.ap()[:, dt * 512 + tt * 128:dt * 512 + tt * 128 + 128],
                    identb.ap()[:, :]))(dt, tt, bank, col))
            B.group("pe", fns, reads=[K("sgf"), K("identb")],
                    writes=[K("ps2"), K("ps3")], tag="sgT")
            B.op("dve", lambda e: e.tensor_tensor(
                xotf.ap()[:, 0:512], xotf.ap()[:, 0:512], ps[2].ap()[:, :], ALU.add),
                reads=[K("xotf"), K("ps2")], writes=[K("xotf")], tag="oadd1")
            B.op("dve", lambda e: e.tensor_tensor(
                xotf.ap()[:, 512:1024], xotf.ap()[:, 512:1024], ps[3].ap()[:, :], ALU.add),
                reads=[K("xotf"), K("ps3")], writes=[K("xotf")], tag="oadd2")
            B.op("dms", (lambda t0: lambda e: e.dma_start(
                out_t[t0:t0 + 128, :], xotf.ap()[:, :]))(t0),
                reads=[K("xotf")], tag="out_st")

    if DEBUG_DUMP:
        dbgs = [("d_ahb", ahb, 8), ("d_bcol", bcolb, 8), ("d_carry", carryb, 8),
                ("d_ych", ych, NT * 512), ("d_sgf", sgf, NT * 512),
                ("d_sldf", sldf, NT * 512)]
        for nm, buf, w in dbgs:
            o = nc.dram_tensor(nm, [128, w], F32, kind="ExternalOutput").ap()
            B.op("dms", (lambda o, buf, w: lambda e: e.dma_start(
                o[:, :], buf.ap()[:, 0:w]))(o, buf, w),
                reads=[K(NAMEOF[id(buf)])], tag="dbg")
        o = nc.dram_tensor("d_stream", [128, 2048], F32, kind="ExternalOutput").ap()
        B.op("dms", lambda e: e.dma_start(o[:, :], st_ap(0, HALO, HALO + 2048)),
             reads=[st_key(0, HALO, HALO + 2048)], tag="dbg")
    B.barrier()
    B.lower(nc)
    return nc


_NC_CACHE = {}


def kernel(**inputs):
    in_maps = _stage_inputs(inputs)
    key = (SUBLAYERS, DEBUG_DUMP)
    if key not in _NC_CACHE:
        _NC_CACHE[key] = build_program(SUBLAYERS)
    nc = _NC_CACHE[key]
    res = run_bass_kernel_spmd(nc, in_maps, core_ids=list(range(8)))
    out = np.zeros((4, 4096, 1024), dtype=np.float32)
    for core in range(8):
        b, half = core // 2, core % 2
        out[b, half * OWN:(half + 1) * OWN, :] = np.asarray(res.results[core]["out"])
    return out


# version 03
# speedup vs baseline: 1.0025x; 1.0025x over previous
import numpy as np
import ml_dtypes
import concourse.bass as bass
import concourse.mybir as mybir
from concourse.bass_utils import run_bass_kernel_spmd

F32 = mybir.dt.float32
BF16 = mybir.dt.bfloat16
AF = mybir.ActivationFunctionType
ALU = mybir.AluOpType

OWN = 2048
HALO = 1792
S = OWN + HALO
SOFF = 244
SW = S - SOFF
SSM = 4096
D = 1024
NT = 8
HEADS = 16
DH = 64
WIN = 256
DI = 2730
NI_T = 22
NLAYER = 6
ATT_OUT = [501, 759, 1017, 1275, 1533, 1791]
FFN_OUT = [502, 760, 1018, 1276, 1534, 1792]
FFN_TB = 896

SUBLAYERS = 12
DEBUG_DUMP = False

bf16 = ml_dtypes.bfloat16


class Chan:
    def __init__(self, name, engine, inc):
        self.name, self.engine, self.inc = name, engine, inc


CHANNELS = {
    "pe": Chan("pe", "tensor", 1),
    "dve": Chan("dve", "vector", 1),
    "act": Chan("act", "scalar", 1),
    "pool": Chan("pool", "gpsimd", 1),
    "dmw": Chan("dmw", "gpsimd", 16),
    "dmw1": Chan("dmw1", "gpsimd", 16),
    "dmx": Chan("dmx", "sync", 16),
    "dms": Chan("dms", "sync", 16),
}
ENGINES = ("tensor", "vector", "scalar", "gpsimd", "sync")


NEPOCH = 6


class Builder:

    def __init__(self):
        self.items = {e: [] for e in ENGINES}
        self.cnt = {c: 0 for c in CHANNELS}
        self.dma_idx = {c: 0 for c in CHANNELS if CHANNELS[c].inc == 16}
        self.dma_cum = {}
        self.dma_rot = {c: False for c in self.dma_idx}
        self.last_w = {}
        self.readers = {}
        self.seen = {e: {} for e in ENGINES}
        self.serial = False

    def semkey(self, ch):
        return f"{ch}@{self.dma_idx[ch]}"

    @staticmethod
    def _ov(recs, lo, hi):
        return [x for x in recs if not (hi <= x[0] or x[1] <= lo)]

    def _dep(self, waits, semkey, val):
        ch = semkey.split("@")[0]
        if "@" in semkey:
            val = self.dma_cum.get(semkey, 0)
            if semkey == self.semkey(ch):
                self.dma_rot[ch] = True
        waits[semkey] = max(waits.get(semkey, 0), val)

    def op(self, ch, fn, reads=(), writes=(), tag=""):
        c = CHANNELS[ch]
        eng = c.engine
        waits = {}
        if self.serial:
            for chp, v in self.cnt.items():
                if v > 0:
                    waits[chp] = v
            for sk, v in self.dma_cum.items():
                if v > 0:
                    self._dep(waits, sk, v)
        else:
            for nm, lo, hi in reads:
                for _, _, sk, vv in self._ov(self.last_w.get(nm, []), lo, hi):
                    self._dep(waits, sk, vv)
            for nm, lo, hi in writes:
                for _, _, sk, vv in self._ov(self.last_w.get(nm, []), lo, hi):
                    self._dep(waits, sk, vv)
                for _, _, sk, vv in self._ov(self.readers.get(nm, []), lo, hi):
                    self._dep(waits, sk, vv)
        if c.inc == 16:
            if self.dma_rot[ch]:
                self.dma_idx[ch] = (self.dma_idx[ch] + 1) % NEPOCH
                self.dma_rot[ch] = False
                prev = self.dma_cum.get(self.semkey(ch), 0)
                if prev > 0:
                    waits[self.semkey(ch)] = max(
                        waits.get(self.semkey(ch), 0), prev)
            sk = self.semkey(ch)
            self.dma_cum[sk] = self.dma_cum.get(sk, 0) + 16
            ev = (sk, self.dma_cum[sk])
        else:
            self.cnt[ch] += 1
            ev = (ch, self.cnt[ch])
        final = []
        for skp, v in sorted(waits.items()):
            if v > self.seen[eng].get(skp, 0):
                final.append((skp, v))
                self.seen[eng][skp] = v
        self.items[eng].append((final, fn, ev[0], tag))
        for nm, lo, hi in reads:
            self.readers.setdefault(nm, []).append((lo, hi, ev[0], ev[1]))
        for nm, lo, hi in writes:
            lw = self.last_w.setdefault(nm, [])
            lw[:] = [x for x in lw if not (lo <= x[0] and x[1] <= hi)]
            lw.append((lo, hi, ev[0], ev[1]))
            rd = self.readers.get(nm)
            if rd:
                rd[:] = [x for x in rd if not (lo <= x[0] and x[1] <= hi)]
        return ev[1]

    def group(self, ch, fns, reads=(), writes=(), tag=""):
        def fused(e):
            inst = None
            for f in fns:
                inst = f(e)
            return inst
        return self.op(ch, fused, reads=reads, writes=writes, tag=tag)

    def barrier(self):
        snap = dict(self.cnt)
        snap.update(self.dma_cum)
        for ch in self.dma_idx:
            self.dma_rot[ch] = True
        for eng in ENGINES:
            w = []
            for skp, v in snap.items():
                if v > 0 and v > self.seen[eng].get(skp, 0):
                    w.append((skp, v))
                    self.seen[eng][skp] = v
            if w:
                self.items[eng].append((w, None, None, "barrier"))

    def lower(self, nc):
        sems = {}
        for chname, c in CHANNELS.items():
            if c.inc == 16:
                for i in range(NEPOCH):
                    sems[f"{chname}@{i}"] = nc.semaphore(f"sem_{chname}_{i}").__enter__()
            else:
                sems[chname] = nc.semaphore(f"sem_{chname}").__enter__()
        with nc.Block() as block:
            def body_for(eng_name):
                def body(eng):
                    for waits, fn, sk, tag in self.items[eng_name]:
                        for skp, v in waits:
                            eng.wait_ge(sems[skp], v)
                        if fn is not None:
                            inst = fn(eng)
                            inst.then_inc(sems[sk],
                                          16 if "@" in sk else 1)
                return body
            block.tensor(body_for("tensor"))
            block.vector(body_for("vector"))
            block.scalar(body_for("scalar"))
            block.gpsimd(body_for("gpsimd"))
            block.sync(body_for("sync"))


def _stage_inputs(inp):
    x = np.asarray(inp["L0_attn_x"], dtype=np.float32)

    prot = np.zeros((128, 128), dtype=np.float32)
    for d in range(128):
        dh = d % 64
        if dh < 32:
            prot[d + 32, d] = -1.0
        else:
            prot[d - 32, d] = 1.0

    half_idx = np.arange(0, DH, 2, dtype=np.float32)
    inv_freq = 1.0 / (10000.0 ** (half_idx / DH))
    base_scale = (half_idx + 0.4 * DH) / (1.4 * DH)
    dh_idx = np.arange(DH) % 32
    cs = np.arange(S, dtype=np.float32)

    def tables(own0):
        tg = (own0 + cs - HALO).astype(np.float32)
        theta = tg[None, :] * inv_freq[dh_idx][:, None]
        power = (tg - 2048.0) / 128.0
        scale = base_scale[dh_idx][:, None] ** power[None, :]
        A = np.cos(theta) * scale
        Bt = np.sin(theta) * scale
        C = np.cos(theta) / scale
        Dt = np.sin(theta) / scale
        two = lambda t: np.ascontiguousarray(
            np.concatenate([t, t], 0).astype(np.float32))
        return two(A), two(Bt), two(C), two(Dt)

    tabs = {0: tables(0), OWN: tables(OWN)}

    corr_left = np.zeros((1, 512), dtype=np.float32)
    ccs = np.arange(1536, 2048)
    tg = ccs - HALO
    corr_left[0, :] = np.where((tg >= 0) & (tg < 256), 256.0 - tg, 0.0)
    corr_left = corr_left.astype(bf16)
    corr_right = np.zeros((1, 512), dtype=bf16)


    fmT = lambda a: np.ascontiguousarray(a.T)
    col8 = lambda v: np.ascontiguousarray(v.reshape(8, 128).T)

    common = {
        "c_prot": prot.astype(bf16),
        "c_onescol": np.ones((128, 1), dtype=bf16),
        "c_onesmat": np.ones((128, 128), dtype=bf16),
        "c_onesrowf": np.ones((1, 128), dtype=np.float32),
        "c_onesrow": np.ones((1, 512), dtype=bf16),
        "c_negones": np.full((1, 128), -1.0, dtype=bf16),
        "c_ident": np.eye(128, dtype=np.float32),
        "c_blkones": np.ascontiguousarray(
            np.stack([np.r_[np.ones(64), np.zeros(64)],
                      np.r_[np.zeros(64), np.ones(64)]], 1).astype(bf16)),
        "ssm_a": col8(np.asarray(inp["L0_attn_a"], np.float32)),
        "ssm_b": col8(np.asarray(inp["L0_attn_b"], np.float32)),
        "w_outT": np.ascontiguousarray(np.asarray(inp["L0_attn_w_out"], np.float32).T),
        "b_out": np.asarray(inp["L0_attn_b_out"], np.float32).reshape(1, D),
        "Wqkv": np.asarray(inp["L0_ffn_Wqkv"], np.float32),
        "q_scale": np.asarray(inp["L0_ffn_q_scale"], np.float32),
        "k_scale": np.asarray(inp["L0_ffn_k_scale"], np.float32),
        "Wo": np.asarray(inp["L0_ffn_Wo"], np.float32),
        "ln_g": np.asarray(inp["L0_ffn_ln_g"], np.float32).reshape(NLAYER, 1, D),
        "ln_b": np.ascontiguousarray(np.stack(
            [col8(r) for r in np.asarray(inp["L0_ffn_ln_b"], np.float32)])),
        "W1": np.asarray(inp["L0_ffn_W1"], np.float32),
        "b1": np.asarray(inp["L0_ffn_b1"], np.float32).reshape(NLAYER, 1, 2 * DI),
        "W2": np.asarray(inp["L0_ffn_W2"], np.float32),
        "b2": np.asarray(inp["L0_ffn_b2"], np.float32).reshape(NLAYER, 1, D),
    }

    in_maps = []
    for core in range(8):
        b, half = core // 2, core % 2
        own0 = half * OWN
        if half == 0:
            xs = np.concatenate([np.zeros((OWN, D), np.float32), x[b, :OWN]], 0)
        else:
            xs = x[b]
        A, Bt, C, Dt = tabs[own0]
        m = dict(common)
        m.update({
            "x_ssm": fmT(xs),
            "x_own": np.ascontiguousarray(x[b, own0:own0 + OWN]),
            "corr": corr_left if half == 0 else corr_right,
            "zflag": (np.zeros((128, 1), np.float32) if half == 0
                      else np.ones((128, 1), np.float32)),
            "tabA": A, "tabB": Bt, "tabC": C, "tabD": Dt,
        })
        in_maps.append(m)
    return in_maps


def build_program(sublayers=SUBLAYERS):
    nc = bass.Bass("TRN2", target_bir_lowering=False)
    B = Builder()

    def din(name, shape, dtype=F32):
        return nc.dram_tensor(name, list(shape), dtype, kind="ExternalInput").ap()

    x_ssm = din("x_ssm", [D, SSM])
    x_own = din("x_own", [OWN, D])
    corr = din("corr", [1, 512], BF16)
    zflag = din("zflag", [128, 1])
    tabs_hbm = {"A": din("tabA", [128, S]), "B": din("tabB", [128, S]),
                "C": din("tabC", [128, S]), "D": din("tabD", [128, S])}
    c_prot = din("c_prot", [128, 128], BF16)
    c_onescol = din("c_onescol", [128, 1], BF16)
    c_onesmat = din("c_onesmat", [128, 128], BF16)
    c_onesrowf = din("c_onesrowf", [1, 128])
    c_onesrow = din("c_onesrow", [1, 512], BF16)
    c_negones = din("c_negones", [1, 128], BF16)
    c_ident = din("c_ident", [128, 128])
    c_blkones = din("c_blkones", [128, 2], BF16)
    ssm_a = din("ssm_a", [128, 8])
    ssm_b = din("ssm_b", [128, 8])
    w_outT = din("w_outT", [D, D])
    b_out = din("b_out", [1, D])
    Wqkv = din("Wqkv", [NLAYER, D, 3 * D])
    q_scale = din("q_scale", [NLAYER, DH])
    k_scale = din("k_scale", [NLAYER, DH])
    Wo_h = din("Wo", [NLAYER, D, D])
    ln_g = din("ln_g", [NLAYER, 1, D])
    ln_b = din("ln_b", [NLAYER, 128, 8])
    W1_h = din("W1", [NLAYER, D, 2 * DI])
    b1_h = din("b1", [NLAYER, 1, 2 * DI])
    W2_h = din("W2", [NLAYER, DI, D])
    b2_h = din("b2", [NLAYER, 1, D])

    out_t = nc.dram_tensor("out", [OWN, D], F32, kind="ExternalOutput").ap()
    s_scr = nc.dram_tensor("s_scratch", [D, OWN], F32, kind="Internal").ap()

    top = nc.sbuf_top
    base = ((top - nc.sbuf_bytes_remaining) + 63) // 64 * 64
    H = {}
    cur = [base]

    NAMEOF = {}

    def salloc(name, shape, dtype, off=None):
        nbytes = int(np.prod(shape[1:])) * mybir.dt.size(dtype)
        if off is None:
            o = cur[0]
            cur[0] = o + (nbytes + 63) // 64 * 64
        else:
            o = off
        assert o + nbytes <= top, f"SBUF overflow placing {name} at {o}+{nbytes} (top {top})"
        h = nc.alloc_sbuf_tensor_at(name, list(shape), dtype, offset=o)
        H[name] = h
        NAMEOF[id(h)] = name
        return h

    stream = salloc("stream", [128, NT * SW], F32)
    protb = salloc("protb", [128, 128], BF16)
    onesmatb = salloc("onesmatb", [128, 128], BF16)
    onesrowfb = salloc("onesrowfb", [1, 128], F32)
    onesrowb = salloc("onesrowb", [1, 512], BF16)
    negonesb = salloc("negonesb", [1, 128], BF16)
    identb = salloc("identb", [128, 128], F32)
    blkonesb = salloc("blkonesb", [128, 2], BF16)
    corrb = salloc("corrb", [1, 512], BF16)
    zflagb = salloc("zflagb", [128, 1], F32)
    ahb = salloc("ahb", [128, 8], F32)
    bcolb = salloc("bcolb", [128, 8], F32)
    carryb = salloc("carryb", [128, 8], F32)
    qspatb = salloc("qspatb", [2, 128], BF16)
    kspatb = salloc("kspatb", [2, 128], BF16)
    betcolb = salloc("betcolb", [128, 8], F32)
    gamrowb = salloc("gamrowb", [1, D], BF16)
    b2rowb = salloc("b2rowb", [1, 128], BF16)
    rowhi2 = salloc("rowhi2", [1, 512], BF16)
    shsnap = salloc("shsnap", [128, 8], F32)
    rowlo2 = salloc("rowlo2", [1, 512], BF16)
    b1rowb = salloc("b1rowb", [1, 256], BF16)
    rowa = salloc("rowa", [2, 512], F32)
    rowb_ = salloc("rowb", [2, 512], F32)
    rowhi = salloc("rowhi", [2, 512], BF16)
    rowlo = salloc("rowlo", [2, 512], BF16)

    OV = cur[0]

    def oalloc(name, shape, dtype, off):
        return salloc(name, shape, dtype, off=OV + off)

    xch = oalloc("xch", [128, NT * 512], F32, 0)
    ych = oalloc("ych", [128, NT * 512], F32, 16384)
    x2ch = oalloc("x2ch", [128, NT * 512], BF16, 32768)
    ybfb = oalloc("ybfb", [128, NT * 512], BF16, 40960)
    wout_sb = oalloc("wout_sb", [128, NT * 1024], BF16, 49152)
    sevacb = oalloc("sevacb", [128, 512], F32, 65536)
    borowb = oalloc("borowb", [1, D], BF16, 67584)
    sgf = oalloc("sgf", [128, NT * 512], F32, 0)
    sldf = oalloc("sldf", [128, NT * 512], F32, 16384)
    xotf = oalloc("xotf", [128, 1024], F32, 32768)
    xsb = oalloc("xsb", [128, NT * 768], BF16, 0)
    kbuf = oalloc("kbuf", [128, NT * 768], BF16, 12288)
    vbuf = oalloc("vbuf", [128, 6 * 1024], BF16, 24576)
    qbuf = oalloc("qbuf", [128, NT * 512], BF16, 36864)
    aob = oalloc("aob", [128, NT * 512], BF16, 45056)
    tQA = oalloc("tQA", [128, 512], F32, 53248)
    tQB = oalloc("tQB", [128, 512], F32, 55296)
    tKC = oalloc("tKC", [128, 768], F32, 57344)
    tKD = oalloc("tKD", [128, 768], F32, 60416)
    qrawb = oalloc("qrawb", [128, 512], BF16, 63488)
    q1b = oalloc("q1b", [128, 512], BF16, 64512)
    qsqb = oalloc("qsqb", [128, 512], BF16, 65536)
    uf32a = oalloc("uf32a", [128, 512], F32, 66560)
    uf32b = oalloc("uf32b", [128, 512], F32, 68608)
    ptb = oalloc("ptb", [128, 4 * 256], BF16, 70656)
    rsbuf = oalloc("rsbuf", [128, 256], F32, 72704)
    wqkb = oalloc("wqkb", [128, 2 * NT * 128], BF16, 73728)
    wvb = oalloc("wvb", [128, NT * 128], BF16, 77824)
    xnfb = oalloc("xnfb", [128, NT * FFN_TB], BF16, 0)
    gatedb = oalloc("gatedb", [128, NI_T * FFN_TB], BF16, 16384)
    gelbb = oalloc("gelbb", [128, 512], F32, 55808)
    fsqb = oalloc("fsqb", [128, 512], BF16, 57856)
    fxbb = oalloc("fxbb", [128, 512], BF16, 58880)
    ft32 = oalloc("ft32", [128, 512], F32, 59904)
    w1bb = oalloc("w1bb", [128, 2 * NT * 256], BF16, 61952)
    w2bb = oalloc("w2bb", [128, 2 * NI_T * 128], BF16, 70144)

    assert OV + 81920 <= top, f"overlay end over top ({OV + 81920} > {top})"
    phases = {
        "ph0": ["xch", "ych", "x2ch", "ybfb", "wout_sb", "sevacb", "borowb"],
        "fin": ["sgf", "sldf", "xotf"],
        "attn": ["xsb", "kbuf", "vbuf", "qbuf", "aob", "tQA", "tQB", "tKC",
                 "tKD", "qrawb", "q1b", "qsqb", "uf32a", "uf32b", "ptb",
                 "rsbuf", "wqkb", "wvb"],
        "ffn": ["xnfb", "gatedb", "gelbb", "fsqb", "fxbb", "ft32", "w1bb",
                "w2bb"],
    }
    for ph, names in phases.items():
        spans = sorted((H[n].manual_sbuf_range, n) for n in names)
        for i in range(len(spans) - 1):
            (a0, a1), n1 = spans[i]
            (b0, b1), n2 = spans[i + 1]
            assert a1 <= b0, f"{ph}: {n1} {a0}-{a1} overlaps {n2} {b0}-{b1}"

    ps = [nc.alloc_psum_tensor(f"bank{i}", [128, 512], F32) for i in range(8)]

    def K(name, lo=0, hi=1 << 30):
        return (name, lo, hi)

    def st_ap(dt, c0, c1):
        assert SOFF <= c0 < c1 <= S, (c0, c1)
        return stream.ap()[:, dt * SW + c0 - SOFF:dt * SW + c1 - SOFF]

    def st_key(dt, c0, c1):
        return ("stream", dt * SW + c0 - SOFF, dt * SW + c1 - SOFF)

    for dst, src in [(protb, c_prot), (onesmatb, c_onesmat),
                     (onesrowfb, c_onesrowf), (onesrowb, c_onesrow),
                     (negonesb, c_negones), (identb, c_ident),
                     (blkonesb, c_blkones), (corrb, corr), (zflagb, zflag),
                     (ahb, ssm_a), (bcolb, ssm_b)]:
        B.op("dmx", (lambda d, s: lambda e: e.dma_start(d.ap()[:, :], s[:, :]))(dst, src),
             writes=[K(NAMEOF[id(dst)])], tag="const")
    B.op("dmw", lambda e: e.dma_start(borowb.ap()[:, :], b_out[:, :]),
         writes=[K("borowb")], tag="b_out")
    B.op("dmw", lambda e: e.dma_start(
        wout_sb.ap()[:, :].rearrange("p (dt e) -> p dt e", dt=NT),
        w_outT[:, :].rearrange("(dt p) e -> p dt e", p=128)),
        writes=[K("wout_sb")], tag="woutT")
    B.op("act", lambda e: e.activation(ahb.ap()[:, :], ahb.ap()[:, :], AF.Tanh),
         reads=[K("ahb")], writes=[K("ahb")], tag="tanh")
    B.op("dve", lambda e: e.memset(carryb.ap()[:, :], 0.0),
         writes=[K("carryb")], tag="carry0")

    for kc in range(SSM // 512):
        c0 = kc * 512
        sc0 = c0 - 256
        xk = K("xch")
        B.op("dmx", (lambda c0: lambda e: e.dma_start(
            xch.ap()[:, :].rearrange("p (dt c) -> p dt c", dt=NT),
            x_ssm[:, c0:c0 + 512].rearrange("(dt p) c -> p dt c", p=128)))(c0),
            writes=[xk], tag=f"xld{kc}")
        B.op("dve", lambda e: e.tensor_copy(x2ch.ap()[:, :], xch.ap()[:, :]),
             reads=[xk], writes=[K("x2ch")], tag="xbf")
        B.group("pe", [(lambda dt: lambda e: e.matmul(
            ps[6].ap()[0:1, :], onesmatb.ap()[:, 0:1],
            x2ch.ap()[:, dt * 512:(dt + 1) * 512],
            start=(dt == 0), stop=(dt == NT - 1)))(dt) for dt in range(NT)],
            reads=[K("x2ch"), K("onesmatb")], writes=[K("ps6")], tag="sumx")
        B.op("act", lambda e: e.activation(x2ch.ap()[:, :], xch.ap()[:, :], AF.Square),
             reads=[xk], writes=[K("x2ch")], tag="sq")
        B.group("pe", [(lambda dt: lambda e: e.matmul(
            ps[7].ap()[0:1, :], onesmatb.ap()[:, 0:1],
            x2ch.ap()[:, dt * 512:(dt + 1) * 512],
            start=(dt == 0), stop=(dt == NT - 1)))(dt) for dt in range(NT)],
            reads=[K("x2ch"), K("onesmatb")], writes=[K("ps7")], tag="sumx2")
        B.op("act", lambda e: e.activation(rowa.ap()[0:1, :], ps[6].ap()[0:1, :],
                                           AF.Identity, bias=0.0, scale=1.0 / D),
             reads=[K("ps6")], writes=[K("rowa", 0, 2048)], tag="mu")
        B.op("act", lambda e: e.activation(rowb_.ap()[0:1, :], rowa.ap()[0:1, :],
                                           AF.Square),
             reads=[K("rowa", 0, 2048)], writes=[K("rowb", 0, 2048)], tag="mu2")
        B.op("dve", lambda e: e.scalar_tensor_tensor(
            rowb_.ap()[0:1, :], ps[7].ap()[0:1, :], 1.0 / D, rowb_.ap()[0:1, :],
            ALU.mult, ALU.subtract),
            reads=[K("ps7"), K("rowb", 0, 2048)], writes=[K("rowb", 0, 2048)], tag="var")
        B.op("dve", lambda e: e.tensor_scalar_add(rowb_.ap()[0:1, :],
                                                  rowb_.ap()[0:1, :], 1e-5),
             reads=[K("rowb", 0, 2048)], writes=[K("rowb", 0, 2048)], tag="eps")
        B.op("act", lambda e: e.activation(rowb_.ap()[0:1, :], rowb_.ap()[0:1, :],
                                           AF.Sqrt),
             reads=[K("rowb", 0, 2048)], writes=[K("rowb", 0, 2048)], tag="std")
        B.op("act", lambda e: e.activation(rowb_.ap()[0:1, :], rowb_.ap()[0:1, :],
                                           AF.Ln),
             reads=[K("rowb", 0, 2048)], writes=[K("rowb", 0, 2048)], tag="rln")
        B.op("act", lambda e: e.activation(rowb_.ap()[0:1, :], rowb_.ap()[0:1, :],
                                           AF.Exp, bias=0.0, scale=-1.0),
             reads=[K("rowb", 0, 2048)], writes=[K("rowb", 0, 2048)], tag="rstd")
        B.op("dve", lambda e: e.scalar_tensor_tensor(
            rowa.ap()[0:1, :], rowa.ap()[0:1, :], -1.0, rowb_.ap()[0:1, :],
            ALU.mult, ALU.mult),
            reads=[K("rowa", 0, 2048), K("rowb", 0, 2048)],
            writes=[K("rowa", 0, 2048)], tag="negmurstd")
        B.group("pe", [lambda e: e.matmul(ps[4].ap()[:, :], onesrowfb.ap()[:, :],
                                          rowb_.ap()[0:1, :], start=True, stop=True)],
                reads=[K("rowb", 0, 2048), K("onesrowfb")], writes=[K("ps4")],
                tag="bc_r1")
        B.group("pe", [lambda e: e.matmul(ps[5].ap()[:, :], onesrowfb.ap()[:, :],
                                          rowa.ap()[0:1, :], start=True, stop=True)],
                reads=[K("rowa", 0, 2048), K("onesrowfb")], writes=[K("ps5")],
                tag="bc_r2")
        for dt in range(NT):
            dk = K("xch", dt * 2048, (dt + 1) * 2048)
            B.op("dve", (lambda dt: lambda e: e.tensor_tensor(
                xch.ap()[:, dt * 512:(dt + 1) * 512],
                xch.ap()[:, dt * 512:(dt + 1) * 512],
                ps[4].ap()[:, :], ALU.mult))(dt),
                reads=[dk, K("ps4")], writes=[dk], tag="xn_mul")
            B.op("dve", (lambda dt: lambda e: e.tensor_tensor(
                xch.ap()[:, dt * 512:(dt + 1) * 512],
                xch.ap()[:, dt * 512:(dt + 1) * 512],
                ps[5].ap()[:, :], ALU.add))(dt),
                reads=[dk, K("ps5")], writes=[dk], tag="xn_add")
            lo = max(SOFF, sc0)
            if lo < sc0 + 512:
                off = lo - sc0
                B.op("act", (lambda dt, off, lo, sc0: lambda e: e.activation(
                    st_ap(dt, lo, sc0 + 512),
                    xch.ap()[:, dt * 512 + off:(dt + 1) * 512], AF.Copy))(dt, off, lo, sc0),
                    reads=[dk], writes=[st_key(dt, lo, sc0 + 512)], tag="xn_st")
            B.op("act", (lambda dt: lambda e: e.activation(
                xch.ap()[:, dt * 512:(dt + 1) * 512],
                xch.ap()[:, dt * 512:(dt + 1) * 512],
                AF.Identity, bias=0.0, scale=bcolb.ap()[:, dt:dt + 1]))(dt),
                reads=[dk, K("bcolb")], writes=[dk], tag="bx")
            yk = K("ych", dt * 2048, (dt + 1) * 2048)
            B.op("dve", (lambda dt: lambda e: e.tensor_tensor_scan(
                ych.ap()[:, dt * 512:(dt + 1) * 512],
                ahb.ap()[:, dt:dt + 1].broadcast_to([128, 512]),
                xch.ap()[:, dt * 512:(dt + 1) * 512],
                carryb.ap()[:, dt:dt + 1], ALU.mult, ALU.add))(dt),
                reads=[dk, K("ahb"), K("carryb", dt * 4, dt * 4 + 4)], writes=[yk],
                tag=f"scan{kc}_{dt}")
            B.op("dve", (lambda dt: lambda e: e.tensor_copy(
                carryb.ap()[:, dt:dt + 1],
                ych.ap()[:, dt * 512 + 511:dt * 512 + 512]))(dt),
                reads=[yk], writes=[K("carryb", dt * 4, dt * 4 + 4)], tag="carry")
        if kc >= 4:
            oc0 = (kc - 4) * 512
            B.op("dve", lambda e: e.tensor_copy(ybfb.ap()[:, :], ych.ap()[:, :]),
                 reads=[K("ych")], writes=[K("ybfb")], tag="ybf")
            for et in range(NT):
                bank = et % 2
                fns = []
                for dt in range(NT):
                    fns.append((lambda dt, et, bank: lambda e: e.matmul(
                        ps[bank].ap()[:, :],
                        wout_sb.ap()[:, dt * 1024 + et * 128:dt * 1024 + et * 128 + 128],
                        ybfb.ap()[:, dt * 512:(dt + 1) * 512],
                        start=(dt == 0), stop=False))(dt, et, bank))
                fns.append((lambda et, bank: lambda e: e.matmul(
                    ps[bank].ap()[:, :],
                    borowb.ap()[:, et * 128:et * 128 + 128],
                    onesrowb.ap()[:, :], start=False, stop=True))(et, bank))
                B.group("pe", fns,
                        reads=[K("ybfb"), K("wout_sb"), K("borowb"), K("onesrowb")],
                        writes=[K(f"ps{bank}")], tag=f"wout{et}")
                B.op("act", (lambda bank: lambda e: e.activation(
                    sevacb.ap()[:, :], ps[bank].ap()[:, :], AF.Copy))(bank),
                    reads=[K(f"ps{bank}")], writes=[K("sevacb")], tag="sevac")
                B.op("dms", (lambda et, oc0: lambda e: e.dma_start(
                    s_scr[et * 128:(et + 1) * 128, oc0:oc0 + 512],
                    sevacb.ap()[:, :]))(et, oc0),
                    reads=[K("sevacb")], tag="s_st")
    B.barrier()

    def emit_attn(l):
        Aq = ATT_OUT[l] - 1
        WQ0 = Aq // WIN
        for pat, scl in [(qspatb, q_scale), (kspatb, k_scale)]:
            B.op("dve", (lambda pat: lambda e: e.memset(pat.ap()[:, :], 0.0))(pat),
                 writes=[K(NAMEOF[id(pat)])], tag="patz")
            B.op("dmw", (lambda pat, scl, l=l: lambda e: e.dma_start(
                pat.ap()[0:1, 0:64], scl[l:l + 1, :]))(pat, scl),
                reads=[K(NAMEOF[id(pat)])], writes=[K(NAMEOF[id(pat)])], tag="pat0")
            B.op("dmw", (lambda pat, scl, l=l: lambda e: e.dma_start(
                pat.ap()[1:2, 64:128], scl[l:l + 1, :]))(pat, scl),
                reads=[K(NAMEOF[id(pat)])], writes=[K(NAMEOF[id(pat)])], tag="pat1")

        groups = []
        w = WQ0
        nwin_total = S // WIN
        while w < nwin_total:
            g1 = min(w + 1, nwin_total - 1)
            groups.append((w, g1))
            w = g1 + 1

        prev_kspan = None
        for (g0, g1) in groups:
            q0 = max(Aq, g0 * WIN)
            q1 = (g1 + 1) * WIN
            qspan = q1 - q0
            assert qspan <= 512
            k0 = (g0 - 1) * WIN
            k1 = q1
            kspan = k1 - k0
            ntt = kspan // 128
            assert kspan <= 768
            for dt in range(NT):
                if prev_kspan is None:
                    cast0 = k0
                    zfill = max(k0, SOFF + 1)
                    if zfill > k0:
                        B.op("pool", (lambda dt, n: lambda e: e.memset(
                            xsb.ap()[:, dt * 768:dt * 768 + n], 0.0))(dt, zfill - k0),
                            writes=[K("xsb", dt * 1536, dt * 1536 + 2 * (zfill - k0))],
                            tag="xs_z")
                else:
                    cast0 = k0 + WIN
                    zfill = cast0
                    B.op("act", (lambda dt, pk: lambda e: e.activation(
                        xsb.ap()[:, dt * 768:dt * 768 + WIN],
                        xsb.ap()[:, dt * 768 + pk - WIN:dt * 768 + pk],
                        AF.Copy))(dt, prev_kspan),
                        reads=[K("xsb", dt * 1536 + 2 * (prev_kspan - WIN),
                                 dt * 1536 + 2 * prev_kspan)],
                        writes=[K("xsb", dt * 1536, dt * 1536 + 2 * WIN)],
                        tag="xs_cp")
                src0 = zfill if dt < 4 else zfill - 1
                B.op("act", (lambda dt, src0, zfill, k0, k1: lambda e: e.activation(
                    xsb.ap()[:, dt * 768 + (zfill - k0):dt * 768 + (k1 - k0)],
                    st_ap(dt, src0, src0 + (k1 - zfill)), AF.Copy))(dt, src0, zfill, k0, k1),
                    reads=[st_key(dt, src0, src0 + (k1 - zfill))],
                    writes=[K("xsb", dt * 1536 + 2 * (zfill - k0),
                              dt * 1536 + 2 * (k1 - k0))],
                    tag="xs")
            prev_kspan = kspan
            for dst, tab, a0, a1 in [(tQA, "A", q0, q1), (tQB, "B", q0, q1),
                                     (tKC, "C", k0, k1), (tKD, "D", k0, k1)]:
                B.op("dmx", (lambda dst, tab, a0, a1: lambda e: e.dma_start(
                    dst.ap()[:, 0:a1 - a0], tabs_hbm[tab][:, a0:a1]))(dst, tab, a0, a1),
                    writes=[K(NAMEOF[id(dst)])], tag="tab")
            for side, wcol0, patb, tA_, tB_, obuf, ospan, s0, s1 in [
                    ("q", 0, qspatb, tQA, tQB, qbuf, 512, q0, q1),
                    ("k", D, kspatb, tKC, tKD, kbuf, 768, k0, k1)]:
                for hp in range(8):
                    wsel = hp % 2
                    B.op("dmw" if wsel == 0 else "dmw1",
                         (lambda hp, wcol0, wsel, l=l: lambda e: e.dma_start(
                        wqkb.ap()[:, wsel * 1024:(wsel + 1) * 1024].rearrange(
                            "p (dt h) -> p dt h", dt=NT),
                        Wqkv[l, :, wcol0 + hp * 128:wcol0 + (hp + 1) * 128].rearrange(
                            "(dt p) h -> p dt h", p=128)))(hp, wcol0, wsel),
                        writes=[K("wqkb", wsel * 2048, (wsel + 1) * 2048)], tag="wqk")
                    span = s1 - s0
                    nch = (span + 511) // 512
                    for cc in range(nch):
                        cw = min(512, span - cc * 512)
                        xoff = (s0 - k0) + cc * 512
                        pit = (0 if side == "q" else 8) + hp * nch + cc
                        pp = pit % 2
                        bank = pp
                        fns = [(lambda dt, cw, xoff, bank, wsel: lambda e: e.matmul(
                            ps[bank].ap()[:, 0:cw],
                            wqkb.ap()[:, wsel * 1024 + dt * 128:wsel * 1024 + (dt + 1) * 128],
                            xsb.ap()[:, dt * 768 + xoff:dt * 768 + xoff + cw],
                            start=(dt == 0), stop=(dt == NT - 1)))(dt, cw, xoff, bank, wsel)
                            for dt in range(NT)]
                        B.group("pe", fns,
                                reads=[K("wqkb", wsel * 2048, (wsel + 1) * 2048), K("xsb")],
                                writes=[K(f"ps{bank}")], tag=f"{side}mm")
                        B.op("act", (lambda bank, cw: lambda e: e.activation(
                            qsqb.ap()[:, 0:cw], ps[bank].ap()[:, 0:cw], AF.Square))(bank, cw),
                            reads=[K(f"ps{bank}")], writes=[K("qsqb")], tag="sqq")
                        B.group("pe", [(lambda cw, pp: lambda e: e.matmul(
                            ps[2 + pp].ap()[0:2, 0:cw], blkonesb.ap()[:, :],
                            qsqb.ap()[:, 0:cw], start=True, stop=True))(cw, pp)],
                            reads=[K("qsqb"), K("blkonesb")], writes=[K(f"ps{2+pp}")], tag="ssq")
                        B.op("act", (lambda cw, pp: lambda e: e.activation(
                            rowa.ap()[:, 0:cw], ps[2 + pp].ap()[0:2, 0:cw], AF.Sqrt))(cw, pp),
                            reads=[K(f"ps{2+pp}")], writes=[K("rowa")], tag="nrm")
                        B.op("dve", (lambda cw: lambda e: e.tensor_scalar_max(
                            rowa.ap()[:, 0:cw], rowa.ap()[:, 0:cw], 1e-12))(cw),
                            reads=[K("rowa")], writes=[K("rowa")], tag="nmx")
                        B.op("act", (lambda cw: lambda e: e.activation(
                            rowa.ap()[:, 0:cw], rowa.ap()[:, 0:cw], AF.Ln))(cw),
                            reads=[K("rowa")], writes=[K("rowa")], tag="nln")
                        B.op("act", (lambda cw: lambda e: e.activation(
                            rowa.ap()[:, 0:cw], rowa.ap()[:, 0:cw], AF.Exp,
                            bias=0.0, scale=-1.0))(cw),
                            reads=[K("rowa")], writes=[K("rowa")], tag="nrc")
                        B.op("dve", (lambda cw: lambda e: e.tensor_copy(
                            rowhi.ap()[:, 0:cw], rowa.ap()[:, 0:cw]))(cw),
                            reads=[K("rowa")], writes=[K("rowhi")], tag="nhi")
                        B.op("dve", (lambda cw: lambda e: e.tensor_tensor(
                            rowlo.ap()[:, 0:cw], rowa.ap()[:, 0:cw],
                            rowhi.ap()[:, 0:cw], ALU.subtract))(cw),
                            reads=[K("rowa"), K("rowhi")], writes=[K("rowlo")], tag="nlo")
                        B.group("pe", [
                            (lambda patb, cw, pp: lambda e: e.matmul(
                                ps[4 + pp].ap()[:, 0:cw], patb.ap()[:, :],
                                rowhi.ap()[:, 0:cw], start=True, stop=False))(patb, cw, pp),
                            (lambda patb, cw, pp: lambda e: e.matmul(
                                ps[4 + pp].ap()[:, 0:cw], patb.ap()[:, :],
                                rowlo.ap()[:, 0:cw], start=False, stop=True))(patb, cw, pp)],
                            reads=[K("rowhi"), K("rowlo"), K(NAMEOF[id(patb)])],
                            writes=[K(f"ps{4+pp}")], tag="rnbc")
                        B.op("act", (lambda cw, pp: lambda e: e.activation(
                            uf32b.ap()[:, 0:cw], ps[4 + pp].ap()[:, 0:cw], AF.Copy))(cw, pp),
                            reads=[K(f"ps{4+pp}")], writes=[K("uf32b")], tag="rnev")
                        B.op("dve", (lambda bank, cw: lambda e: e.tensor_tensor(
                            uf32a.ap()[:, 0:cw], ps[bank].ap()[:, 0:cw],
                            uf32b.ap()[:, 0:cw], ALU.mult))(bank, cw),
                            reads=[K(f"ps{bank}"), K("uf32b")], writes=[K("uf32a")], tag="q1f")
                        B.op("dve", (lambda cw: lambda e: e.tensor_copy(
                            q1b.ap()[:, 0:cw], uf32a.ap()[:, 0:cw]))(cw),
                            reads=[K("uf32a")], writes=[K("q1b")], tag="q1b")
                        B.group("pe", [(lambda cw, pp: lambda e: e.matmul(
                            ps[6 + pp].ap()[:, 0:cw], protb.ap()[:, :],
                            q1b.ap()[:, 0:cw], start=True, stop=True))(cw, pp)],
                            reads=[K("q1b"), K("protb")], writes=[K(f"ps{6+pp}")], tag="rot")
                        toff = cc * 512
                        B.op("dve", (lambda tA_, cw, toff: lambda e: e.tensor_tensor(
                            uf32a.ap()[:, 0:cw], tA_.ap()[:, toff:toff + cw],
                            uf32a.ap()[:, 0:cw], ALU.mult))(tA_, cw, toff),
                            reads=[K(NAMEOF[id(tA_)]), K("uf32a")], writes=[K("uf32a")], tag="rotA")
                        B.op("dve", (lambda tB_, cw, toff, pp: lambda e: e.tensor_tensor(
                            uf32b.ap()[:, 0:cw], tB_.ap()[:, toff:toff + cw],
                            ps[6 + pp].ap()[:, 0:cw], ALU.mult))(tB_, cw, toff, pp),
                            reads=[K(NAMEOF[id(tB_)]), K(f"ps{6+pp}")], writes=[K("uf32b")], tag="rotB")
                        B.op("dve", (lambda obuf, hp, ospan, cw, cc: lambda e: e.tensor_tensor(
                            obuf.ap()[:, hp * ospan + cc * 512:hp * ospan + cc * 512 + cw],
                            uf32a.ap()[:, 0:cw], uf32b.ap()[:, 0:cw],
                            ALU.add))(obuf, hp, ospan, cw, cc),
                            reads=[K("uf32a"), K("uf32b")],
                            writes=[K(NAMEOF[id(obuf)], (hp * ospan + cc * 512) * 2,
                                      (hp * ospan + cc * 512 + cw) * 2)], tag="qkrot")
            for vh in range(8):
                B.op("dmw", (lambda vh, l=l: lambda e: e.dma_start(
                    wvb.ap()[:, :].rearrange("p (dt h) -> p dt h", dt=NT),
                    Wqkv[l, :, 2 * D + vh * 128:2 * D + (vh + 1) * 128].rearrange(
                        "(dt p) h -> p dt h", p=128)))(vh),
                    writes=[K("wvb")], tag="wv")
                for tt in range(ntt):
                    bank = 4 + tt % 2
                    fns = [(lambda dt, tt, bank: lambda e: e.matmul(
                        ps[bank].ap()[:, 0:128],
                        xsb.ap()[:, dt * 768 + tt * 128:dt * 768 + (tt + 1) * 128],
                        wvb.ap()[:, dt * 128:(dt + 1) * 128],
                        start=(dt == 0), stop=(dt == NT - 1)))(dt, tt, bank)
                        for dt in range(NT)]
                    B.group("pe", fns, reads=[K("xsb"), K("wvb")],
                            writes=[K(f"ps{bank}")], tag="vmm")
                    B.op("act", (lambda tt, vh, bank: lambda e: e.activation(
                        vbuf.ap()[:, tt * 1024 + vh * 128:tt * 1024 + (vh + 1) * 128],
                        ps[bank].ap()[:, 0:128], AF.Copy))(tt, vh, bank),
                        reads=[K(f"ps{bank}")],
                        writes=[K("vbuf", (tt * 1024 + vh * 128) * 2,
                                  (tt * 1024 + (vh + 1) * 128) * 2)],
                        tag="vev")
            iters = []
            for w in range(g0, g1 + 1):
                qw0 = max(Aq, w * WIN)
                qw1 = (w + 1) * WIN
                qn = qw1 - qw0
                jb0 = (w - 1) * WIN - k0
                for h in range(HEADS):
                    iters.append((w, qw0, qn, jb0, h))

            def em_scores(it, i):
                w, qw0, qn, jb0, h = it
                p = i % 2
                sb = (2, 3) if p == 0 else (0, 1)
                hp, hs = h // 2, (h % 2) * 64
                live = []
                for jc in range(4):
                    base1 = qw0 - (w - 1) * WIN - 128 * jc
                    base2 = (w - 1) * WIN + 128 * jc + WIN - qw0
                    if base1 + qn - 1 < 0 or base2 + 127 < 0:
                        continue
                    live.append(jc)
                fns = []
                for jc in live:
                    bank, colo = sb[jc // 2], (jc % 2) * 256
                    fns.append((lambda jc, bank, colo, hp, hs, qn, qw0, jb0, q0:
                                lambda e: e.matmul(
                        ps[bank].ap()[:, colo:colo + qn],
                        kbuf.ap()[hs:hs + 64,
                                  hp * 768 + jb0 + jc * 128:hp * 768 + jb0 + (jc + 1) * 128],
                        qbuf.ap()[hs:hs + 64,
                                  hp * 512 + (qw0 - q0):hp * 512 + (qw0 - q0) + qn],
                        start=True, stop=True))(jc, bank, colo, hp, hs, qn, qw0, jb0, q0))
                B.group("pe", fns, reads=[K("kbuf"), K("qbuf")],
                        writes=[K(f"ps{sb[0]}"), K(f"ps{sb[1]}")], tag="sc")
                return live

            def em_mask(it, i, live):
                w, qw0, qn, jb0, h = it
                p = i % 2
                sb = (2, 3) if p == 0 else (0, 1)
                for jc in range(4):
                    pk = K("ptb", jc * 512, jc * 512 + qn * 2)
                    if jc not in live:
                        B.op("pool", (lambda jc, qn: lambda e: e.memset(
                            ptb.ap()[:, jc * 256:jc * 256 + qn], 0.0))(jc, qn),
                            writes=[pk], tag="pz")
                        continue
                    bank, colo = sb[jc // 2], (jc % 2) * 256
                    B.op("act", (lambda jc, bank, colo, qn: lambda e: e.activation(
                        ptb.ap()[:, jc * 256:jc * 256 + qn],
                        ps[bank].ap()[:, colo:colo + qn],
                        AF.Exp, bias=0.0, scale=8.0))(jc, bank, colo, qn),
                        reads=[K(f"ps{bank}")], writes=[pk], tag="exp")
                    base1 = qw0 - (w - 1) * WIN - 128 * jc
                    if base1 - 127 < 0:
                        B.op("pool", (lambda jc, qn, base1: lambda e: e.affine_select(
                            out=ptb.ap()[:, jc * 256:jc * 256 + qn],
                            in_=ptb.ap()[:, jc * 256:jc * 256 + qn],
                            compare_op=ALU.is_ge, fill=0.0, base=base1,
                            pattern=[[1, qn]], channel_multiplier=-1))(jc, qn, base1),
                            reads=[pk], writes=[pk], tag="m1")
                    base2 = (w - 1) * WIN + 128 * jc + WIN - qw0
                    if base2 - (qn - 1) < 0:
                        B.op("pool", (lambda jc, qn, base2: lambda e: e.affine_select(
                            out=ptb.ap()[:, jc * 256:jc * 256 + qn],
                            in_=ptb.ap()[:, jc * 256:jc * 256 + qn],
                            compare_op=ALU.is_ge, fill=0.0, base=base2,
                            pattern=[[-1, qn]], channel_multiplier=1))(jc, qn, base2),
                            reads=[pk], writes=[pk], tag="m2")

            def em_sumav(it, i):
                w, qw0, qn, jb0, h = it
                p = i % 2
                hp, hs = h // 2, (h % 2) * 64
                sumb, avb = 4 + p, 6 + p
                fns = [(lambda jc, qn, sumb: lambda e: e.matmul(
                    ps[sumb].ap()[:, 0:qn], onesmatb.ap()[:, :],
                    ptb.ap()[:, jc * 256:jc * 256 + qn],
                    start=(jc == 0), stop=False))(jc, qn, sumb) for jc in range(4)]
                if 1536 <= qw0 < 2048:
                    fns.append((lambda qn, qw0, sumb: lambda e: e.matmul(
                        ps[sumb].ap()[:, 0:qn], negonesb.ap()[:, :],
                        corrb.ap()[:, qw0 - 1536:qw0 - 1536 + qn],
                        start=False, stop=True))(qn, qw0, sumb))
                else:
                    fns[-1] = (lambda qn, sumb: lambda e: e.matmul(
                        ps[sumb].ap()[:, 0:qn], onesmatb.ap()[:, :],
                        ptb.ap()[:, 3 * 256:3 * 256 + qn],
                        start=False, stop=True))(qn, sumb)
                B.group("pe", fns,
                        reads=[K("ptb"), K("onesmatb"), K("negonesb"), K("corrb")],
                        writes=[K(f"ps{sumb}")], tag="sum")
                fns = [(lambda jc, qn, h, jb0, avb: lambda e: e.matmul(
                    ps[avb].ap()[0:64, 0:qn],
                    vbuf.ap()[:, (jb0 // 128 + jc) * 1024 + h * 64:
                              (jb0 // 128 + jc) * 1024 + (h + 1) * 64],
                    ptb.ap()[:, jc * 256:jc * 256 + qn],
                    start=(jc == 0), stop=(jc == 3)))(jc, qn, h, jb0, avb)
                    for jc in range(4)]
                B.group("pe", fns, reads=[K("ptb"), K("vbuf")],
                        writes=[K(f"ps{avb}")], tag="av")
                B.op("act", (lambda qn, sumb: lambda e: e.activation(
                    rsbuf.ap()[:, 0:qn], ps[sumb].ap()[:, 0:qn], AF.Ln))(qn, sumb),
                    reads=[K(f"ps{sumb}")], writes=[K("rsbuf")], tag="rln")
                B.op("act", (lambda qn: lambda e: e.activation(
                    rsbuf.ap()[:, 0:qn], rsbuf.ap()[:, 0:qn], AF.Exp,
                    bias=0.0, scale=-1.0))(qn),
                    reads=[K("rsbuf")], writes=[K("rsbuf")], tag="rs")
                B.op("dve", (lambda qn, hp, hs, qw0, q0, avb: lambda e: e.tensor_tensor(
                    aob.ap()[hs:hs + 64,
                             hp * 512 + (qw0 - q0):hp * 512 + (qw0 - q0) + qn],
                    ps[avb].ap()[0:64, 0:qn], rsbuf.ap()[0:64, 0:qn],
                    ALU.mult))(qn, hp, hs, qw0, q0, avb),
                    reads=[K(f"ps{avb}"), K("rsbuf")],
                    writes=[K("aob", (hp * 512 + (qw0 - q0)) * 2,
                              (hp * 512 + (qw0 - q0) + qn) * 2)], tag="aon")

            live_i = em_scores(iters[0], 0)
            for i, it in enumerate(iters):
                em_mask(it, i, live_i)
                if i + 1 < len(iters):
                    live_i = em_scores(iters[i + 1], i + 1)
                em_sumav(it, i)
            for et in range(NT):
                wsel = et % 2
                B.op("dmw" if wsel == 0 else "dmw1",
                     (lambda et, wsel, l=l: lambda e: e.dma_start(
                    wqkb.ap()[:, wsel * 1024:(wsel + 1) * 1024].rearrange(
                        "p (it ec) -> p it ec", it=NT),
                    Wo_h[l, :, et * 128:(et + 1) * 128].rearrange(
                        "(it p) ec -> p it ec", p=128)))(et, wsel),
                    writes=[K("wqkb", wsel * 2048, (wsel + 1) * 2048)], tag="wo")
                bank = 6 + et % 2
                fns = [(lambda it, qspan, bank, wsel: lambda e: e.matmul(
                    ps[bank].ap()[:, 0:qspan],
                    wqkb.ap()[:, wsel * 1024 + it * 128:wsel * 1024 + (it + 1) * 128],
                    aob.ap()[:, it * 512:it * 512 + qspan],
                    start=(it == 0), stop=(it == NT - 1)))(it, qspan, bank, wsel)
                    for it in range(NT)]
                B.group("pe", fns,
                        reads=[K("wqkb", wsel * 2048, (wsel + 1) * 2048), K("aob")],
                        writes=[K(f"ps{bank}")], tag="womm")
                B.op("dve", (lambda et, qspan, bank, q0: lambda e: e.tensor_tensor(
                    st_ap(et, q0, q0 + qspan), st_ap(et, q0, q0 + qspan),
                    ps[bank].ap()[:, 0:qspan], ALU.add))(et, qspan, bank, q0),
                    reads=[st_key(et, q0, q0 + qspan), K(f"ps{bank}")],
                    writes=[st_key(et, q0, q0 + qspan)], tag="ares")
        B.barrier()

    def emit_ffn(l):
        Fq = FFN_OUT[l]
        B.op("dmw", (lambda l=l: lambda e: e.dma_start(gamrowb.ap()[:, :], ln_g[l, :, :]))(),
             writes=[K("gamrowb")], tag="gam")
        B.op("dmx", (lambda l=l: lambda e: e.dma_start(betcolb.ap()[:, :], ln_b[l, :, :]))(),
             writes=[K("betcolb")], tag="bet")
        span_all = S - Fq
        nblk = (span_all + FFN_TB - 1) // FFN_TB
        for blk in range(nblk):
            t0 = Fq + blk * FFN_TB
            t1 = min(S, t0 + FFN_TB)
            tb = t1 - t0
            nch = (tb + 511) // 512
            if t1 < S:
                sb0 = (blk % 2) * 4
                for dt in range(4, NT):
                    B.op("dve", (lambda dt, t1, sb0: lambda e: e.tensor_copy(
                        shsnap.ap()[:, sb0 + dt - 4:sb0 + dt - 3],
                        st_ap(dt, t1 - 1, t1)))(dt, t1, sb0),
                        reads=[st_key(dt, t1 - 1, t1)],
                        writes=[K("shsnap", (sb0 + dt - 4) * 4, (sb0 + dt - 3) * 4)],
                        tag="shsn")
            for cc in range(nch):
                cw = min(512, tb - cc * 512)
                cs0 = t0 + cc * 512
                for dt in range(NT):
                    src0 = cs0 if dt < 4 else cs0 - 1
                    patch = (blk > 0 and cc == 0 and dt >= 4)
                    if patch:
                        pb0 = ((blk - 1) % 2) * 4
                        B.op("dve", (lambda dt, pb0: lambda e: e.tensor_copy(
                            fxbb.ap()[:, 0:1], shsnap.ap()[:, pb0 + dt - 4:pb0 + dt - 3]))(dt, pb0),
                            reads=[K("shsnap", (pb0 + dt - 4) * 4, (pb0 + dt - 3) * 4)],
                            writes=[K("fxbb", 0, 2)], tag="fxb0")
                        B.op("act", (lambda dt, cs0, cw: lambda e: e.activation(
                            fxbb.ap()[:, 1:cw], st_ap(dt, cs0, cs0 + cw - 1),
                            AF.Copy))(dt, cs0, cw),
                            reads=[st_key(dt, cs0, cs0 + cw - 1)],
                            writes=[K("fxbb", 2, 2 * cw)], tag="fxb")
                    else:
                        B.op("act", (lambda dt, src0, cw: lambda e: e.activation(
                            fxbb.ap()[:, 0:cw], st_ap(dt, src0, src0 + cw), AF.Copy))(dt, src0, cw),
                            reads=[st_key(dt, src0, src0 + cw)], writes=[K("fxbb")], tag="fxb")
                    B.group("pe", [(lambda dt, cw: lambda e: e.matmul(
                        ps[6].ap()[0:1, 0:cw], onesmatb.ap()[:, 0:1],
                        fxbb.ap()[:, 0:cw], start=(dt == 0), stop=(dt == NT - 1)))(dt, cw)],
                        reads=[K("fxbb"), K("onesmatb")], writes=[K("ps6")], tag="fsx")
                    B.op("act", (lambda cw: lambda e: e.activation(
                        fsqb.ap()[:, 0:cw], fxbb.ap()[:, 0:cw], AF.Square))(cw),
                        reads=[K("fxbb")], writes=[K("fsqb")], tag="fsq")
                    B.group("pe", [(lambda dt, cw: lambda e: e.matmul(
                        ps[7].ap()[0:1, 0:cw], onesmatb.ap()[:, 0:1],
                        fsqb.ap()[:, 0:cw], start=(dt == 0), stop=(dt == NT - 1)))(dt, cw)],
                        reads=[K("fsqb"), K("onesmatb")], writes=[K("ps7")], tag="fsx2")
                B.op("act", (lambda cw: lambda e: e.activation(
                    rowa.ap()[0:1, 0:cw], ps[6].ap()[0:1, 0:cw],
                    AF.Identity, bias=0.0, scale=1.0 / D))(cw),
                    reads=[K("ps6")], writes=[K("rowa", 0, 2048)], tag="fmu")
                B.op("act", (lambda cw: lambda e: e.activation(
                    rowb_.ap()[0:1, 0:cw], rowa.ap()[0:1, 0:cw], AF.Square))(cw),
                    reads=[K("rowa", 0, 2048)], writes=[K("rowb", 0, 2048)], tag="fmu2")
                B.op("dve", (lambda cw: lambda e: e.scalar_tensor_tensor(
                    rowb_.ap()[0:1, 0:cw], ps[7].ap()[0:1, 0:cw], 1.0 / D,
                    rowb_.ap()[0:1, 0:cw], ALU.mult, ALU.subtract))(cw),
                    reads=[K("ps7"), K("rowb", 0, 2048)], writes=[K("rowb", 0, 2048)],
                    tag="fvar")
                B.op("dve", (lambda cw: lambda e: e.tensor_scalar_add(
                    rowb_.ap()[0:1, 0:cw], rowb_.ap()[0:1, 0:cw], 1e-5))(cw),
                    reads=[K("rowb", 0, 2048)], writes=[K("rowb", 0, 2048)], tag="feps")
                B.op("act", (lambda cw: lambda e: e.activation(
                    rowb_.ap()[0:1, 0:cw], rowb_.ap()[0:1, 0:cw],
                    AF.Sqrt))(cw),
                    reads=[K("rowb", 0, 2048)], writes=[K("rowb", 0, 2048)], tag="fstd")
                B.op("act", (lambda cw: lambda e: e.activation(
                    rowb_.ap()[0:1, 0:cw], rowb_.ap()[0:1, 0:cw], AF.Ln))(cw),
                    reads=[K("rowb", 0, 2048)], writes=[K("rowb", 0, 2048)], tag="fln")
                B.op("act", (lambda cw: lambda e: e.activation(
                    rowb_.ap()[0:1, 0:cw], rowb_.ap()[0:1, 0:cw], AF.Exp,
                    bias=0.0, scale=-1.0))(cw),
                    reads=[K("rowb", 0, 2048)], writes=[K("rowb", 0, 2048)], tag="frstd")
                B.op("dve", (lambda cw: lambda e: e.scalar_tensor_tensor(
                    rowa.ap()[0:1, 0:cw], rowa.ap()[0:1, 0:cw], -1.0,
                    rowb_.ap()[0:1, 0:cw], ALU.mult, ALU.mult))(cw),
                    reads=[K("rowa", 0, 2048), K("rowb", 0, 2048)],
                    writes=[K("rowa", 0, 2048)], tag="fr2")
                B.op("dve", (lambda cw: lambda e: e.tensor_copy(
                    rowhi.ap()[0:1, 0:cw], rowb_.ap()[0:1, 0:cw]))(cw),
                    reads=[K("rowb", 0, 2048)], writes=[K("rowhi", 0, 1024)], tag="fh1")
                B.op("dve", (lambda cw: lambda e: e.tensor_tensor(
                    rowlo.ap()[0:1, 0:cw], rowb_.ap()[0:1, 0:cw],
                    rowhi.ap()[0:1, 0:cw], ALU.subtract))(cw),
                    reads=[K("rowb", 0, 2048), K("rowhi", 0, 1024)],
                    writes=[K("rowlo", 0, 1024)], tag="fl1")
                B.op("dve", (lambda cw: lambda e: e.tensor_copy(
                    rowhi2.ap()[0:1, 0:cw], rowa.ap()[0:1, 0:cw]))(cw),
                    reads=[K("rowa", 0, 2048)], writes=[K("rowhi2")], tag="fh2")
                B.op("dve", (lambda cw: lambda e: e.tensor_tensor(
                    rowlo2.ap()[0:1, 0:cw], rowa.ap()[0:1, 0:cw],
                    rowhi2.ap()[0:1, 0:cw], ALU.subtract))(cw),
                    reads=[K("rowa", 0, 2048), K("rowhi2")],
                    writes=[K("rowlo2")], tag="fl2")
                for dt in range(NT):
                    B.group("pe", [
                        (lambda dt, cw: lambda e: e.matmul(
                            ps[4].ap()[:, 0:cw], gamrowb.ap()[:, dt * 128:(dt + 1) * 128],
                            rowhi.ap()[0:1, 0:cw], start=True, stop=False))(dt, cw),
                        (lambda dt, cw: lambda e: e.matmul(
                            ps[4].ap()[:, 0:cw], gamrowb.ap()[:, dt * 128:(dt + 1) * 128],
                            rowlo.ap()[0:1, 0:cw], start=False, stop=True))(dt, cw)],
                        reads=[K("rowhi", 0, 1024), K("rowlo", 0, 1024), K("gamrowb")],
                        writes=[K("ps4")], tag="fbc1")
                    B.group("pe", [
                        (lambda dt, cw: lambda e: e.matmul(
                            ps[5].ap()[:, 0:cw], gamrowb.ap()[:, dt * 128:(dt + 1) * 128],
                            rowhi2.ap()[0:1, 0:cw], start=True, stop=False))(dt, cw),
                        (lambda dt, cw: lambda e: e.matmul(
                            ps[5].ap()[:, 0:cw], gamrowb.ap()[:, dt * 128:(dt + 1) * 128],
                            rowlo2.ap()[0:1, 0:cw], start=False, stop=True))(dt, cw)],
                        reads=[K("rowhi2"), K("rowlo2"), K("gamrowb")],
                        writes=[K("ps5")], tag="fbc2")
                    src0 = cs0 if dt < 4 else cs0 - 1
                    if blk > 0 and cc == 0 and dt >= 4:
                        pb0 = ((blk - 1) % 2) * 4
                        B.op("dve", (lambda dt, pb0: lambda e: e.tensor_tensor(
                            ft32.ap()[:, 0:1], shsnap.ap()[:, pb0 + dt - 4:pb0 + dt - 3],
                            ps[4].ap()[:, 0:1], ALU.mult))(dt, pb0),
                            reads=[K("shsnap", (pb0 + dt - 4) * 4, (pb0 + dt - 3) * 4), K("ps4")],
                            writes=[K("ft32", 0, 4)], tag="fxn1a")
                        B.op("dve", (lambda dt, cs0, cw: lambda e: e.tensor_tensor(
                            ft32.ap()[:, 1:cw], st_ap(dt, cs0, cs0 + cw - 1),
                            ps[4].ap()[:, 1:cw], ALU.mult))(dt, cs0, cw),
                            reads=[st_key(dt, cs0, cs0 + cw - 1), K("ps4")],
                            writes=[K("ft32", 4, 4 * cw)], tag="fxn1b")
                    else:
                        B.op("dve", (lambda dt, src0, cw: lambda e: e.tensor_tensor(
                            ft32.ap()[:, 0:cw], st_ap(dt, src0, src0 + cw),
                            ps[4].ap()[:, 0:cw], ALU.mult))(dt, src0, cw),
                            reads=[st_key(dt, src0, src0 + cw), K("ps4")],
                            writes=[K("ft32")], tag="fxn1")
                    B.op("dve", (lambda dt, cc, cw: lambda e: e.scalar_tensor_tensor(
                        xnfb.ap()[:, dt * FFN_TB + cc * 512:dt * FFN_TB + cc * 512 + cw],
                        ft32.ap()[:, 0:cw], betcolb.ap()[:, dt:dt + 1],
                        ps[5].ap()[:, 0:cw], ALU.add, ALU.add))(dt, cc, cw),
                        reads=[K("ft32"), K("betcolb"), K("ps5")],
                        writes=[K("xnfb", (dt * FFN_TB + cc * 512) * 2,
                                  (dt * FFN_TB + cc * 512 + cw) * 2)], tag="fxn2")
            for ip in range(NI_T):
                iw = min(128, DI - ip * 128)
                wsel = ip % 2
                B.op("dmw" if wsel == 0 else "dmw1",
                     (lambda ip, iw, wsel, l=l: lambda e: e.dma_start(
                    w1bb.ap()[:, wsel * 2048:wsel * 2048 + NT * iw].rearrange(
                        "p (dt h) -> p dt h", dt=NT),
                    W1_h[l, :, ip * 128:ip * 128 + iw].rearrange(
                        "(dt p) h -> p dt h", p=128)))(ip, iw, wsel),
                    writes=[K("w1bb", wsel * 4096, wsel * 4096 + 4096)], tag="w1a")
                B.op("dmw" if wsel == 0 else "dmw1",
                     (lambda ip, iw, wsel, l=l: lambda e: e.dma_start(
                    w1bb.ap()[:, wsel * 2048 + NT * iw:wsel * 2048 + 2 * NT * iw].rearrange(
                        "p (dt h) -> p dt h", dt=NT),
                    W1_h[l, :, DI + ip * 128:DI + ip * 128 + iw].rearrange(
                        "(dt p) h -> p dt h", p=128)))(ip, iw, wsel),
                    writes=[K("w1bb", wsel * 4096, wsel * 4096 + 4096)], tag="w1g")
                B.op("dmw", (lambda ip, iw, l=l: lambda e: e.dma_start(
                    b1rowb.ap()[0:1, 0:iw], b1_h[l, :, ip * 128:ip * 128 + iw]))(ip, iw),
                    writes=[K("b1rowb", 0, 256)], tag="b1a")
                B.op("dmw", (lambda ip, iw, l=l: lambda e: e.dma_start(
                    b1rowb.ap()[0:1, 128:128 + iw],
                    b1_h[l, :, DI + ip * 128:DI + ip * 128 + iw]))(ip, iw),
                    writes=[K("b1rowb", 256, 512)], tag="b1g")
                for cc in range(nch):
                    cw = min(512, tb - cc * 512)
                    for ag in range(2):
                        bank = (ip % 2) * 2 + ag
                        fns = [(lambda dt, iw, cc, cw, ag, bank, wsel: lambda e: e.matmul(
                            ps[bank].ap()[0:iw, 0:cw],
                            w1bb.ap()[:, wsel * 2048 + ag * NT * iw + dt * iw:
                                      wsel * 2048 + ag * NT * iw + (dt + 1) * iw],
                            xnfb.ap()[:, dt * FFN_TB + cc * 512:dt * FFN_TB + cc * 512 + cw],
                            start=(dt == 0), stop=False))(dt, iw, cc, cw, ag, bank, wsel)
                            for dt in range(NT)]
                        fns.append((lambda iw, cc, cw, ag, bank: lambda e: e.matmul(
                            ps[bank].ap()[0:iw, 0:cw],
                            b1rowb.ap()[0:1, ag * 128:ag * 128 + iw],
                            onesrowb.ap()[:, 0:cw], start=False, stop=True))(iw, cc, cw, ag, bank))
                        B.group("pe", fns,
                                reads=[K("w1bb", wsel * 4096, wsel * 4096 + 4096),
                                       K("xnfb"), K("b1rowb"), K("onesrowb")],
                                writes=[K(f"ps{bank}")], tag="w1mm")
                    gb = (ip % 2) * 2 + 1
                    B.op("act", (lambda iw, cw, gb: lambda e: e.activation(
                        gelbb.ap()[0:iw, 0:cw], ps[gb].ap()[0:iw, 0:cw], AF.Gelu))(iw, cw, gb),
                        reads=[K(f"ps{gb}")], writes=[K("gelbb")], tag="gelu")
                    ab = (ip % 2) * 2
                    B.op("dve", (lambda ip, iw, cc, cw, ab: lambda e: e.tensor_tensor(
                        gatedb.ap()[0:iw, ip * FFN_TB + cc * 512:ip * FFN_TB + cc * 512 + cw],
                        ps[ab].ap()[0:iw, 0:cw], gelbb.ap()[0:iw, 0:cw],
                        ALU.mult))(ip, iw, cc, cw, ab),
                        reads=[K(f"ps{ab}"), K("gelbb")],
                        writes=[K("gatedb", (ip * FFN_TB + cc * 512) * 2,
                                  (ip * FFN_TB + cc * 512 + cw) * 2)], tag="gated")
            for et in range(NT):
                wsel = et % 2
                B.op("dmw", (lambda et, l=l: lambda e: e.dma_start(
                    b2rowb.ap()[:, :], b2_h[l, :, et * 128:(et + 1) * 128]))(et),
                    writes=[K("b2rowb")], tag="b2")
                B.op("dmw" if wsel == 0 else "dmw1",
                     (lambda et, wsel, l=l: lambda e: e.dma_start(
                    w2bb.ap()[:, wsel * NI_T * 128:wsel * NI_T * 128 + 21 * 128].rearrange(
                        "p (it ec) -> p it ec", it=21),
                    W2_h[l, 0:2688, et * 128:(et + 1) * 128].rearrange(
                        "(it p) ec -> p it ec", p=128)))(et, wsel),
                    writes=[K("w2bb", wsel * 5632, (wsel + 1) * 5632)], tag="w2a")
                B.op("dmw" if wsel == 0 else "dmw1",
                     (lambda et, wsel, l=l: lambda e: e.dma_start(
                    w2bb.ap()[0:42, wsel * NI_T * 128 + 21 * 128:wsel * NI_T * 128 + 22 * 128],
                    W2_h[l, 2688:2730, et * 128:(et + 1) * 128]))(et, wsel),
                    writes=[K("w2bb", wsel * 5632, (wsel + 1) * 5632)], tag="w2b")
                for cc in range(nch):
                    cw = min(512, tb - cc * 512)
                    bank = 4 + (et * nch + cc) % 2
                    fns = []
                    for it in range(NI_T):
                        iw = min(128, DI - it * 128)
                        fns.append((lambda it, iw, cw, bank, wsel, cc=cc: lambda e: e.matmul(
                            ps[bank].ap()[:, 0:cw],
                            w2bb.ap()[0:iw, wsel * NI_T * 128 + it * 128:
                                      wsel * NI_T * 128 + it * 128 + 128],
                            gatedb.ap()[0:iw, it * FFN_TB + cc * 512:it * FFN_TB + cc * 512 + cw],
                            start=(it == 0), stop=False))(it, iw, cw, bank, wsel))
                    fns.append((lambda et, cw, bank: lambda e: e.matmul(
                        ps[bank].ap()[:, 0:cw], b2rowb.ap()[:, 0:128],
                        onesrowb.ap()[:, 0:cw], start=False, stop=True))(et, cw, bank))
                    B.group("pe", fns,
                            reads=[K("w2bb", wsel * 5632, (wsel + 1) * 5632),
                                   K("gatedb"), K("b2rowb"), K("onesrowb")],
                            writes=[K(f"ps{bank}")], tag="w2mm")
                    B.op("dve", (lambda et, t0, cc, cw, bank: lambda e: e.tensor_tensor(
                        st_ap(et, t0 + cc * 512, t0 + cc * 512 + cw),
                        st_ap(et, t0 + cc * 512, t0 + cc * 512 + cw),
                        ps[bank].ap()[:, 0:cw], ALU.add))(et, t0, cc, cw, bank),
                        reads=[st_key(et, t0 + cc * 512, t0 + cc * 512 + cw),
                               K(f"ps{bank}")],
                        writes=[st_key(et, t0 + cc * 512, t0 + cc * 512 + cw)],
                        tag="fres")
        if l < NLAYER - 1:
            z0, z1 = Fq, HALO
            c = z0
            while c < z1:
                cw = min(512, z1 - c)
                for dt in range(NT):
                    B.op("act", (lambda dt, c, cw: lambda e: e.activation(
                        st_ap(dt, c, c + cw), st_ap(dt, c, c + cw),
                        AF.Identity, bias=0.0, scale=zflagb.ap()[:, :]))(dt, c, cw),
                        reads=[st_key(dt, c, c + cw), K("zflagb")],
                        writes=[st_key(dt, c, c + cw)], tag="zap")
                c += cw
        B.barrier()

    for sl in range(min(sublayers, 2 * NLAYER)):
        l, which = sl // 2, sl % 2
        if which == 0:
            emit_attn(l)
        else:
            emit_ffn(l)

    for oc in range(OWN // 512):
        c0 = HALO + oc * 512
        B.op("dmx", (lambda oc: lambda e: e.dma_start(
            sldf.ap()[:, :].rearrange("p (dt c) -> p dt c", dt=NT),
            s_scr[:, oc * 512:(oc + 1) * 512].rearrange("(dt p) c -> p dt c", p=128)))(oc),
            writes=[K("sldf")], tag="s_ld")
        for dt in range(NT):
            B.op("act", (lambda dt, c0: lambda e: e.activation(
                sgf.ap()[:, dt * 512:(dt + 1) * 512],
                st_ap(dt, c0, c0 + 512), AF.Silu))(dt, c0),
                reads=[st_key(dt, c0, c0 + 512)],
                writes=[K("sgf", dt * 2048, (dt + 1) * 2048)], tag="silu")
            B.op("dve", (lambda dt: lambda e: e.tensor_tensor(
                sgf.ap()[:, dt * 512:(dt + 1) * 512],
                sgf.ap()[:, dt * 512:(dt + 1) * 512],
                sldf.ap()[:, dt * 512:(dt + 1) * 512], ALU.mult))(dt),
                reads=[K("sgf", dt * 2048, (dt + 1) * 2048),
                       K("sldf", dt * 2048, (dt + 1) * 2048)],
                writes=[K("sgf", dt * 2048, (dt + 1) * 2048)], tag="sg")
        for tt in range(4):
            t0 = oc * 512 + tt * 128
            B.op("dmx", (lambda t0: lambda e: e.dma_start(
                xotf.ap()[:, :], x_own[t0:t0 + 128, :]))(t0),
                writes=[K("xotf")], tag="xo_ld")
            fns = []
            for dt in range(NT):
                bank = 2 + dt // 4
                col = (dt % 4) * 128
                fns.append((lambda dt, tt, bank, col: lambda e: e.transpose(
                    ps[bank].ap()[:, col:col + 128],
                    sgf.ap()[:, dt * 512 + tt * 128:dt * 512 + tt * 128 + 128],
                    identb.ap()[:, :]))(dt, tt, bank, col))
            B.group("pe", fns, reads=[K("sgf"), K("identb")],
                    writes=[K("ps2"), K("ps3")], tag="sgT")
            B.op("dve", lambda e: e.tensor_tensor(
                xotf.ap()[:, 0:512], xotf.ap()[:, 0:512], ps[2].ap()[:, :], ALU.add),
                reads=[K("xotf"), K("ps2")], writes=[K("xotf")], tag="oadd1")
            B.op("dve", lambda e: e.tensor_tensor(
                xotf.ap()[:, 512:1024], xotf.ap()[:, 512:1024], ps[3].ap()[:, :], ALU.add),
                reads=[K("xotf"), K("ps3")], writes=[K("xotf")], tag="oadd2")
            B.op("dms", (lambda t0: lambda e: e.dma_start(
                out_t[t0:t0 + 128, :], xotf.ap()[:, :]))(t0),
                reads=[K("xotf")], tag="out_st")

    if DEBUG_DUMP:
        dbgs = [("d_ahb", ahb, 8), ("d_bcol", bcolb, 8), ("d_carry", carryb, 8),
                ("d_ych", ych, NT * 512), ("d_sgf", sgf, NT * 512),
                ("d_sldf", sldf, NT * 512)]
        for nm, buf, w in dbgs:
            o = nc.dram_tensor(nm, [128, w], F32, kind="ExternalOutput").ap()
            B.op("dms", (lambda o, buf, w: lambda e: e.dma_start(
                o[:, :], buf.ap()[:, 0:w]))(o, buf, w),
                reads=[K(NAMEOF[id(buf)])], tag="dbg")
        o = nc.dram_tensor("d_stream", [128, 2048], F32, kind="ExternalOutput").ap()
        B.op("dms", lambda e: e.dma_start(o[:, :], st_ap(0, HALO, HALO + 2048)),
             reads=[st_key(0, HALO, HALO + 2048)], tag="dbg")
    B.barrier()
    B.lower(nc)
    return nc


_NC_CACHE = {}


def kernel(**inputs):
    in_maps = _stage_inputs(inputs)
    key = (SUBLAYERS, DEBUG_DUMP)
    if key not in _NC_CACHE:
        _NC_CACHE[key] = build_program(SUBLAYERS)
    nc = _NC_CACHE[key]
    res = run_bass_kernel_spmd(nc, in_maps, core_ids=list(range(8)))
    out = np.zeros((4, 4096, 1024), dtype=np.float32)
    for core in range(8):
        b, half = core // 2, core % 2
        out[b, half * OWN:(half + 1) * OWN, :] = np.asarray(res.results[core]["out"])
    return out


# version 04 (final)
# speedup vs baseline: 1.0443x; 1.0417x over previous
import numpy as np
import ml_dtypes
import concourse.bass as bass
import concourse.mybir as mybir
from concourse.bass_utils import run_bass_kernel_spmd

F32 = mybir.dt.float32
BF16 = mybir.dt.bfloat16
AF = mybir.ActivationFunctionType
ALU = mybir.AluOpType

OWN = 2048
HALO = 1792
S = OWN + HALO
SOFF = 244
SW = S - SOFF
SSM = 4096
D = 1024
NT = 8
HEADS = 16
DH = 64
WIN = 256
DI = 2730
NI_T = 22
NLAYER = 6
ATT_OUT = [501, 759, 1017, 1275, 1533, 1791]
FFN_OUT = [502, 760, 1018, 1276, 1534, 1792]
FFN_TB = 896

SUBLAYERS = 12
DEBUG_DUMP = False

bf16 = ml_dtypes.bfloat16


class Chan:
    def __init__(self, name, engine, inc):
        self.name, self.engine, self.inc = name, engine, inc


CHANNELS = {
    "pe": Chan("pe", "tensor", 1),
    "dve": Chan("dve", "vector", 1),
    "act": Chan("act", "scalar", 1),
    "pool": Chan("pool", "gpsimd", 1),
    "dmw": Chan("dmw", "gpsimd", 16),
    "dmw1": Chan("dmw1", "gpsimd", 16),
    "dmw2": Chan("dmw2", "gpsimd", 16),
    "dmx": Chan("dmx", "sync", 16),
    "dms": Chan("dms", "sync", 16),
}
ENGINES = ("tensor", "vector", "scalar", "gpsimd", "sync")


NEPOCH = 6


class Builder:

    def __init__(self):
        self.items = {e: [] for e in ENGINES}
        self.cnt = {c: 0 for c in CHANNELS}
        self.dma_idx = {c: 0 for c in CHANNELS if CHANNELS[c].inc == 16}
        self.dma_cum = {}
        self.dma_rot = {c: False for c in self.dma_idx}
        self.last_w = {}
        self.readers = {}
        self.seen = {e: {} for e in ENGINES}
        self.serial = False

    def semkey(self, ch):
        return f"{ch}@{self.dma_idx[ch]}"

    @staticmethod
    def _ov(recs, lo, hi):
        return [x for x in recs if not (hi <= x[0] or x[1] <= lo)]

    def _dep(self, waits, semkey, val):
        ch = semkey.split("@")[0]
        if "@" in semkey:
            val = self.dma_cum.get(semkey, 0)
            if semkey == self.semkey(ch):
                self.dma_rot[ch] = True
        waits[semkey] = max(waits.get(semkey, 0), val)

    def op(self, ch, fn, reads=(), writes=(), tag=""):
        c = CHANNELS[ch]
        eng = c.engine
        waits = {}
        if self.serial:
            for chp, v in self.cnt.items():
                if v > 0:
                    waits[chp] = v
            for sk, v in self.dma_cum.items():
                if v > 0:
                    self._dep(waits, sk, v)
        else:
            for nm, lo, hi in reads:
                for _, _, sk, vv in self._ov(self.last_w.get(nm, []), lo, hi):
                    self._dep(waits, sk, vv)
            for nm, lo, hi in writes:
                for _, _, sk, vv in self._ov(self.last_w.get(nm, []), lo, hi):
                    self._dep(waits, sk, vv)
                for _, _, sk, vv in self._ov(self.readers.get(nm, []), lo, hi):
                    self._dep(waits, sk, vv)
        if c.inc == 16:
            if self.dma_rot[ch]:
                self.dma_idx[ch] = (self.dma_idx[ch] + 1) % NEPOCH
                self.dma_rot[ch] = False
                prev = self.dma_cum.get(self.semkey(ch), 0)
                if prev > 0:
                    waits[self.semkey(ch)] = max(
                        waits.get(self.semkey(ch), 0), prev)
            sk = self.semkey(ch)
            self.dma_cum[sk] = self.dma_cum.get(sk, 0) + 16
            ev = (sk, self.dma_cum[sk])
        else:
            self.cnt[ch] += 1
            ev = (ch, self.cnt[ch])
        final = []
        for skp, v in sorted(waits.items()):
            if v > self.seen[eng].get(skp, 0):
                final.append((skp, v))
                self.seen[eng][skp] = v
        self.items[eng].append((final, fn, ev[0], tag))
        for nm, lo, hi in reads:
            self.readers.setdefault(nm, []).append((lo, hi, ev[0], ev[1]))
        for nm, lo, hi in writes:
            lw = self.last_w.setdefault(nm, [])
            lw[:] = [x for x in lw if not (lo <= x[0] and x[1] <= hi)]
            lw.append((lo, hi, ev[0], ev[1]))
            rd = self.readers.get(nm)
            if rd:
                rd[:] = [x for x in rd if not (lo <= x[0] and x[1] <= hi)]
        return ev[1]

    def group(self, ch, fns, reads=(), writes=(), tag=""):
        def fused(e):
            inst = None
            for f in fns:
                inst = f(e)
            return inst
        return self.op(ch, fused, reads=reads, writes=writes, tag=tag)

    def barrier(self):
        snap = dict(self.cnt)
        snap.update(self.dma_cum)
        for ch in self.dma_idx:
            self.dma_rot[ch] = True
        for eng in ENGINES:
            w = []
            for skp, v in snap.items():
                if v > 0 and v > self.seen[eng].get(skp, 0):
                    w.append((skp, v))
                    self.seen[eng][skp] = v
            if w:
                self.items[eng].append((w, None, None, "barrier"))

    def lower(self, nc):
        sems = {}
        for chname, c in CHANNELS.items():
            if c.inc == 16:
                for i in range(NEPOCH):
                    sems[f"{chname}@{i}"] = nc.semaphore(f"sem_{chname}_{i}").__enter__()
            else:
                sems[chname] = nc.semaphore(f"sem_{chname}").__enter__()
        with nc.Block() as block:
            def body_for(eng_name):
                def body(eng):
                    for waits, fn, sk, tag in self.items[eng_name]:
                        for skp, v in waits:
                            eng.wait_ge(sems[skp], v)
                        if fn is not None:
                            inst = fn(eng)
                            inst.then_inc(sems[sk],
                                          16 if "@" in sk else 1)
                return body
            block.tensor(body_for("tensor"))
            block.vector(body_for("vector"))
            block.scalar(body_for("scalar"))
            block.gpsimd(body_for("gpsimd"))
            block.sync(body_for("sync"))


def _stage_inputs(inp):
    x = np.asarray(inp["L0_attn_x"], dtype=np.float32)

    prot = np.zeros((128, 128), dtype=np.float32)
    for d in range(128):
        dh = d % 64
        if dh < 32:
            prot[d + 32, d] = -1.0
        else:
            prot[d - 32, d] = 1.0

    half_idx = np.arange(0, DH, 2, dtype=np.float32)
    inv_freq = 1.0 / (10000.0 ** (half_idx / DH))
    base_scale = (half_idx + 0.4 * DH) / (1.4 * DH)
    dh_idx = np.arange(DH) % 32
    cs = np.arange(S, dtype=np.float32)

    def tables(own0):
        tg = (own0 + cs - HALO).astype(np.float32)
        theta = tg[None, :] * inv_freq[dh_idx][:, None]
        power = (tg - 2048.0) / 128.0
        scale = base_scale[dh_idx][:, None] ** power[None, :]
        A = np.cos(theta) * scale
        Bt = np.sin(theta) * scale
        C = np.cos(theta) / scale
        Dt = np.sin(theta) / scale
        two = lambda t: np.ascontiguousarray(
            np.concatenate([t, t], 0).astype(np.float32))
        return two(A), two(Bt), two(C), two(Dt)

    tabs = {0: tables(0), OWN: tables(OWN)}

    corr_left = np.zeros((1, 512), dtype=np.float32)
    ccs = np.arange(1536, 2048)
    tg = ccs - HALO
    corr_left[0, :] = np.where((tg >= 0) & (tg < 256), 256.0 - tg, 0.0)
    corr_left = corr_left.astype(bf16)
    corr_right = np.zeros((1, 512), dtype=bf16)


    fmT = lambda a: np.ascontiguousarray(a.T)
    col8 = lambda v: np.ascontiguousarray(v.reshape(8, 128).T)

    common = {
        "c_prot": prot.astype(bf16),
        "c_onescol": np.ones((128, 1), dtype=bf16),
        "c_onesmat": np.ones((128, 128), dtype=bf16),
        "c_onesrowf": np.ones((1, 128), dtype=np.float32),
        "c_onesrow": np.ones((1, 512), dtype=bf16),
        "c_negones": np.full((1, 128), -1.0, dtype=bf16),
        "c_ident": np.eye(128, dtype=np.float32),
        "c_blkones": np.ascontiguousarray(
            np.stack([np.r_[np.ones(64), np.zeros(64)],
                      np.r_[np.zeros(64), np.ones(64)]], 1).astype(bf16)),
        "ssm_a": col8(np.asarray(inp["L0_attn_a"], np.float32)),
        "ssm_b": col8(np.asarray(inp["L0_attn_b"], np.float32)),
        "w_outT": np.ascontiguousarray(np.asarray(inp["L0_attn_w_out"], np.float32).T),
        "b_out": np.asarray(inp["L0_attn_b_out"], np.float32).reshape(1, D),
        "Wqkv": np.asarray(inp["L0_ffn_Wqkv"], np.float32),
        "q_scale": np.asarray(inp["L0_ffn_q_scale"], np.float32),
        "k_scale": np.asarray(inp["L0_ffn_k_scale"], np.float32),
        "Wo": np.asarray(inp["L0_ffn_Wo"], np.float32),
        "ln_g": np.asarray(inp["L0_ffn_ln_g"], np.float32).reshape(NLAYER, 1, D),
        "ln_b": np.ascontiguousarray(np.stack(
            [col8(r) for r in np.asarray(inp["L0_ffn_ln_b"], np.float32)])),
        "W1": np.asarray(inp["L0_ffn_W1"], np.float32),
        "b1": np.asarray(inp["L0_ffn_b1"], np.float32).reshape(NLAYER, 1, 2 * DI),
        "W2": np.asarray(inp["L0_ffn_W2"], np.float32),
        "b2": np.asarray(inp["L0_ffn_b2"], np.float32).reshape(NLAYER, 1, D),
    }

    in_maps = []
    for core in range(8):
        b, half = core // 2, core % 2
        own0 = half * OWN
        if half == 0:
            xs = np.concatenate([np.zeros((OWN, D), np.float32), x[b, :OWN]], 0)
        else:
            xs = x[b]
        A, Bt, C, Dt = tabs[own0]
        m = dict(common)
        m.update({
            "x_ssm": fmT(xs),
            "x_own": np.ascontiguousarray(x[b, own0:own0 + OWN]),
            "corr": corr_left if half == 0 else corr_right,
            "zflag": (np.zeros((128, 1), np.float32) if half == 0
                      else np.ones((128, 1), np.float32)),
            "tabA": A, "tabB": Bt, "tabC": C, "tabD": Dt,
        })
        in_maps.append(m)
    return in_maps


def build_program(sublayers=SUBLAYERS):
    nc = bass.Bass("TRN2", target_bir_lowering=False)
    B = Builder()

    def din(name, shape, dtype=F32):
        return nc.dram_tensor(name, list(shape), dtype, kind="ExternalInput").ap()

    x_ssm = din("x_ssm", [D, SSM])
    x_own = din("x_own", [OWN, D])
    corr = din("corr", [1, 512], BF16)
    zflag = din("zflag", [128, 1])
    tabs_hbm = {"A": din("tabA", [128, S]), "B": din("tabB", [128, S]),
                "C": din("tabC", [128, S]), "D": din("tabD", [128, S])}
    c_prot = din("c_prot", [128, 128], BF16)
    c_onescol = din("c_onescol", [128, 1], BF16)
    c_onesmat = din("c_onesmat", [128, 128], BF16)
    c_onesrowf = din("c_onesrowf", [1, 128])
    c_onesrow = din("c_onesrow", [1, 512], BF16)
    c_negones = din("c_negones", [1, 128], BF16)
    c_ident = din("c_ident", [128, 128])
    c_blkones = din("c_blkones", [128, 2], BF16)
    ssm_a = din("ssm_a", [128, 8])
    ssm_b = din("ssm_b", [128, 8])
    w_outT = din("w_outT", [D, D])
    b_out = din("b_out", [1, D])
    Wqkv = din("Wqkv", [NLAYER, D, 3 * D])
    q_scale = din("q_scale", [NLAYER, DH])
    k_scale = din("k_scale", [NLAYER, DH])
    Wo_h = din("Wo", [NLAYER, D, D])
    ln_g = din("ln_g", [NLAYER, 1, D])
    ln_b = din("ln_b", [NLAYER, 128, 8])
    W1_h = din("W1", [NLAYER, D, 2 * DI])
    b1_h = din("b1", [NLAYER, 1, 2 * DI])
    W2_h = din("W2", [NLAYER, DI, D])
    b2_h = din("b2", [NLAYER, 1, D])

    out_t = nc.dram_tensor("out", [OWN, D], F32, kind="ExternalOutput").ap()
    s_scr = nc.dram_tensor("s_scratch", [D, OWN], F32, kind="Internal").ap()

    top = nc.sbuf_top
    base = ((top - nc.sbuf_bytes_remaining) + 63) // 64 * 64
    H = {}
    cur = [base]

    NAMEOF = {}

    def salloc(name, shape, dtype, off=None):
        nbytes = int(np.prod(shape[1:])) * mybir.dt.size(dtype)
        if off is None:
            o = cur[0]
            cur[0] = o + (nbytes + 63) // 64 * 64
        else:
            o = off
        assert o + nbytes <= top, f"SBUF overflow placing {name} at {o}+{nbytes} (top {top})"
        h = nc.alloc_sbuf_tensor_at(name, list(shape), dtype, offset=o)
        H[name] = h
        NAMEOF[id(h)] = name
        return h

    stream = salloc("stream", [128, NT * SW], F32)
    protb = salloc("protb", [128, 128], BF16)
    onesmatb = salloc("onesmatb", [128, 128], BF16)
    onesrowfb = salloc("onesrowfb", [1, 128], F32)
    onesrowb = salloc("onesrowb", [1, 512], BF16)
    negonesb = salloc("negonesb", [1, 128], BF16)
    identb = salloc("identb", [128, 128], F32)
    blkonesb = salloc("blkonesb", [128, 2], BF16)
    corrb = salloc("corrb", [1, 512], BF16)
    zflagb = salloc("zflagb", [128, 1], F32)
    ahb = salloc("ahb", [128, 8], F32)
    bcolb = salloc("bcolb", [128, 8], F32)
    carryb = salloc("carryb", [128, 8], F32)
    qspatb = salloc("qspatb", [2, 128], BF16)
    kspatb = salloc("kspatb", [2, 128], BF16)
    betcolb = salloc("betcolb", [128, 8], F32)
    gamrowb = salloc("gamrowb", [1, D], BF16)
    b2rowb = salloc("b2rowb", [1, 128], BF16)
    rowhi2 = salloc("rowhi2", [1, 512], BF16)
    shsnap = salloc("shsnap", [128, 8], F32)
    rowlo2 = salloc("rowlo2", [1, 512], BF16)
    b1rowb = salloc("b1rowb", [1, 256], BF16)
    rowa = salloc("rowa", [2, 512], F32)
    rowb_off = cur[0]
    rowb_ = salloc("rowb", [2, 512], F32)
    rowhi = salloc("rowhi", [2, 512], BF16)
    rowlo = salloc("rowlo", [2, 512], BF16)
    mtrib = salloc("mtrib", [128, 4 * 256], BF16, off=rowb_off)

    OV = cur[0]

    def oalloc(name, shape, dtype, off):
        return salloc(name, shape, dtype, off=OV + off)

    xch = oalloc("xch", [128, NT * 512], F32, 0)
    ych = oalloc("ych", [128, NT * 512], F32, 16384)
    x2ch = oalloc("x2ch", [128, NT * 512], BF16, 32768)
    ybfb = oalloc("ybfb", [128, NT * 512], BF16, 40960)
    wout_sb = oalloc("wout_sb", [128, NT * 1024], BF16, 49152)
    sevacb = oalloc("sevacb", [128, 512], F32, 65536)
    borowb = oalloc("borowb", [1, D], BF16, 67584)
    sgf = oalloc("sgf", [128, NT * 512], F32, 0)
    sldf = oalloc("sldf", [128, NT * 512], F32, 16384)
    xotf = oalloc("xotf", [128, 1024], F32, 32768)
    xsb = oalloc("xsb", [128, NT * 768], BF16, 0)
    kbuf = oalloc("kbuf", [128, NT * 768], BF16, 12288)
    vbuf = oalloc("vbuf", [128, 6 * 1024], BF16, 24576)
    qbuf = oalloc("qbuf", [128, NT * 512], BF16, 36864)
    aob = oalloc("aob", [128, NT * 512], BF16, 45056)
    tQA = oalloc("tQA", [128, 512], F32, 53248)
    tQB = oalloc("tQB", [128, 512], F32, 55296)
    tKC = oalloc("tKC", [128, 768], F32, 57344)
    tKD = oalloc("tKD", [128, 768], F32, 60416)
    q1b = oalloc("q1b", [128, 512], BF16, 64512)
    qsqb = oalloc("qsqb", [128, 512], BF16, 65536)
    uf32a = oalloc("uf32a", [128, 512], F32, 66560)
    uf32b = oalloc("uf32b", [128, 512], F32, 68608)
    ptb = oalloc("ptb", [128, 4 * 256], BF16, 70656)
    rsbuf = oalloc("rsbuf", [128, 256], F32, 72704)
    wqkb = oalloc("wqkb", [128, 2 * NT * 128], BF16, 73728)
    wvb = oalloc("wvb", [128, NT * 128], BF16, 77824)
    xnfb = oalloc("xnfb", [128, NT * FFN_TB], BF16, 0)
    gatedb = oalloc("gatedb", [128, NI_T * FFN_TB], BF16, 16384)
    gelbb = oalloc("gelbb", [128, 512], F32, 55808)
    fsqb = oalloc("fsqb", [128, 512], BF16, 57856)
    fxbb = oalloc("fxbb", [128, 512], BF16, 58880)
    ft32 = oalloc("ft32", [128, 512], F32, 59904)
    w1bb = oalloc("w1bb", [128, 2 * NT * 256], BF16, 61952)
    w2bb = oalloc("w2bb", [128, 2 * NI_T * 128], BF16, 70144)

    assert OV + 81920 <= top, f"overlay end over top ({OV + 81920} > {top})"
    phases = {
        "ph0": ["xch", "ych", "x2ch", "ybfb", "wout_sb", "sevacb", "borowb"],
        "fin": ["sgf", "sldf", "xotf"],
        "attn": ["xsb", "kbuf", "vbuf", "qbuf", "aob", "tQA", "tQB", "tKC",
                 "tKD", "q1b", "qsqb", "uf32a", "uf32b", "ptb",
                 "rsbuf", "wqkb", "wvb"],
        "ffn": ["xnfb", "gatedb", "gelbb", "fsqb", "fxbb", "ft32", "w1bb",
                "w2bb"],
    }
    for ph, names in phases.items():
        spans = sorted((H[n].manual_sbuf_range, n) for n in names)
        for i in range(len(spans) - 1):
            (a0, a1), n1 = spans[i]
            (b0, b1), n2 = spans[i + 1]
            assert a1 <= b0, f"{ph}: {n1} {a0}-{a1} overlaps {n2} {b0}-{b1}"

    ps = [nc.alloc_psum_tensor(f"bank{i}", [128, 512], F32) for i in range(8)]

    def K(name, lo=0, hi=1 << 30):
        return (name, lo, hi)

    def st_ap(dt, c0, c1):
        assert SOFF <= c0 < c1 <= S, (c0, c1)
        return stream.ap()[:, dt * SW + c0 - SOFF:dt * SW + c1 - SOFF]

    def st_key(dt, c0, c1):
        return ("stream", dt * SW + c0 - SOFF, dt * SW + c1 - SOFF)

    for dst, src in [(protb, c_prot), (onesmatb, c_onesmat),
                     (onesrowfb, c_onesrowf), (onesrowb, c_onesrow),
                     (negonesb, c_negones), (identb, c_ident),
                     (blkonesb, c_blkones), (corrb, corr), (zflagb, zflag),
                     (ahb, ssm_a), (bcolb, ssm_b)]:
        B.op("dmx", (lambda d, s: lambda e: e.dma_start(d.ap()[:, :], s[:, :]))(dst, src),
             writes=[K(NAMEOF[id(dst)])], tag="const")
    B.op("dmw", lambda e: e.dma_start(borowb.ap()[:, :], b_out[:, :]),
         writes=[K("borowb")], tag="b_out")
    B.op("dmw", lambda e: e.dma_start(
        wout_sb.ap()[:, :].rearrange("p (dt e) -> p dt e", dt=NT),
        w_outT[:, :].rearrange("(dt p) e -> p dt e", p=128)),
        writes=[K("wout_sb")], tag="woutT")
    B.op("act", lambda e: e.activation(ahb.ap()[:, :], ahb.ap()[:, :], AF.Tanh),
         reads=[K("ahb")], writes=[K("ahb")], tag="tanh")
    B.op("dve", lambda e: e.memset(carryb.ap()[:, :], 0.0),
         writes=[K("carryb")], tag="carry0")

    for kc in range(SSM // 512):
        c0 = kc * 512
        sc0 = c0 - 256
        xk = K("xch")
        B.op("dmx", (lambda c0: lambda e: e.dma_start(
            xch.ap()[:, :].rearrange("p (dt c) -> p dt c", dt=NT),
            x_ssm[:, c0:c0 + 512].rearrange("(dt p) c -> p dt c", p=128)))(c0),
            writes=[xk], tag=f"xld{kc}")
        B.op("dve", lambda e: e.tensor_copy(x2ch.ap()[:, :], xch.ap()[:, :]),
             reads=[xk], writes=[K("x2ch")], tag="xbf")
        B.group("pe", [(lambda dt: lambda e: e.matmul(
            ps[6].ap()[0:1, :], onesmatb.ap()[:, 0:1],
            x2ch.ap()[:, dt * 512:(dt + 1) * 512],
            start=(dt == 0), stop=(dt == NT - 1)))(dt) for dt in range(NT)],
            reads=[K("x2ch"), K("onesmatb")], writes=[K("ps6")], tag="sumx")
        B.op("act", lambda e: e.activation(x2ch.ap()[:, :], xch.ap()[:, :], AF.Square),
             reads=[xk], writes=[K("x2ch")], tag="sq")
        B.group("pe", [(lambda dt: lambda e: e.matmul(
            ps[7].ap()[0:1, :], onesmatb.ap()[:, 0:1],
            x2ch.ap()[:, dt * 512:(dt + 1) * 512],
            start=(dt == 0), stop=(dt == NT - 1)))(dt) for dt in range(NT)],
            reads=[K("x2ch"), K("onesmatb")], writes=[K("ps7")], tag="sumx2")
        B.op("act", lambda e: e.activation(rowa.ap()[0:1, :], ps[6].ap()[0:1, :],
                                           AF.Identity, bias=0.0, scale=1.0 / D),
             reads=[K("ps6")], writes=[K("rowa", 0, 2048)], tag="mu")
        B.op("act", lambda e: e.activation(rowb_.ap()[0:1, :], rowa.ap()[0:1, :],
                                           AF.Square),
             reads=[K("rowa", 0, 2048)], writes=[K("rowb", 0, 2048)], tag="mu2")
        B.op("dve", lambda e: e.scalar_tensor_tensor(
            rowb_.ap()[0:1, :], ps[7].ap()[0:1, :], 1.0 / D, rowb_.ap()[0:1, :],
            ALU.mult, ALU.subtract),
            reads=[K("ps7"), K("rowb", 0, 2048)], writes=[K("rowb", 0, 2048)], tag="var")
        B.op("dve", lambda e: e.tensor_scalar_add(rowb_.ap()[0:1, :],
                                                  rowb_.ap()[0:1, :], 1e-5),
             reads=[K("rowb", 0, 2048)], writes=[K("rowb", 0, 2048)], tag="eps")
        B.op("act", lambda e: e.activation(rowb_.ap()[0:1, :], rowb_.ap()[0:1, :],
                                           AF.Sqrt),
             reads=[K("rowb", 0, 2048)], writes=[K("rowb", 0, 2048)], tag="std")
        B.op("act", lambda e: e.activation(rowb_.ap()[0:1, :], rowb_.ap()[0:1, :],
                                           AF.Ln),
             reads=[K("rowb", 0, 2048)], writes=[K("rowb", 0, 2048)], tag="rln")
        B.op("act", lambda e: e.activation(rowb_.ap()[0:1, :], rowb_.ap()[0:1, :],
                                           AF.Exp, bias=0.0, scale=-1.0),
             reads=[K("rowb", 0, 2048)], writes=[K("rowb", 0, 2048)], tag="rstd")
        B.op("dve", lambda e: e.scalar_tensor_tensor(
            rowa.ap()[0:1, :], rowa.ap()[0:1, :], -1.0, rowb_.ap()[0:1, :],
            ALU.mult, ALU.mult),
            reads=[K("rowa", 0, 2048), K("rowb", 0, 2048)],
            writes=[K("rowa", 0, 2048)], tag="negmurstd")
        B.group("pe", [lambda e: e.matmul(ps[4].ap()[:, :], onesrowfb.ap()[:, :],
                                          rowb_.ap()[0:1, :], start=True, stop=True)],
                reads=[K("rowb", 0, 2048), K("onesrowfb")], writes=[K("ps4")],
                tag="bc_r1")
        B.group("pe", [lambda e: e.matmul(ps[5].ap()[:, :], onesrowfb.ap()[:, :],
                                          rowa.ap()[0:1, :], start=True, stop=True)],
                reads=[K("rowa", 0, 2048), K("onesrowfb")], writes=[K("ps5")],
                tag="bc_r2")
        for dt in range(NT):
            dk = K("xch", dt * 2048, (dt + 1) * 2048)
            B.op("dve", (lambda dt: lambda e: e.tensor_tensor(
                xch.ap()[:, dt * 512:(dt + 1) * 512],
                xch.ap()[:, dt * 512:(dt + 1) * 512],
                ps[4].ap()[:, :], ALU.mult))(dt),
                reads=[dk, K("ps4")], writes=[dk], tag="xn_mul")
            B.op("dve", (lambda dt: lambda e: e.tensor_tensor(
                xch.ap()[:, dt * 512:(dt + 1) * 512],
                xch.ap()[:, dt * 512:(dt + 1) * 512],
                ps[5].ap()[:, :], ALU.add))(dt),
                reads=[dk, K("ps5")], writes=[dk], tag="xn_add")
            lo = max(SOFF, sc0)
            if lo < sc0 + 512:
                off = lo - sc0
                B.op("act", (lambda dt, off, lo, sc0: lambda e: e.activation(
                    st_ap(dt, lo, sc0 + 512),
                    xch.ap()[:, dt * 512 + off:(dt + 1) * 512], AF.Copy))(dt, off, lo, sc0),
                    reads=[dk], writes=[st_key(dt, lo, sc0 + 512)], tag="xn_st")
            B.op("act", (lambda dt: lambda e: e.activation(
                xch.ap()[:, dt * 512:(dt + 1) * 512],
                xch.ap()[:, dt * 512:(dt + 1) * 512],
                AF.Identity, bias=0.0, scale=bcolb.ap()[:, dt:dt + 1]))(dt),
                reads=[dk, K("bcolb")], writes=[dk], tag="bx")
            yk = K("ych", dt * 2048, (dt + 1) * 2048)
            B.op("dve", (lambda dt: lambda e: e.tensor_tensor_scan(
                ych.ap()[:, dt * 512:(dt + 1) * 512],
                ahb.ap()[:, dt:dt + 1].broadcast_to([128, 512]),
                xch.ap()[:, dt * 512:(dt + 1) * 512],
                carryb.ap()[:, dt:dt + 1], ALU.mult, ALU.add))(dt),
                reads=[dk, K("ahb"), K("carryb", dt * 4, dt * 4 + 4)], writes=[yk],
                tag=f"scan{kc}_{dt}")
            B.op("dve", (lambda dt: lambda e: e.tensor_copy(
                carryb.ap()[:, dt:dt + 1],
                ych.ap()[:, dt * 512 + 511:dt * 512 + 512]))(dt),
                reads=[yk], writes=[K("carryb", dt * 4, dt * 4 + 4)], tag="carry")
        if kc >= 4:
            oc0 = (kc - 4) * 512
            B.op("dve", lambda e: e.tensor_copy(ybfb.ap()[:, :], ych.ap()[:, :]),
                 reads=[K("ych")], writes=[K("ybfb")], tag="ybf")
            for et in range(NT):
                bank = et % 2
                fns = []
                for dt in range(NT):
                    fns.append((lambda dt, et, bank: lambda e: e.matmul(
                        ps[bank].ap()[:, :],
                        wout_sb.ap()[:, dt * 1024 + et * 128:dt * 1024 + et * 128 + 128],
                        ybfb.ap()[:, dt * 512:(dt + 1) * 512],
                        start=(dt == 0), stop=False))(dt, et, bank))
                fns.append((lambda et, bank: lambda e: e.matmul(
                    ps[bank].ap()[:, :],
                    borowb.ap()[:, et * 128:et * 128 + 128],
                    onesrowb.ap()[:, :], start=False, stop=True))(et, bank))
                B.group("pe", fns,
                        reads=[K("ybfb"), K("wout_sb"), K("borowb"), K("onesrowb")],
                        writes=[K(f"ps{bank}")], tag=f"wout{et}")
                B.op("act", (lambda bank: lambda e: e.activation(
                    sevacb.ap()[:, :], ps[bank].ap()[:, :], AF.Copy))(bank),
                    reads=[K(f"ps{bank}")], writes=[K("sevacb")], tag="sevac")
                B.op("dms", (lambda et, oc0: lambda e: e.dma_start(
                    s_scr[et * 128:(et + 1) * 128, oc0:oc0 + 512],
                    sevacb.ap()[:, :]))(et, oc0),
                    reads=[K("sevacb")], tag="s_st")
    B.barrier()

    def emit_attn(l):
        Aq = ATT_OUT[l] - 1
        WQ0 = Aq // WIN
        B.op("pool", lambda e: e.memset(mtrib.ap()[:, :], 1.0),
             writes=[K("mtrib")], tag="mgen")
        for jc, mb, cm, stp in [(0, 0, 1, -1), (1, 128, 1, -1),
                                (2, 0, -1, 1), (3, -128, -1, 1)]:
            B.op("pool", (lambda jc, mb, cm, stp: lambda e: e.affine_select(
                out=mtrib.ap()[:, jc * 256:(jc + 1) * 256],
                in_=mtrib.ap()[:, jc * 256:(jc + 1) * 256],
                compare_op=ALU.is_ge, fill=0.0, base=mb,
                pattern=[[stp, 256]], channel_multiplier=cm))(jc, mb, cm, stp),
                reads=[K("mtrib", jc * 512, (jc + 1) * 512)],
                writes=[K("mtrib", jc * 512, (jc + 1) * 512)], tag="mgen")
        for pat, scl in [(qspatb, q_scale), (kspatb, k_scale)]:
            B.op("dve", (lambda pat: lambda e: e.memset(pat.ap()[:, :], 0.0))(pat),
                 writes=[K(NAMEOF[id(pat)])], tag="patz")
            B.op("dmw", (lambda pat, scl, l=l: lambda e: e.dma_start(
                pat.ap()[0:1, 0:64], scl[l:l + 1, :]))(pat, scl),
                reads=[K(NAMEOF[id(pat)])], writes=[K(NAMEOF[id(pat)])], tag="pat0")
            B.op("dmw", (lambda pat, scl, l=l: lambda e: e.dma_start(
                pat.ap()[1:2, 64:128], scl[l:l + 1, :]))(pat, scl),
                reads=[K(NAMEOF[id(pat)])], writes=[K(NAMEOF[id(pat)])], tag="pat1")

        groups = []
        w = WQ0
        nwin_total = S // WIN
        while w < nwin_total:
            g1 = min(w + 1, nwin_total - 1)
            groups.append((w, g1))
            w = g1 + 1

        prev_kspan = None
        for (g0, g1) in groups:
            q0 = max(Aq, g0 * WIN)
            q1 = (g1 + 1) * WIN
            qspan = q1 - q0
            assert qspan <= 512
            k0 = (g0 - 1) * WIN
            k1 = q1
            kspan = k1 - k0
            ntt = kspan // 128
            assert kspan <= 768
            for dt in range(NT):
                if prev_kspan is None:
                    cast0 = k0
                    zfill = max(k0, SOFF + 1)
                    if zfill > k0:
                        B.op("pool", (lambda dt, n: lambda e: e.memset(
                            xsb.ap()[:, dt * 768:dt * 768 + n], 0.0))(dt, zfill - k0),
                            writes=[K("xsb", dt * 1536, dt * 1536 + 2 * (zfill - k0))],
                            tag="xs_z")
                else:
                    cast0 = k0 + WIN
                    zfill = cast0
                    B.op("act", (lambda dt, pk: lambda e: e.activation(
                        xsb.ap()[:, dt * 768:dt * 768 + WIN],
                        xsb.ap()[:, dt * 768 + pk - WIN:dt * 768 + pk],
                        AF.Copy))(dt, prev_kspan),
                        reads=[K("xsb", dt * 1536 + 2 * (prev_kspan - WIN),
                                 dt * 1536 + 2 * prev_kspan)],
                        writes=[K("xsb", dt * 1536, dt * 1536 + 2 * WIN)],
                        tag="xs_cp")
                src0 = zfill if dt < 4 else zfill - 1
                B.op("act", (lambda dt, src0, zfill, k0, k1: lambda e: e.activation(
                    xsb.ap()[:, dt * 768 + (zfill - k0):dt * 768 + (k1 - k0)],
                    st_ap(dt, src0, src0 + (k1 - zfill)), AF.Copy))(dt, src0, zfill, k0, k1),
                    reads=[st_key(dt, src0, src0 + (k1 - zfill))],
                    writes=[K("xsb", dt * 1536 + 2 * (zfill - k0),
                              dt * 1536 + 2 * (k1 - k0))],
                    tag="xs")
            prev_kspan = kspan
            for dst, tab, a0, a1 in [(tQA, "A", q0, q1), (tQB, "B", q0, q1),
                                     (tKC, "C", k0, k1), (tKD, "D", k0, k1)]:
                B.op("dmx", (lambda dst, tab, a0, a1: lambda e: e.dma_start(
                    dst.ap()[:, 0:a1 - a0], tabs_hbm[tab][:, a0:a1]))(dst, tab, a0, a1),
                    writes=[K(NAMEOF[id(dst)])], tag="tab")
            vthunks = []
            gvc = 0
            for vh in range(8):
                def wv_t(vh=vh, l=l):
                    B.op("dmw2", (lambda vh, l: lambda e: e.dma_start(
                        wvb.ap()[:, :].rearrange("p (dt h) -> p dt h", dt=NT),
                        Wqkv[l, :, 2 * D + vh * 128:2 * D + (vh + 1) * 128].rearrange(
                            "(dt p) h -> p dt h", p=128)))(vh, l),
                        writes=[K("wvb")], tag="wv")
                vthunks.append(("wv", wv_t))
                for tt in range(ntt):
                    bank = 4 + gvc % 4
                    def vmm_t(vh=vh, tt=tt, bank=bank):
                        fns = [(lambda dt, tt, bank: lambda e: e.matmul(
                            ps[bank].ap()[:, 0:128],
                            xsb.ap()[:, dt * 768 + tt * 128:dt * 768 + (tt + 1) * 128],
                            wvb.ap()[:, dt * 128:(dt + 1) * 128],
                            start=(dt == 0), stop=(dt == NT - 1)))(dt, tt, bank)
                            for dt in range(NT)]
                        B.group("pe", fns, reads=[K("xsb"), K("wvb")],
                                writes=[K(f"ps{bank}")], tag="vmm")
                        B.op("dve", (lambda tt, vh, bank: lambda e: e.tensor_copy(
                            vbuf.ap()[:, tt * 1024 + vh * 128:tt * 1024 + (vh + 1) * 128],
                            ps[bank].ap()[:, 0:128]))(tt, vh, bank),
                            reads=[K(f"ps{bank}")],
                            writes=[K("vbuf", (tt * 1024 + vh * 128) * 2,
                                      (tt * 1024 + (vh + 1) * 128) * 2)],
                            tag="vev")
                    vthunks.append(("v", vmm_t))
                    gvc += 1
            vpos = [0]

            def vprime():
                while vpos[0] < len(vthunks) and vthunks[vpos[0]][0] == "wv":
                    vthunks[vpos[0]][1]()
                    vpos[0] += 1

            def vfill():
                if vpos[0] < len(vthunks):
                    vthunks[vpos[0]][1]()
                    vpos[0] += 1
                    vprime()
            vprime()
            for side, wcol0, patb, tA_, tB_, obuf, ospan, s0, s1 in [
                    ("q", 0, qspatb, tQA, tQB, qbuf, 512, q0, q1),
                    ("k", D, kspatb, tKC, tKD, kbuf, 768, k0, k1)]:
                for hp in range(8):
                    wsel = hp % 2
                    B.op("dmw" if wsel == 0 else "dmw1",
                         (lambda hp, wcol0, wsel, l=l: lambda e: e.dma_start(
                        wqkb.ap()[:, wsel * 1024:(wsel + 1) * 1024].rearrange(
                            "p (dt h) -> p dt h", dt=NT),
                        Wqkv[l, :, wcol0 + hp * 128:wcol0 + (hp + 1) * 128].rearrange(
                            "(dt p) h -> p dt h", p=128)))(hp, wcol0, wsel),
                        writes=[K("wqkb", wsel * 2048, (wsel + 1) * 2048)], tag="wqk")
                    span = s1 - s0
                    nch = (span + 511) // 512
                    for cc in range(nch):
                        cw = min(512, span - cc * 512)
                        xoff = (s0 - k0) + cc * 512
                        pit = (0 if side == "q" else 8) + hp * nch + cc
                        pp = pit % 2
                        bank = pp
                        fns = [(lambda dt, cw, xoff, bank, wsel: lambda e: e.matmul(
                            ps[bank].ap()[:, 0:cw],
                            wqkb.ap()[:, wsel * 1024 + dt * 128:wsel * 1024 + (dt + 1) * 128],
                            xsb.ap()[:, dt * 768 + xoff:dt * 768 + xoff + cw],
                            start=(dt == 0), stop=(dt == NT - 1)))(dt, cw, xoff, bank, wsel)
                            for dt in range(NT)]
                        B.group("pe", fns,
                                reads=[K("wqkb", wsel * 2048, (wsel + 1) * 2048), K("xsb")],
                                writes=[K(f"ps{bank}")], tag=f"{side}mm")
                        B.op("act", (lambda bank, cw: lambda e: e.activation(
                            qsqb.ap()[:, 0:cw], ps[bank].ap()[:, 0:cw], AF.Square))(bank, cw),
                            reads=[K(f"ps{bank}")], writes=[K("qsqb")], tag="sqq")
                        vfill()
                        B.group("pe", [(lambda cw, pp: lambda e: e.matmul(
                            ps[2 + pp].ap()[0:2, 0:cw], blkonesb.ap()[:, :],
                            qsqb.ap()[:, 0:cw], start=True, stop=True))(cw, pp)],
                            reads=[K("qsqb"), K("blkonesb")], writes=[K(f"ps{2+pp}")], tag="ssq")
                        B.op("act", (lambda cw, pp: lambda e: e.activation(
                            rowa.ap()[:, 0:cw], ps[2 + pp].ap()[0:2, 0:cw], AF.Sqrt))(cw, pp),
                            reads=[K(f"ps{2+pp}")], writes=[K("rowa")], tag="nrm")
                        B.op("dve", (lambda cw: lambda e: e.tensor_scalar_max(
                            rowa.ap()[:, 0:cw], rowa.ap()[:, 0:cw], 1e-12))(cw),
                            reads=[K("rowa")], writes=[K("rowa")], tag="nmx")
                        B.op("act", (lambda cw: lambda e: e.activation(
                            rowa.ap()[:, 0:cw], rowa.ap()[:, 0:cw], AF.Ln))(cw),
                            reads=[K("rowa")], writes=[K("rowa")], tag="nln")
                        B.op("act", (lambda cw: lambda e: e.activation(
                            rowa.ap()[:, 0:cw], rowa.ap()[:, 0:cw], AF.Exp,
                            bias=0.0, scale=-1.0))(cw),
                            reads=[K("rowa")], writes=[K("rowa")], tag="nrc")
                        B.op("dve", (lambda cw: lambda e: e.tensor_copy(
                            rowhi.ap()[:, 0:cw], rowa.ap()[:, 0:cw]))(cw),
                            reads=[K("rowa")], writes=[K("rowhi")], tag="nhi")
                        B.op("dve", (lambda cw: lambda e: e.tensor_tensor(
                            rowlo.ap()[:, 0:cw], rowa.ap()[:, 0:cw],
                            rowhi.ap()[:, 0:cw], ALU.subtract))(cw),
                            reads=[K("rowa"), K("rowhi")], writes=[K("rowlo")], tag="nlo")
                        vfill()
                        B.group("pe", [
                            (lambda patb, cw, pp: lambda e: e.matmul(
                                ps[2 + pp].ap()[:, 0:cw], patb.ap()[:, :],
                                rowhi.ap()[:, 0:cw], start=True, stop=False))(patb, cw, pp),
                            (lambda patb, cw, pp: lambda e: e.matmul(
                                ps[2 + pp].ap()[:, 0:cw], patb.ap()[:, :],
                                rowlo.ap()[:, 0:cw], start=False, stop=True))(patb, cw, pp)],
                            reads=[K("rowhi"), K("rowlo"), K(NAMEOF[id(patb)])],
                            writes=[K(f"ps{2+pp}")], tag="rnbc")
                        B.op("act", (lambda cw, pp: lambda e: e.activation(
                            uf32b.ap()[:, 0:cw], ps[2 + pp].ap()[:, 0:cw], AF.Copy))(cw, pp),
                            reads=[K(f"ps{2+pp}")], writes=[K("uf32b")], tag="rnev")
                        B.op("dve", (lambda bank, cw: lambda e: e.tensor_tensor(
                            uf32a.ap()[:, 0:cw], ps[bank].ap()[:, 0:cw],
                            uf32b.ap()[:, 0:cw], ALU.mult))(bank, cw),
                            reads=[K(f"ps{bank}"), K("uf32b")], writes=[K("uf32a")], tag="q1f")
                        B.op("dve", (lambda cw: lambda e: e.tensor_copy(
                            q1b.ap()[:, 0:cw], uf32a.ap()[:, 0:cw]))(cw),
                            reads=[K("uf32a")], writes=[K("q1b")], tag="q1b")
                        vfill()
                        B.group("pe", [(lambda cw, pp: lambda e: e.matmul(
                            ps[2 + pp].ap()[:, 0:cw], protb.ap()[:, :],
                            q1b.ap()[:, 0:cw], start=True, stop=True))(cw, pp)],
                            reads=[K("q1b"), K("protb")], writes=[K(f"ps{2+pp}")], tag="rot")
                        toff = cc * 512
                        B.op("dve", (lambda tA_, cw, toff: lambda e: e.tensor_tensor(
                            uf32a.ap()[:, 0:cw], tA_.ap()[:, toff:toff + cw],
                            uf32a.ap()[:, 0:cw], ALU.mult))(tA_, cw, toff),
                            reads=[K(NAMEOF[id(tA_)]), K("uf32a")], writes=[K("uf32a")], tag="rotA")
                        B.op("dve", (lambda tB_, cw, toff, pp: lambda e: e.tensor_tensor(
                            uf32b.ap()[:, 0:cw], tB_.ap()[:, toff:toff + cw],
                            ps[2 + pp].ap()[:, 0:cw], ALU.mult))(tB_, cw, toff, pp),
                            reads=[K(NAMEOF[id(tB_)]), K(f"ps{2+pp}")], writes=[K("uf32b")], tag="rotB")
                        B.op("dve", (lambda obuf, hp, ospan, cw, cc: lambda e: e.tensor_tensor(
                            obuf.ap()[:, hp * ospan + cc * 512:hp * ospan + cc * 512 + cw],
                            uf32a.ap()[:, 0:cw], uf32b.ap()[:, 0:cw],
                            ALU.add))(obuf, hp, ospan, cw, cc),
                            reads=[K("uf32a"), K("uf32b")],
                            writes=[K(NAMEOF[id(obuf)], (hp * ospan + cc * 512) * 2,
                                      (hp * ospan + cc * 512 + cw) * 2)], tag="qkrot")
            while vpos[0] < len(vthunks):
                vfill()
            iters = []
            for w in range(g0, g1 + 1):
                qw0 = max(Aq, w * WIN)
                qw1 = (w + 1) * WIN
                qn = qw1 - qw0
                jb0 = (w - 1) * WIN - k0
                for h in range(HEADS):
                    iters.append((w, qw0, qn, jb0, h))

            def em_scores(it, i):
                w, qw0, qn, jb0, h = it
                p = i % 2
                sb = (2, 3) if p == 0 else (0, 1)
                hp, hs = h // 2, (h % 2) * 64
                live = []
                for jc in range(4):
                    base1 = qw0 - (w - 1) * WIN - 128 * jc
                    base2 = (w - 1) * WIN + 128 * jc + WIN - qw0
                    if base1 + qn - 1 < 0 or base2 + 127 < 0:
                        continue
                    live.append(jc)
                fns = []
                for jc in live:
                    bank, colo = sb[jc // 2], (jc % 2) * 256
                    fns.append((lambda jc, bank, colo, hp, hs, qn, qw0, jb0, q0:
                                lambda e: e.matmul(
                        ps[bank].ap()[:, colo:colo + qn],
                        kbuf.ap()[hs:hs + 64,
                                  hp * 768 + jb0 + jc * 128:hp * 768 + jb0 + (jc + 1) * 128],
                        qbuf.ap()[hs:hs + 64,
                                  hp * 512 + (qw0 - q0):hp * 512 + (qw0 - q0) + qn],
                        start=True, stop=True))(jc, bank, colo, hp, hs, qn, qw0, jb0, q0))
                B.group("pe", fns, reads=[K("kbuf"), K("qbuf")],
                        writes=[K(f"ps{sb[0]}"), K(f"ps{sb[1]}")], tag="sc")
                return live

            def em_mask(it, i, live):
                w, qw0, qn, jb0, h = it
                p = i % 2
                sb = (2, 3) if p == 0 else (0, 1)
                if qn == 256:
                    assert len(live) == 4
                    for half in range(2):
                        B.op("act", (lambda half, sb: lambda e: e.activation(
                            ptb.ap()[:, half * 512:(half + 1) * 512],
                            ps[sb[half]].ap()[:, 0:512],
                            AF.Exp, bias=0.0, scale=8.0))(half, sb),
                            reads=[K(f"ps{sb[half]}")],
                            writes=[K("ptb", half * 1024, (half + 1) * 1024)],
                            tag="exp")
                    for jc in range(4):
                        B.op("dve", (lambda jc: lambda e: e.tensor_tensor(
                            ptb.ap()[:, jc * 256:(jc + 1) * 256],
                            ptb.ap()[:, jc * 256:(jc + 1) * 256],
                            mtrib.ap()[:, jc * 256:(jc + 1) * 256],
                            ALU.mult))(jc),
                            reads=[K("ptb", jc * 512, (jc + 1) * 512), K("mtrib")],
                            writes=[K("ptb", jc * 512, (jc + 1) * 512)], tag="msk")
                    return
                for jc in range(4):
                    pk = K("ptb", jc * 512, jc * 512 + qn * 2)
                    if jc not in live:
                        B.op("pool", (lambda jc, qn: lambda e: e.memset(
                            ptb.ap()[:, jc * 256:jc * 256 + qn], 0.0))(jc, qn),
                            writes=[pk], tag="pz")
                        continue
                    bank, colo = sb[jc // 2], (jc % 2) * 256
                    B.op("act", (lambda jc, bank, colo, qn: lambda e: e.activation(
                        ptb.ap()[:, jc * 256:jc * 256 + qn],
                        ps[bank].ap()[:, colo:colo + qn],
                        AF.Exp, bias=0.0, scale=8.0))(jc, bank, colo, qn),
                        reads=[K(f"ps{bank}")], writes=[pk], tag="exp")
                    base1 = qw0 - (w - 1) * WIN - 128 * jc
                    if base1 - 127 < 0:
                        B.op("pool", (lambda jc, qn, base1: lambda e: e.affine_select(
                            out=ptb.ap()[:, jc * 256:jc * 256 + qn],
                            in_=ptb.ap()[:, jc * 256:jc * 256 + qn],
                            compare_op=ALU.is_ge, fill=0.0, base=base1,
                            pattern=[[1, qn]], channel_multiplier=-1))(jc, qn, base1),
                            reads=[pk], writes=[pk], tag="m1")
                    base2 = (w - 1) * WIN + 128 * jc + WIN - qw0
                    if base2 - (qn - 1) < 0:
                        B.op("pool", (lambda jc, qn, base2: lambda e: e.affine_select(
                            out=ptb.ap()[:, jc * 256:jc * 256 + qn],
                            in_=ptb.ap()[:, jc * 256:jc * 256 + qn],
                            compare_op=ALU.is_ge, fill=0.0, base=base2,
                            pattern=[[-1, qn]], channel_multiplier=1))(jc, qn, base2),
                            reads=[pk], writes=[pk], tag="m2")

            def em_sumav(it, i):
                w, qw0, qn, jb0, h = it
                p = i % 2
                hp, hs = h // 2, (h % 2) * 64
                sumb, avb = 4 + p, 6 + p
                fns = [(lambda jc, qn, sumb: lambda e: e.matmul(
                    ps[sumb].ap()[:, 0:qn], onesmatb.ap()[:, :],
                    ptb.ap()[:, jc * 256:jc * 256 + qn],
                    start=(jc == 0), stop=False))(jc, qn, sumb) for jc in range(4)]
                if 1536 <= qw0 < 2048:
                    fns.append((lambda qn, qw0, sumb: lambda e: e.matmul(
                        ps[sumb].ap()[:, 0:qn], negonesb.ap()[:, :],
                        corrb.ap()[:, qw0 - 1536:qw0 - 1536 + qn],
                        start=False, stop=True))(qn, qw0, sumb))
                else:
                    fns[-1] = (lambda qn, sumb: lambda e: e.matmul(
                        ps[sumb].ap()[:, 0:qn], onesmatb.ap()[:, :],
                        ptb.ap()[:, 3 * 256:3 * 256 + qn],
                        start=False, stop=True))(qn, sumb)
                B.group("pe", fns,
                        reads=[K("ptb"), K("onesmatb"), K("negonesb"), K("corrb")],
                        writes=[K(f"ps{sumb}")], tag="sum")
                fns = [(lambda jc, qn, h, jb0, avb: lambda e: e.matmul(
                    ps[avb].ap()[0:64, 0:qn],
                    vbuf.ap()[:, (jb0 // 128 + jc) * 1024 + h * 64:
                              (jb0 // 128 + jc) * 1024 + (h + 1) * 64],
                    ptb.ap()[:, jc * 256:jc * 256 + qn],
                    start=(jc == 0), stop=(jc == 3)))(jc, qn, h, jb0, avb)
                    for jc in range(4)]
                B.group("pe", fns, reads=[K("ptb"), K("vbuf")],
                        writes=[K(f"ps{avb}")], tag="av")
                B.op("act", (lambda qn, sumb: lambda e: e.activation(
                    rsbuf.ap()[:, 0:qn], ps[sumb].ap()[:, 0:qn], AF.Ln))(qn, sumb),
                    reads=[K(f"ps{sumb}")], writes=[K("rsbuf")], tag="rln")
                B.op("act", (lambda qn: lambda e: e.activation(
                    rsbuf.ap()[:, 0:qn], rsbuf.ap()[:, 0:qn], AF.Exp,
                    bias=0.0, scale=-1.0))(qn),
                    reads=[K("rsbuf")], writes=[K("rsbuf")], tag="rs")
                B.op("dve", (lambda qn, hp, hs, qw0, q0, avb: lambda e: e.tensor_tensor(
                    aob.ap()[hs:hs + 64,
                             hp * 512 + (qw0 - q0):hp * 512 + (qw0 - q0) + qn],
                    ps[avb].ap()[0:64, 0:qn], rsbuf.ap()[0:64, 0:qn],
                    ALU.mult))(qn, hp, hs, qw0, q0, avb),
                    reads=[K(f"ps{avb}"), K("rsbuf")],
                    writes=[K("aob", (hp * 512 + (qw0 - q0)) * 2,
                              (hp * 512 + (qw0 - q0) + qn) * 2)], tag="aon")

            live_i = em_scores(iters[0], 0)
            for i, it in enumerate(iters):
                em_mask(it, i, live_i)
                if i + 1 < len(iters):
                    live_i = em_scores(iters[i + 1], i + 1)
                em_sumav(it, i)
            for et in range(NT):
                wsel = et % 2
                B.op("dmw" if wsel == 0 else "dmw1",
                     (lambda et, wsel, l=l: lambda e: e.dma_start(
                    wqkb.ap()[:, wsel * 1024:(wsel + 1) * 1024].rearrange(
                        "p (it ec) -> p it ec", it=NT),
                    Wo_h[l, :, et * 128:(et + 1) * 128].rearrange(
                        "(it p) ec -> p it ec", p=128)))(et, wsel),
                    writes=[K("wqkb", wsel * 2048, (wsel + 1) * 2048)], tag="wo")
                bank = 6 + et % 2
                fns = [(lambda it, qspan, bank, wsel: lambda e: e.matmul(
                    ps[bank].ap()[:, 0:qspan],
                    wqkb.ap()[:, wsel * 1024 + it * 128:wsel * 1024 + (it + 1) * 128],
                    aob.ap()[:, it * 512:it * 512 + qspan],
                    start=(it == 0), stop=(it == NT - 1)))(it, qspan, bank, wsel)
                    for it in range(NT)]
                B.group("pe", fns,
                        reads=[K("wqkb", wsel * 2048, (wsel + 1) * 2048), K("aob")],
                        writes=[K(f"ps{bank}")], tag="womm")
                B.op("dve", (lambda et, qspan, bank, q0: lambda e: e.tensor_tensor(
                    st_ap(et, q0, q0 + qspan), st_ap(et, q0, q0 + qspan),
                    ps[bank].ap()[:, 0:qspan], ALU.add))(et, qspan, bank, q0),
                    reads=[st_key(et, q0, q0 + qspan), K(f"ps{bank}")],
                    writes=[st_key(et, q0, q0 + qspan)], tag="ares")
        B.barrier()

    def emit_ffn(l):
        Fq = FFN_OUT[l]
        B.op("dmw", (lambda l=l: lambda e: e.dma_start(gamrowb.ap()[:, :], ln_g[l, :, :]))(),
             writes=[K("gamrowb")], tag="gam")
        B.op("dmx", (lambda l=l: lambda e: e.dma_start(betcolb.ap()[:, :], ln_b[l, :, :]))(),
             writes=[K("betcolb")], tag="bet")
        span_all = S - Fq
        nblk = (span_all + FFN_TB - 1) // FFN_TB
        for blk in range(nblk):
            t0 = Fq + blk * FFN_TB
            t1 = min(S, t0 + FFN_TB)
            tb = t1 - t0
            nch = (tb + 511) // 512
            if t1 < S:
                sb0 = (blk % 2) * 4
                for dt in range(4, NT):
                    B.op("dve", (lambda dt, t1, sb0: lambda e: e.tensor_copy(
                        shsnap.ap()[:, sb0 + dt - 4:sb0 + dt - 3],
                        st_ap(dt, t1 - 1, t1)))(dt, t1, sb0),
                        reads=[st_key(dt, t1 - 1, t1)],
                        writes=[K("shsnap", (sb0 + dt - 4) * 4, (sb0 + dt - 3) * 4)],
                        tag="shsn")
            for cc in range(nch):
                cw = min(512, tb - cc * 512)
                cs0 = t0 + cc * 512
                for dt in range(NT):
                    src0 = cs0 if dt < 4 else cs0 - 1
                    patch = (blk > 0 and cc == 0 and dt >= 4)
                    if patch:
                        pb0 = ((blk - 1) % 2) * 4
                        B.op("dve", (lambda dt, pb0: lambda e: e.tensor_copy(
                            fxbb.ap()[:, 0:1], shsnap.ap()[:, pb0 + dt - 4:pb0 + dt - 3]))(dt, pb0),
                            reads=[K("shsnap", (pb0 + dt - 4) * 4, (pb0 + dt - 3) * 4)],
                            writes=[K("fxbb", 0, 2)], tag="fxb0")
                        B.op("act", (lambda dt, cs0, cw: lambda e: e.activation(
                            fxbb.ap()[:, 1:cw], st_ap(dt, cs0, cs0 + cw - 1),
                            AF.Copy))(dt, cs0, cw),
                            reads=[st_key(dt, cs0, cs0 + cw - 1)],
                            writes=[K("fxbb", 2, 2 * cw)], tag="fxb")
                    else:
                        B.op("act", (lambda dt, src0, cw: lambda e: e.activation(
                            fxbb.ap()[:, 0:cw], st_ap(dt, src0, src0 + cw), AF.Copy))(dt, src0, cw),
                            reads=[st_key(dt, src0, src0 + cw)], writes=[K("fxbb")], tag="fxb")
                    B.group("pe", [(lambda dt, cw: lambda e: e.matmul(
                        ps[6].ap()[0:1, 0:cw], onesmatb.ap()[:, 0:1],
                        fxbb.ap()[:, 0:cw], start=(dt == 0), stop=(dt == NT - 1)))(dt, cw)],
                        reads=[K("fxbb"), K("onesmatb")], writes=[K("ps6")], tag="fsx")
                    B.op("act", (lambda cw: lambda e: e.activation(
                        fsqb.ap()[:, 0:cw], fxbb.ap()[:, 0:cw], AF.Square))(cw),
                        reads=[K("fxbb")], writes=[K("fsqb")], tag="fsq")
                    B.group("pe", [(lambda dt, cw: lambda e: e.matmul(
                        ps[7].ap()[0:1, 0:cw], onesmatb.ap()[:, 0:1],
                        fsqb.ap()[:, 0:cw], start=(dt == 0), stop=(dt == NT - 1)))(dt, cw)],
                        reads=[K("fsqb"), K("onesmatb")], writes=[K("ps7")], tag="fsx2")
                B.op("act", (lambda cw: lambda e: e.activation(
                    rowa.ap()[0:1, 0:cw], ps[6].ap()[0:1, 0:cw],
                    AF.Identity, bias=0.0, scale=1.0 / D))(cw),
                    reads=[K("ps6")], writes=[K("rowa", 0, 2048)], tag="fmu")
                B.op("act", (lambda cw: lambda e: e.activation(
                    rowb_.ap()[0:1, 0:cw], rowa.ap()[0:1, 0:cw], AF.Square))(cw),
                    reads=[K("rowa", 0, 2048)], writes=[K("rowb", 0, 2048)], tag="fmu2")
                B.op("dve", (lambda cw: lambda e: e.scalar_tensor_tensor(
                    rowb_.ap()[0:1, 0:cw], ps[7].ap()[0:1, 0:cw], 1.0 / D,
                    rowb_.ap()[0:1, 0:cw], ALU.mult, ALU.subtract))(cw),
                    reads=[K("ps7"), K("rowb", 0, 2048)], writes=[K("rowb", 0, 2048)],
                    tag="fvar")
                B.op("dve", (lambda cw: lambda e: e.tensor_scalar_add(
                    rowb_.ap()[0:1, 0:cw], rowb_.ap()[0:1, 0:cw], 1e-5))(cw),
                    reads=[K("rowb", 0, 2048)], writes=[K("rowb", 0, 2048)], tag="feps")
                B.op("act", (lambda cw: lambda e: e.activation(
                    rowb_.ap()[0:1, 0:cw], rowb_.ap()[0:1, 0:cw],
                    AF.Sqrt))(cw),
                    reads=[K("rowb", 0, 2048)], writes=[K("rowb", 0, 2048)], tag="fstd")
                B.op("act", (lambda cw: lambda e: e.activation(
                    rowb_.ap()[0:1, 0:cw], rowb_.ap()[0:1, 0:cw], AF.Ln))(cw),
                    reads=[K("rowb", 0, 2048)], writes=[K("rowb", 0, 2048)], tag="fln")
                B.op("act", (lambda cw: lambda e: e.activation(
                    rowb_.ap()[0:1, 0:cw], rowb_.ap()[0:1, 0:cw], AF.Exp,
                    bias=0.0, scale=-1.0))(cw),
                    reads=[K("rowb", 0, 2048)], writes=[K("rowb", 0, 2048)], tag="frstd")
                B.op("dve", (lambda cw: lambda e: e.scalar_tensor_tensor(
                    rowa.ap()[0:1, 0:cw], rowa.ap()[0:1, 0:cw], -1.0,
                    rowb_.ap()[0:1, 0:cw], ALU.mult, ALU.mult))(cw),
                    reads=[K("rowa", 0, 2048), K("rowb", 0, 2048)],
                    writes=[K("rowa", 0, 2048)], tag="fr2")
                B.op("dve", (lambda cw: lambda e: e.tensor_copy(
                    rowhi.ap()[0:1, 0:cw], rowb_.ap()[0:1, 0:cw]))(cw),
                    reads=[K("rowb", 0, 2048)], writes=[K("rowhi", 0, 1024)], tag="fh1")
                B.op("dve", (lambda cw: lambda e: e.tensor_tensor(
                    rowlo.ap()[0:1, 0:cw], rowb_.ap()[0:1, 0:cw],
                    rowhi.ap()[0:1, 0:cw], ALU.subtract))(cw),
                    reads=[K("rowb", 0, 2048), K("rowhi", 0, 1024)],
                    writes=[K("rowlo", 0, 1024)], tag="fl1")
                B.op("dve", (lambda cw: lambda e: e.tensor_copy(
                    rowhi2.ap()[0:1, 0:cw], rowa.ap()[0:1, 0:cw]))(cw),
                    reads=[K("rowa", 0, 2048)], writes=[K("rowhi2")], tag="fh2")
                B.op("dve", (lambda cw: lambda e: e.tensor_tensor(
                    rowlo2.ap()[0:1, 0:cw], rowa.ap()[0:1, 0:cw],
                    rowhi2.ap()[0:1, 0:cw], ALU.subtract))(cw),
                    reads=[K("rowa", 0, 2048), K("rowhi2")],
                    writes=[K("rowlo2")], tag="fl2")
                for dt in range(NT):
                    B.group("pe", [
                        (lambda dt, cw: lambda e: e.matmul(
                            ps[4].ap()[:, 0:cw], gamrowb.ap()[:, dt * 128:(dt + 1) * 128],
                            rowhi.ap()[0:1, 0:cw], start=True, stop=False))(dt, cw),
                        (lambda dt, cw: lambda e: e.matmul(
                            ps[4].ap()[:, 0:cw], gamrowb.ap()[:, dt * 128:(dt + 1) * 128],
                            rowlo.ap()[0:1, 0:cw], start=False, stop=True))(dt, cw)],
                        reads=[K("rowhi", 0, 1024), K("rowlo", 0, 1024), K("gamrowb")],
                        writes=[K("ps4")], tag="fbc1")
                    B.group("pe", [
                        (lambda dt, cw: lambda e: e.matmul(
                            ps[5].ap()[:, 0:cw], gamrowb.ap()[:, dt * 128:(dt + 1) * 128],
                            rowhi2.ap()[0:1, 0:cw], start=True, stop=False))(dt, cw),
                        (lambda dt, cw: lambda e: e.matmul(
                            ps[5].ap()[:, 0:cw], gamrowb.ap()[:, dt * 128:(dt + 1) * 128],
                            rowlo2.ap()[0:1, 0:cw], start=False, stop=True))(dt, cw)],
                        reads=[K("rowhi2"), K("rowlo2"), K("gamrowb")],
                        writes=[K("ps5")], tag="fbc2")
                    src0 = cs0 if dt < 4 else cs0 - 1
                    if blk > 0 and cc == 0 and dt >= 4:
                        pb0 = ((blk - 1) % 2) * 4
                        B.op("dve", (lambda dt, pb0: lambda e: e.tensor_tensor(
                            ft32.ap()[:, 0:1], shsnap.ap()[:, pb0 + dt - 4:pb0 + dt - 3],
                            ps[4].ap()[:, 0:1], ALU.mult))(dt, pb0),
                            reads=[K("shsnap", (pb0 + dt - 4) * 4, (pb0 + dt - 3) * 4), K("ps4")],
                            writes=[K("ft32", 0, 4)], tag="fxn1a")
                        B.op("dve", (lambda dt, cs0, cw: lambda e: e.tensor_tensor(
                            ft32.ap()[:, 1:cw], st_ap(dt, cs0, cs0 + cw - 1),
                            ps[4].ap()[:, 1:cw], ALU.mult))(dt, cs0, cw),
                            reads=[st_key(dt, cs0, cs0 + cw - 1), K("ps4")],
                            writes=[K("ft32", 4, 4 * cw)], tag="fxn1b")
                    else:
                        B.op("dve", (lambda dt, src0, cw: lambda e: e.tensor_tensor(
                            ft32.ap()[:, 0:cw], st_ap(dt, src0, src0 + cw),
                            ps[4].ap()[:, 0:cw], ALU.mult))(dt, src0, cw),
                            reads=[st_key(dt, src0, src0 + cw), K("ps4")],
                            writes=[K("ft32")], tag="fxn1")
                    B.op("dve", (lambda dt, cc, cw: lambda e: e.scalar_tensor_tensor(
                        xnfb.ap()[:, dt * FFN_TB + cc * 512:dt * FFN_TB + cc * 512 + cw],
                        ft32.ap()[:, 0:cw], betcolb.ap()[:, dt:dt + 1],
                        ps[5].ap()[:, 0:cw], ALU.add, ALU.add))(dt, cc, cw),
                        reads=[K("ft32"), K("betcolb"), K("ps5")],
                        writes=[K("xnfb", (dt * FFN_TB + cc * 512) * 2,
                                  (dt * FFN_TB + cc * 512 + cw) * 2)], tag="fxn2")
            for ip in range(NI_T):
                iw = min(128, DI - ip * 128)
                wsel = ip % 2
                B.op("dmw" if wsel == 0 else "dmw1",
                     (lambda ip, iw, wsel, l=l: lambda e: e.dma_start(
                    w1bb.ap()[:, wsel * 2048:wsel * 2048 + NT * iw].rearrange(
                        "p (dt h) -> p dt h", dt=NT),
                    W1_h[l, :, ip * 128:ip * 128 + iw].rearrange(
                        "(dt p) h -> p dt h", p=128)))(ip, iw, wsel),
                    writes=[K("w1bb", wsel * 4096, wsel * 4096 + 4096)], tag="w1a")
                B.op("dmw" if wsel == 0 else "dmw1",
                     (lambda ip, iw, wsel, l=l: lambda e: e.dma_start(
                    w1bb.ap()[:, wsel * 2048 + NT * iw:wsel * 2048 + 2 * NT * iw].rearrange(
                        "p (dt h) -> p dt h", dt=NT),
                    W1_h[l, :, DI + ip * 128:DI + ip * 128 + iw].rearrange(
                        "(dt p) h -> p dt h", p=128)))(ip, iw, wsel),
                    writes=[K("w1bb", wsel * 4096, wsel * 4096 + 4096)], tag="w1g")
                B.op("dmw", (lambda ip, iw, l=l: lambda e: e.dma_start(
                    b1rowb.ap()[0:1, 0:iw], b1_h[l, :, ip * 128:ip * 128 + iw]))(ip, iw),
                    writes=[K("b1rowb", 0, 256)], tag="b1a")
                B.op("dmw", (lambda ip, iw, l=l: lambda e: e.dma_start(
                    b1rowb.ap()[0:1, 128:128 + iw],
                    b1_h[l, :, DI + ip * 128:DI + ip * 128 + iw]))(ip, iw),
                    writes=[K("b1rowb", 256, 512)], tag="b1g")
                for cc in range(nch):
                    cw = min(512, tb - cc * 512)
                    for ag in range(2):
                        bank = (ip % 2) * 2 + ag
                        fns = [(lambda dt, iw, cc, cw, ag, bank, wsel: lambda e: e.matmul(
                            ps[bank].ap()[0:iw, 0:cw],
                            w1bb.ap()[:, wsel * 2048 + ag * NT * iw + dt * iw:
                                      wsel * 2048 + ag * NT * iw + (dt + 1) * iw],
                            xnfb.ap()[:, dt * FFN_TB + cc * 512:dt * FFN_TB + cc * 512 + cw],
                            start=(dt == 0), stop=False))(dt, iw, cc, cw, ag, bank, wsel)
                            for dt in range(NT)]
                        fns.append((lambda iw, cc, cw, ag, bank: lambda e: e.matmul(
                            ps[bank].ap()[0:iw, 0:cw],
                            b1rowb.ap()[0:1, ag * 128:ag * 128 + iw],
                            onesrowb.ap()[:, 0:cw], start=False, stop=True))(iw, cc, cw, ag, bank))
                        B.group("pe", fns,
                                reads=[K("w1bb", wsel * 4096, wsel * 4096 + 4096),
                                       K("xnfb"), K("b1rowb"), K("onesrowb")],
                                writes=[K(f"ps{bank}")], tag="w1mm")
                    gb = (ip % 2) * 2 + 1
                    B.op("act", (lambda iw, cw, gb: lambda e: e.activation(
                        gelbb.ap()[0:iw, 0:cw], ps[gb].ap()[0:iw, 0:cw], AF.Gelu))(iw, cw, gb),
                        reads=[K(f"ps{gb}")], writes=[K("gelbb")], tag="gelu")
                    ab = (ip % 2) * 2
                    B.op("dve", (lambda ip, iw, cc, cw, ab: lambda e: e.tensor_tensor(
                        gatedb.ap()[0:iw, ip * FFN_TB + cc * 512:ip * FFN_TB + cc * 512 + cw],
                        ps[ab].ap()[0:iw, 0:cw], gelbb.ap()[0:iw, 0:cw],
                        ALU.mult))(ip, iw, cc, cw, ab),
                        reads=[K(f"ps{ab}"), K("gelbb")],
                        writes=[K("gatedb", (ip * FFN_TB + cc * 512) * 2,
                                  (ip * FFN_TB + cc * 512 + cw) * 2)], tag="gated")
            for et in range(NT):
                wsel = et % 2
                B.op("dmw", (lambda et, l=l: lambda e: e.dma_start(
                    b2rowb.ap()[:, :], b2_h[l, :, et * 128:(et + 1) * 128]))(et),
                    writes=[K("b2rowb")], tag="b2")
                B.op("dmw" if wsel == 0 else "dmw1",
                     (lambda et, wsel, l=l: lambda e: e.dma_start(
                    w2bb.ap()[:, wsel * NI_T * 128:wsel * NI_T * 128 + 21 * 128].rearrange(
                        "p (it ec) -> p it ec", it=21),
                    W2_h[l, 0:2688, et * 128:(et + 1) * 128].rearrange(
                        "(it p) ec -> p it ec", p=128)))(et, wsel),
                    writes=[K("w2bb", wsel * 5632, (wsel + 1) * 5632)], tag="w2a")
                B.op("dmw" if wsel == 0 else "dmw1",
                     (lambda et, wsel, l=l: lambda e: e.dma_start(
                    w2bb.ap()[0:42, wsel * NI_T * 128 + 21 * 128:wsel * NI_T * 128 + 22 * 128],
                    W2_h[l, 2688:2730, et * 128:(et + 1) * 128]))(et, wsel),
                    writes=[K("w2bb", wsel * 5632, (wsel + 1) * 5632)], tag="w2b")
                for cc in range(nch):
                    cw = min(512, tb - cc * 512)
                    bank = 4 + (et * nch + cc) % 2
                    fns = []
                    for it in range(NI_T):
                        iw = min(128, DI - it * 128)
                        fns.append((lambda it, iw, cw, bank, wsel, cc=cc: lambda e: e.matmul(
                            ps[bank].ap()[:, 0:cw],
                            w2bb.ap()[0:iw, wsel * NI_T * 128 + it * 128:
                                      wsel * NI_T * 128 + it * 128 + 128],
                            gatedb.ap()[0:iw, it * FFN_TB + cc * 512:it * FFN_TB + cc * 512 + cw],
                            start=(it == 0), stop=False))(it, iw, cw, bank, wsel))
                    fns.append((lambda et, cw, bank: lambda e: e.matmul(
                        ps[bank].ap()[:, 0:cw], b2rowb.ap()[:, 0:128],
                        onesrowb.ap()[:, 0:cw], start=False, stop=True))(et, cw, bank))
                    B.group("pe", fns,
                            reads=[K("w2bb", wsel * 5632, (wsel + 1) * 5632),
                                   K("gatedb"), K("b2rowb"), K("onesrowb")],
                            writes=[K(f"ps{bank}")], tag="w2mm")
                    B.op("dve", (lambda et, t0, cc, cw, bank: lambda e: e.tensor_tensor(
                        st_ap(et, t0 + cc * 512, t0 + cc * 512 + cw),
                        st_ap(et, t0 + cc * 512, t0 + cc * 512 + cw),
                        ps[bank].ap()[:, 0:cw], ALU.add))(et, t0, cc, cw, bank),
                        reads=[st_key(et, t0 + cc * 512, t0 + cc * 512 + cw),
                               K(f"ps{bank}")],
                        writes=[st_key(et, t0 + cc * 512, t0 + cc * 512 + cw)],
                        tag="fres")
        if l < NLAYER - 1:
            z0, z1 = Fq, HALO
            c = z0
            while c < z1:
                cw = min(512, z1 - c)
                for dt in range(NT):
                    B.op("act", (lambda dt, c, cw: lambda e: e.activation(
                        st_ap(dt, c, c + cw), st_ap(dt, c, c + cw),
                        AF.Identity, bias=0.0, scale=zflagb.ap()[:, :]))(dt, c, cw),
                        reads=[st_key(dt, c, c + cw), K("zflagb")],
                        writes=[st_key(dt, c, c + cw)], tag="zap")
                c += cw
        B.barrier()

    for sl in range(min(sublayers, 2 * NLAYER)):
        l, which = sl // 2, sl % 2
        if which == 0:
            emit_attn(l)
        else:
            emit_ffn(l)

    for oc in range(OWN // 512):
        c0 = HALO + oc * 512
        B.op("dmx", (lambda oc: lambda e: e.dma_start(
            sldf.ap()[:, :].rearrange("p (dt c) -> p dt c", dt=NT),
            s_scr[:, oc * 512:(oc + 1) * 512].rearrange("(dt p) c -> p dt c", p=128)))(oc),
            writes=[K("sldf")], tag="s_ld")
        for dt in range(NT):
            B.op("act", (lambda dt, c0: lambda e: e.activation(
                sgf.ap()[:, dt * 512:(dt + 1) * 512],
                st_ap(dt, c0, c0 + 512), AF.Silu))(dt, c0),
                reads=[st_key(dt, c0, c0 + 512)],
                writes=[K("sgf", dt * 2048, (dt + 1) * 2048)], tag="silu")
            B.op("dve", (lambda dt: lambda e: e.tensor_tensor(
                sgf.ap()[:, dt * 512:(dt + 1) * 512],
                sgf.ap()[:, dt * 512:(dt + 1) * 512],
                sldf.ap()[:, dt * 512:(dt + 1) * 512], ALU.mult))(dt),
                reads=[K("sgf", dt * 2048, (dt + 1) * 2048),
                       K("sldf", dt * 2048, (dt + 1) * 2048)],
                writes=[K("sgf", dt * 2048, (dt + 1) * 2048)], tag="sg")
        for tt in range(4):
            t0 = oc * 512 + tt * 128
            B.op("dmx", (lambda t0: lambda e: e.dma_start(
                xotf.ap()[:, :], x_own[t0:t0 + 128, :]))(t0),
                writes=[K("xotf")], tag="xo_ld")
            fns = []
            for dt in range(NT):
                bank = 2 + dt // 4
                col = (dt % 4) * 128
                fns.append((lambda dt, tt, bank, col: lambda e: e.transpose(
                    ps[bank].ap()[:, col:col + 128],
                    sgf.ap()[:, dt * 512 + tt * 128:dt * 512 + tt * 128 + 128],
                    identb.ap()[:, :]))(dt, tt, bank, col))
            B.group("pe", fns, reads=[K("sgf"), K("identb")],
                    writes=[K("ps2"), K("ps3")], tag="sgT")
            B.op("dve", lambda e: e.tensor_tensor(
                xotf.ap()[:, 0:512], xotf.ap()[:, 0:512], ps[2].ap()[:, :], ALU.add),
                reads=[K("xotf"), K("ps2")], writes=[K("xotf")], tag="oadd1")
            B.op("dve", lambda e: e.tensor_tensor(
                xotf.ap()[:, 512:1024], xotf.ap()[:, 512:1024], ps[3].ap()[:, :], ALU.add),
                reads=[K("xotf"), K("ps3")], writes=[K("xotf")], tag="oadd2")
            B.op("dms", (lambda t0: lambda e: e.dma_start(
                out_t[t0:t0 + 128, :], xotf.ap()[:, :]))(t0),
                reads=[K("xotf")], tag="out_st")

    if DEBUG_DUMP:
        dbgs = [("d_ahb", ahb, 8), ("d_bcol", bcolb, 8), ("d_carry", carryb, 8),
                ("d_ych", ych, NT * 512), ("d_sgf", sgf, NT * 512),
                ("d_sldf", sldf, NT * 512)]
        for nm, buf, w in dbgs:
            o = nc.dram_tensor(nm, [128, w], F32, kind="ExternalOutput").ap()
            B.op("dms", (lambda o, buf, w: lambda e: e.dma_start(
                o[:, :], buf.ap()[:, 0:w]))(o, buf, w),
                reads=[K(NAMEOF[id(buf)])], tag="dbg")
        o = nc.dram_tensor("d_stream", [128, 2048], F32, kind="ExternalOutput").ap()
        B.op("dms", lambda e: e.dma_start(o[:, :], st_ap(0, HALO, HALO + 2048)),
             reads=[st_key(0, HALO, HALO + 2048)], tag="dbg")
    B.barrier()
    B.lower(nc)
    return nc


_NC_CACHE = {}


def kernel(**inputs):
    in_maps = _stage_inputs(inputs)
    key = (SUBLAYERS, DEBUG_DUMP)
    if key not in _NC_CACHE:
        _NC_CACHE[key] = build_program(SUBLAYERS)
    nc = _NC_CACHE[key]
    res = run_bass_kernel_spmd(nc, in_maps, core_ids=list(range(8)))
    out = np.zeros((4, 4096, 1024), dtype=np.float32)
    for core in range(8):
        b, half = core // 2, core % 2
        out[b, half * OWN:(half + 1) * OWN, :] = np.asarray(res.results[core]["out"])
    return out
